# version 16
# baseline (speedup 1.0000x reference)
import math
import numpy as np
import ml_dtypes
import concourse.bass as bass
import concourse.mybir as mybir
from concourse.bass_utils import run_bass_kernel_spmd
from contextlib import ExitStack

F32 = mybir.dt.float32
BF16 = mybir.dt.bfloat16
AF = mybir.ActivationFunctionType
ALU = mybir.AluOpType

B, T, D, H, HD = 4, 4096, 2048, 16, 128
NR = 8
SLAB = T // NR
TPB = SLAB // 128
DT = D // 128
GC = 32
GCS = SLAB // GC
NGC = T // GC
VH = D // 2
SCALE = HD ** -0.5
LNSC = math.log(SCALE)
SHIFT = -12.0
EPS = 1e-5
N_SL = 4


class Ctr:
    def __init__(self, ctx, nc, name):
        self.h = ctx.enter_context(nc.semaphore(name))
        self.n = 0

    def inc(self, instr, k=1):
        instr.then_inc(self.h, k)
        self.n += k
        return self.n


def build(n_sl=N_SL):
    nc = bass.Bass("TRN2", target_bir_lowering=False, debug=False)

    x_in = nc.dram_tensor("x_my", [B, SLAB, D], F32, kind="ExternalInput").ap()
    w_qkv = nc.dram_tensor("w_qkv", [24, 128, DT, 256], F32,
                           kind="ExternalInput").ap()
    b_qkv = nc.dram_tensor("b_qkv", [3 * D], F32, kind="ExternalInput").ap()
    b_qkv_col = nc.dram_tensor("b_qkv_col", [128, 48], F32, kind="ExternalInput").ap()
    w_proj = nc.dram_tensor("w_proj", [8, 128, DT, 256], F32,
                            kind="ExternalInput").ap()
    b_proj = nc.dram_tensor("b_proj", [D], F32, kind="ExternalInput").ap()
    w_baseT = nc.dram_tensor("w_baseT", [8, 128, DT, 256], F32,
                             kind="ExternalInput").ap()
    b_base = nc.dram_tensor("b_base", [D], F32, kind="ExternalInput").ap()
    routerT = nc.dram_tensor("routerT", [128, DT, 8], F32,
                             kind="ExternalInput").ap()
    a_T = nc.dram_tensor("a_T", [128, DT, 128], F32,
                         kind="ExternalInput").ap()
    bw_flat = nc.dram_tensor("bw_flat", [128, D], F32, kind="ExternalInput").ap()
    wqT = nc.dram_tensor("wqT", [128, DT, HD], F32,
                         kind="ExternalInput").ap()
    wkT = nc.dram_tensor("wkT", [128, DT, HD], F32,
                         kind="ExternalInput").ap()
    wogT = nc.dram_tensor("wogT", [8, 128, DT, 256], F32,
                          kind="ExternalInput").ap()
    wigT = nc.dram_tensor("wigT", [8, 128, DT, 256], F32,
                          kind="ExternalInput").ap()
    g_w = nc.dram_tensor("g_w", [D], F32, kind="ExternalInput").ap()
    triu_in = nc.dram_tensor("triu_in", [128, 128], F32, kind="ExternalInput").ap()
    ident_in = nc.dram_tensor("ident_in", [128, 128], F32, kind="ExternalInput").ap()
    y_out = nc.dram_tensor("y_my", [B, SLAB, D], F32, kind="ExternalOutput").ap()

    SEC = 128 * 512
    ag1_in = [nc.dram_tensor(f"ag1_in{b}", [H, 3, SEC], BF16) for b in range(B)]
    ag1_out = [nc.dram_tensor(f"ag1_out{b}", [NR, H, 3, SEC], BF16,
                              addr_space="Shared") for b in range(B)]
    ag2_in = [[nc.dram_tensor(f"ag2_in{b}_{j}", [1, 128, T], BF16)
               for j in range(2)] for b in range(B)]
    ag2_out = [[nc.dram_tensor(f"ag2_out{b}_{j}", [NR, 1, 128, T], BF16,
                               addr_space="Shared") for j in range(2)]
               for b in range(B)]
    ag2_out_unused = [nc.dram_tensor(f"ag2_ou{b}", [NR, 2, 128, T], BF16,
                              addr_space="Shared") for b in range(B)]
    pair_scr = [nc.dram_tensor(f"pair_scr{i}", [NR, 3, SEC], BF16)
                for i in range(2)]
    proj_scr = nc.dram_tensor("proj_scr", [B, NR, 2, 128, 512], BF16)
    ag3_abk = nc.dram_tensor("ag3_abk", [B, 3, GCS, 128 * GC], BF16)
    ag3_abk_o = nc.dram_tensor("ag3_abk_o", [NR, B, 3, GCS, 128 * GC], BF16,
                               addr_space="Shared")
    ag3_lam = nc.dram_tensor("ag3_lam", [B, 128, GCS], F32)
    ag3_lam_o = nc.dram_tensor("ag3_lam_o", [NR, B, 128, GCS], F32,
                               addr_space="Shared")
    ag3_v = nc.dram_tensor("ag3_v", [B, SLAB, D], BF16)
    ag3_v_o = nc.dram_tensor("ag3_v_o", [NR, B, SLAB, D], BF16,
                             addr_space="Shared")
    ag4_ot = nc.dram_tensor("ag4_ot", [8, 128, T], BF16)
    ag4_ot_o = nc.dram_tensor("ag4_ot_o", [NR, 8, 128, T], BF16,
                              addr_space="Shared")
    ag4_o2 = nc.dram_tensor("ag4_o2", [1, T], F32)
    ag4_o2_o = nc.dram_tensor("ag4_o2_o", [NR, 1, T], F32, addr_space="Shared")
    go_scr = nc.dram_tensor("go_scr", [B, SLAB, D], F32).ap()
    wog_bf = nc.dram_tensor("wog_bf", [8, 128, DT, 256], BF16)
    wig_bf = nc.dram_tensor("wig_bf", [8, 128, DT, 256], BF16)
    abk_scr = nc.dram_tensor("abk_scr", [NR, 3, GCS, 128 * GC], BF16)
    lam_scr = nc.dram_tensor("lam_scr", [NR, 128, GCS], F32)
    v_scr = nc.dram_tensor("v_scr", [NR, SLAB, VH], BF16)
    ep_ot_scr = nc.dram_tensor("ep_ot_scr", [NR, 8, 128, 512], BF16)
    ep_o2_scr = nc.dram_tensor("ep_o2_scr", [NR, 1, 512], F32)
    xlnt_scr = nc.dram_tensor("xlnt_scr", [B, 128, DT * SLAB], BF16).ap()

    with ExitStack() as ctx:
        E = ctx.enter_context

        xr = [[E(nc.sbuf_tensor(f"xr{b}_{t}", [128, D], F32)) for t in range(TPB)]
              for b in range(B)]
        triu_bf = E(nc.sbuf_tensor("triu_bf", [128, 128], BF16))
        ident_bf = E(nc.sbuf_tensor("ident_bf", [128, 128], BF16))
        ones_col_bf = E(nc.sbuf_tensor("ones_col_bf", [128, 1], BF16))
        ones_row_f32 = E(nc.sbuf_tensor("ones_row_f32", [1, 128], F32))
        consts = E(nc.sbuf_tensor("consts", [128, 16], F32))
        bias_qkv_col = E(nc.sbuf_tensor("bias_qkv_col", [128, 48], F32))
        brow_buf = E(nc.sbuf_tensor("brow_buf", [1, 512], F32))
        brow_bf = E(nc.sbuf_tensor("brow_bf", [1, 512], BF16))
        ones_row_bf = E(nc.sbuf_tensor("ones_row_bf", [1, 128], BF16))
        ln_sum = E(nc.sbuf_tensor("ln_sum", [128, 4], F32))
        ln_sq = E(nc.sbuf_tensor("ln_sq", [128, 4], F32))
        ln_m = E(nc.sbuf_tensor("ln_m", [128, 4], F32))
        ln_v = E(nc.sbuf_tensor("ln_v", [128, 4], F32))
        LN_BUFS = {}
        stage_bf = [E(nc.sbuf_tensor(f"stgbf{i}", [128, 512], BF16)) for i in range(2)]

        psA = E(nc.psum_tensor("psA", [128, 512], F32))
        psB = E(nc.psum_tensor("psB", [128, 512], F32))
        psC = E(nc.psum_tensor("psC", [128, 512], F32))
        psD = E(nc.psum_tensor("psD", [128, 512], F32))
        psE = E(nc.psum_tensor("psE", [128, 512], F32))
        psF = E(nc.psum_tensor("psF", [128, 512], F32))
        psG = E(nc.psum_tensor("psG", [128, 256], F32))
        ps_row = E(nc.psum_tensor("ps_row", [1, 512], F32))

        sx = Ctr(ctx, nc, "sx")
        sw = Ctr(ctx, nc, "sw")
        sld = Ctr(ctx, nc, "sld")
        sst = Ctr(ctx, nc, "sst")
        sag = Ctr(ctx, nc, "sag")
        smm = Ctr(ctx, nc, "smm")
        sact = Ctr(ctx, nc, "sact")
        sdve = Ctr(ctx, nc, "sdve")
        sout = Ctr(ctx, nc, "sout")
        sw2 = Ctr(ctx, nc, "sw2")
        st1s = [Ctr(ctx, nc, f"st1_{i}") for i in range(4)]
        sw3 = Ctr(ctx, nc, "sw3")
        spre = Ctr(ctx, nc, "spre")

        blk = E(nc.Block())

        CONST_VALUES = sorted({SHIFT, SCALE, LNSC, -1.0, -0.5, 2.0, 1.0 / D,
                               EPS, 0.5, 1.0, 1.0 / HD})
        for i, val in enumerate(CONST_VALUES):
            nc.const_aps.register(F32, float(val), consts[:, i:i + 1])

        pid = nc.partition_id()
        V, SC, TN, SY, GP = nc.vector, nc.scalar, nc.tensor, nc.sync, nc.gpsimd
        LAST_WAIT = {}

        def wge(eng, ctr, count):
            key = (id(eng), id(ctr))
            if LAST_WAIT.get(key, -1) >= count:
                return
            eng.wait_ge(ctr.h, count)
            LAST_WAIT[key] = count
        pid_s = nc.snap(pid)
        h_slice = [bass.ts(nc.snap(pid * 2 + j), 1) for j in range(2)]
        bch_slice = bass.ts(nc.snap(pid // 2), 1)
        vh_s = nc.snap(pid % 2)
        vh_cols = bass.ts(vh_s, VH)
        colslab = bass.ts(pid_s, 512)
        slab_slice = bass.ts(pid_s, 1)

        for b in range(B):
            for t in range(TPB):
                sx.inc(SY.dma_start(out=xr[b][t][:],
                                    in_=x_in[b, t * 128:(t + 1) * 128, :]), 16)
        sx.inc(SY.dma_start(out=bias_qkv_col[:], in_=b_qkv_col), 16)
        wge(GP, sw, sw.n)
        sw.inc(GP.dma_start(out=triu_bf[:], in_=triu_in), 16)
        wge(GP, sw, sw.n)
        sw.inc(GP.dma_start(out=ident_bf[:], in_=ident_in), 16)
        for i, val in enumerate(CONST_VALUES):
            V.memset(consts[:, i:i + 1], float(val))
        V.memset(ones_col_bf[:], 1.0)
        V.memset(ones_row_bf[:], 1.0)
        sdve.inc(V.memset(ones_row_f32[:], 1.0))

        for eng in (V, SC, TN, SY, GP):
            wge(eng, sx, sx.n)
            wge(eng, sw, sw.n)
            wge(eng, sdve, sdve.n)

        BANK_WR = {}
        BANK_RD = {}

        def pe_acquire(*banks):
            for bank in banks:
                c = BANK_RD.get(bank.name)
                if c is not None:
                    wge(TN, c[0], c[1])

        def pe_done(bank, cnt):
            BANK_WR[bank.name] = cnt

        def rd_acquire(eng, *banks):
            for bank in banks:
                c = BANK_WR.get(bank.name)
                if c is not None:
                    wge(eng, smm, c)

        def rd_done(bank, ctr, cnt):
            BANK_RD[bank.name] = (ctr, cnt)

        def emit_ln_slab(b, xln_cons=None, sst_cons=None):
            wge(SC, sdve, sdve.n)
            wge(V, sst, sst.n if sst_cons is None else sst_cons)
            wge(V, smm, smm.n if xln_cons is None else xln_cons)
            ln_sqs, xln_buf = LN_BUFS["sqs"], LN_BUFS["xln"]
            for t in range(TPB):
                i1 = V.tensor_scalar(out=xln_buf[t][:], in0=xr[b][t][:], scalar1=0.0,
                                     scalar2=0.0, op0=ALU.add, op1=ALU.add,
                                     accum_out=ln_sum[:, t:t + 1])
                sdve.inc(i1)
                if t < TPB - 1:
                    wge(V, sdve, sdve.n)
            for t in range(TPB):
                i2 = SC.activation(out=ln_sqs[:], in_=xr[b][t][:], func=AF.Square,
                                   accum_out=ln_sq[:, t:t + 1])
                sact.inc(i2)
                if t < TPB - 1:
                    wge(SC, sact, sact.n)
            wge(V, sdve, sdve.n)
            wge(V, sact, sact.n)
            sdve.inc(V.tensor_scalar(out=ln_m[:], in0=ln_sum[:], scalar1=1.0 / D,
                                     scalar2=None, op0=ALU.mult))
            wge(V, sdve, sdve.n)
            sdve.inc(V.scalar_tensor_tensor(out=ln_v[:], in0=ln_m[:], scalar=-1.0,
                                            in1=ln_m[:], op0=ALU.mult,
                                            op1=ALU.mult))
            wge(V, sdve, sdve.n)
            sdve.inc(V.scalar_tensor_tensor(out=ln_v[:], in0=ln_sq[:],
                                            scalar=1.0 / D, in1=ln_v[:],
                                            op0=ALU.mult, op1=ALU.add))
            wge(SC, sdve, sdve.n)
            sact.inc(SC.activation(out=ln_v[:], in_=ln_v[:], func=AF.Ln, bias=EPS))
            wge(SC, sact, sact.n)
            sact.inc(SC.activation(out=ln_v[:], in_=ln_v[:], func=AF.Exp,
                                   scale=-0.5))
            wge(V, sact, sact.n)
            outs = []
            for t in range(TPB):
                i7 = V.tensor_scalar(out=xln_buf[t][:], in0=xr[b][t][:],
                                     scalar1=ln_m[:, t:t + 1],
                                     scalar2=ln_v[:, t:t + 1],
                                     op0=ALU.subtract, op1=ALU.mult)
                outs.append(sdve.inc(i7))
            return outs

        def transpose_tile_into(xlnt_sb, t, xi, xln_ready, xlnt_guard=None,
                                ready_ctr=None, sst_guard=None):
            xln_buf = LN_BUFS["xln"]
            wge(TN, ready_ctr if ready_ctr is not None else sdve, xln_ready)
            for grp in range(4):
                bank = (psA, psB)[grp % 2]
                pe_acquire(bank)
                last = None
                for u in range(4):
                    i = grp * 4 + u
                    last = TN.matmul(bank[:, u * 128:(u + 1) * 128],
                                     xln_buf[xi][:, i * 128:(i + 1) * 128],
                                     ident_bf[:], start=True, stop=True)
                pe_done(bank, smm.inc(last))
                rd_acquire(SC, bank)
                if xlnt_guard is not None:
                    wge(SC, smm, xlnt_guard)
                if sst_guard is not None:
                    wge(SC, sst, sst_guard)
                dst = xlnt_sb[:].rearrange("p (i n) -> p i n", i=DT)[
                    :, grp * 4:(grp + 1) * 4, t * 128:(t + 1) * 128]
                ev = SC.copy(dst, bank[:].rearrange("p (u n) -> p u n", u=4))
                rd_done(bank, sact, sact.inc(ev))

        def ln_alloc(EE, tag):

            LN_BUFS["sqs"] = EE(nc.sbuf_tensor(f"lnsqs{tag}", [128, D], BF16))
            LN_BUFS["xln"] = [EE(nc.sbuf_tensor(f"xln{tag}{i}", [128, D], BF16))
                              for i in range(4)]

        def emit_ln_transpose_slab(b, xlnt_sb, store_scr=False, v_dst=None):
            rs = emit_ln_slab(b)
            for t in range(TPB):
                transpose_tile_into(xlnt_sb, t, t, rs[t])
                if v_dst is not None:
                    wge(SY, sdve, rs[t])
                    wge(SY, sst, sst.n)
                    sst.inc(SY.dma_start(
                        out=v_dst[b, t * 128:(t + 1) * 128, :],
                        in_=LN_BUFS["xln"][t][:]), 16)
            if store_scr:
                wge(SY, sact, sact.n)
                wge(SY, sst, sst.n)
                sst.inc(SY.dma_start(out=xlnt_scr[b], in_=xlnt_sb[:]), 16)

        def load_panel(buf_sb, src_cols_ap, ncols):
            nt = DT
            return GP.dma_start(
                out=buf_sb[:].rearrange("p (i n) -> p i n", i=nt),
                in_=src_cols_ap.rearrange("(i p) n -> p i n", p=128))

        def allgather(in_ap, out_ap):
            return sag.inc(GP.collective_compute(
                "AllGather", ALU.bypass, replica_groups=[list(range(NR))],
                ins=[in_ap.opt()], outs=[out_ap.opt()]))

        def phase_fence():
            for eng in (V, SC, TN, SY, GP):
                for c in (sld, sst, smm, sact, sdve, sw, sw2, sw3, spre,
                          st1s[0], st1s[1], st1s[2], st1s[3]):
                    wge(eng, c, c.n)

        if n_sl >= 1:
            with ExitStack() as c1:
                E1 = c1.enter_context
                xlnt2 = [E1(nc.sbuf_tensor(f"xlnt1_{i}", [128, DT * SLAB], BF16))
                         for i in range(2)]
                ln_alloc(E1, "p1")
                wpan = [E1(nc.sbuf_tensor(f"wpan1_{i}", [128, DT * 256], BF16))
                        for i in range(2)]
                stg1 = [E1(nc.sbuf_tensor(f"stg1_{i}", [128, 512], BF16))
                        for i in range(4)]
                AG1_done = []
                panel_used = [0, 0]
                stage_st = [0, 0, 0, 0]
                stg_i = [0]
                brow_rd = 0
                QEND = 0
                p1_sst0 = sst.n
                prev_trans = 0
                for pr in range(2):
                    for bb in range(2):
                        b = 2 * pr + bb
                        rs = emit_ln_slab(b, xln_cons=prev_trans,
                                          sst_cons=p1_sst0)
                        for t in range(TPB):
                            transpose_tile_into(xlnt2[bb], t, t, rs[t],
                                                xlnt_guard=QEND if pr else None)
                        prev_trans = smm.n
                    trans_done = sact.n
                    for pan in range(24):
                        pbuf = wpan[pan % 2]
                        psem = (sw, sw2)[pan % 2]
                        GP.wait_ge(smm.h, panel_used[pan % 2])
                        wge(GP, psem, psem.n)
                        psem.inc(GP.dma_start(
                            out=pbuf[:].rearrange("p (i n) -> p i n", i=DT),
                            in_=w_qkv[pan]), 16)
                        PANLD = psem.n
                        if pan >= 16:
                            pv = pan - 16
                            wge(GP, sw3, sw3.n)
                            wge(GP, smm, brow_rd)
                            sw3.inc(GP.dma_start(
                                out=brow_bf[0:1, 0:256],
                                in_=b_qkv.unsqueeze(0)[
                                    0:1, 2 * D + pv * 256:
                                    2 * D + (pv + 1) * 256]), 16)
                            BROWLD = sw3.n
                        wge(TN, psem, PANLD)
                        if pan == 0:
                            wge(TN, sact, trans_done)
                        if pan < 16:
                            for su in range(2):
                                ot = pan * 2 + su
                                for bb in range(2):
                                    b = 2 * pr + bb
                                    bank = (psC, psD, psE, psF)[
                                        (su % 2) * 2 + bb]
                                    pe_acquire(bank)
                                    last = None
                                    for i in range(DT):
                                        last = TN.matmul(
                                            bank[:],
                                            pbuf[:, i * 256 + su * 128:
                                                 i * 256 + (su + 1) * 128],
                                            xlnt2[bb][:, i * 512:(i + 1) * 512],
                                            start=(i == 0), stop=(i == DT - 1))
                                    pe_done(bank, smm.inc(last))
                                    rd_acquire(SC, bank)
                                    sl = stg_i[0] % 4
                                    stg_i[0] += 1
                                    wge(SC, st1s[sl], stage_st[sl])
                                    st = stg1[sl]
                                    ev = SC.activation(
                                        out=st[:], in_=bank[:],
                                        func=AF.Identity,
                                        bias=bias_qkv_col[:, ot:ot + 1])
                                    rd_done(bank, sact, sact.inc(ev))
                                    wge(SY, sact, sact.n)
                                    hh, kind = (ot, 0) if ot < 16 else (ot - 16, 1)
                                    dst = ag1_in[b].ap()[hh, kind].rearrange(
                                        "(p n) -> p n", n=512)
                                    stage_st[sl] = st1s[sl].inc(
                                        SY.dma_start(out=dst, in_=st[:]), 16)
                        else:
                            pv = pan - 16
                            wge(TN, sw3, BROWLD)
                            for t in range(TPB):
                                for bb in range(2):
                                    b = 2 * pr + bb
                                    bank = (psC, psD, psE, psF)[
                                        (t % 2) * 2 + bb]
                                    pe_acquire(bank)
                                    last = None
                                    for i in range(DT):
                                        last = TN.matmul(
                                            bank[:, 0:256],
                                            xlnt2[bb][:, i * 512 + t * 128:
                                                      i * 512 + (t + 1) * 128],
                                            pbuf[:, i * 256:(i + 1) * 256],
                                            start=(i == 0), stop=False)
                                    last = TN.matmul(bank[:, 0:256],
                                                     ones_row_bf[:],
                                                     brow_bf[0:1, 0:256],
                                                     start=False, stop=True)
                                    pe_done(bank, smm.inc(last))
                                    brow_rd = smm.n
                                    rd_acquire(SC, bank)
                                    sl = stg_i[0] % 4
                                    stg_i[0] += 1
                                    wge(SC, st1s[sl], stage_st[sl])
                                    st = stg1[sl]
                                    ev = SC.copy(st[:, 0:256], bank[:, 0:256])
                                    rd_done(bank, sact, sact.inc(ev))
                                    wge(SY, sact, sact.n)
                                    dst = ag1_in[b].ap()[
                                        2 * pv:2 * pv + 2, 2].rearrange(
                                        "h (t2 v) -> t2 h v", v=128)[
                                        t * 128:(t + 1) * 128, :, :]
                                    srcv = st[:, 0:256].rearrange(
                                        "p (h v) -> p h v", h=2)
                                    stage_st[sl] = st1s[sl].inc(
                                        SY.dma_start(out=dst, in_=srcv), 16)
                        panel_used[pan % 2] = smm.n
                    QEND = smm.n
                    if pr == 1:
                        for j in range(2):
                            wge(GP, sag, AG1_done[0])
                            if j:
                                wge(GP, spre, 16)
                            gsrc0 = ag1_out[0].ap().rearrange(
                                "r h k m -> h r k m")[h_slice[j]]
                            spre.inc(GP.dma_start(
                                out=pair_scr[j].ap().unsqueeze(0),
                                in_=gsrc0), 16)
                    for bb in range(2):
                        b = 2 * pr + bb
                        wge(GP, sst, sst.n)
                        for k in range(4):
                            wge(GP, st1s[k], st1s[k].n)
                        allgather(ag1_in[b].ap(), ag1_out[b].ap())
                        AG1_done.append(sag.n)


            with ExitStack() as c2:
                E2 = c2.enter_context
                phase_fence()
                kt_sb2 = [E2(nc.sbuf_tensor(f"kt_sb{i}", [128, T], BF16))
                          for i in range(2)]
                v_sb2 = [E2(nc.sbuf_tensor(f"v_sb{i}", [128, T], BF16))
                         for i in range(2)]
                qt_sb2 = [E2(nc.sbuf_tensor(f"qt_sb{i}", [128, T], BF16))
                          for i in range(2)]
                pt_sb = [E2(nc.sbuf_tensor(f"pt{i}", [128, 512], BF16))
                         for i in range(4)]
                attn_d = E2(nc.sbuf_tensor("attn_d", [1, 512], F32))
                attn_db = E2(nc.sbuf_tensor("attn_db", [128, 512], F32))
                wpan3 = [E2(nc.sbuf_tensor(f"wpan3_{i}", [128, DT * 256], BF16))
                         for i in range(2)]

                sbanks = [psA, psB, psE, psF]
                obank = [psC, psD]
                kv_used = [smm.n, smm.n]
                kv_ld = [0, 0]
                pt_pv = [0, 0, 0, 0]
                stage_o = [sst.n, sst.n]
                pend_bc = [None]
                pend_ml = [0]

                def emit_pair_loads(p):
                    b, j = p >> 1, p & 1
                    wge(SY, smm, kv_used[p % 2])
                    if p >= 2:
                        wge(SY, sag, AG1_done[b])
                        wge(SY, sld, sld.n)
                        gsrc = ag1_out[b].ap().rearrange(
                            "r h k m -> h r k m")[h_slice[j]]
                        sld.inc(SY.dma_start(
                            out=pair_scr[p % 2].ap().unsqueeze(0),
                            in_=gsrc), 16)
                        wge(SY, sld, sld.n)
                    else:
                        wge(SY, sld, sld.n)
                        wge(SY, spre, 16 * (p + 1))
                    rows_pn = pair_scr[p % 2].ap().rearrange(
                        "r k (p n) -> (r k p) n", n=512)
                    rows_v = pair_scr[p % 2].ap().rearrange(
                        "r k (j2 p v) -> (r k j2 p) v", j2=4, p=128)
                    for r in range(NR):
                        base = r * 3
                        sld.inc(SY.dma_start(
                            out=qt_sb2[p % 2][:, r * 512:(r + 1) * 512],
                            in_=rows_pn[(base + 0) * 128:(base + 1) * 128, :]), 16)
                        sld.inc(SY.dma_start(
                            out=kt_sb2[p % 2][:, r * 512:(r + 1) * 512],
                            in_=rows_pn[(base + 1) * 128:(base + 2) * 128, :]), 16)
                        for j2l in range(4):
                            rr = (base + 2) * 4 + j2l
                            sld.inc(SY.dma_start(
                                out=v_sb2[p % 2][:, (4 * r + j2l) * 128:
                                                 (4 * r + j2l + 1) * 128],
                                in_=rows_v[rr * 128:(rr + 1) * 128, :]), 16)
                    kv_ld[p % 2] = sld.n

                AG2_done = {}
                po_used = [smm.n]
                p3_pan_used = [smm.n, smm.n]
                p3_brow_rd = [smm.n]

                def emit_proj(b, par):
                    wge(SY, sag, AG2_done[b])
                    wge(SY, sld, sld.n)
                    for jj2 in range(2):
                        sld.inc(SY.dma_start(
                            out=proj_scr.ap()[b, :, jj2],
                            in_=ag2_out[b][jj2].ap()[:, 0, :, colslab]), 16)
                    wge(SY, sld, sld.n)
                    wge(SY, smm, kv_used[par])
                    wge(SY, smm, po_used[0])
                    for h in range(H):
                        r, jj = h >> 1, h & 1
                        buf = kt_sb2[par] if h < 8 else v_sb2[par]
                        hh = h % 8
                        sld.inc(SY.dma_start(
                            out=buf[:, hh * 512:(hh + 1) * 512],
                            in_=proj_scr.ap()[b, r, jj]), 16)
                    po_ld = sld.n
                    for hp in range(8):
                        pbuf = wpan3[hp % 2]
                        psem = (sw, sw2)[hp % 2]
                        wge(GP, smm, p3_pan_used[hp % 2])
                        wge(GP, psem, psem.n)
                        psem.inc(GP.dma_start(
                            out=pbuf[:].rearrange("p (i n) -> p i n", i=DT),
                            in_=w_proj[hp]), 16)
                        PANLD = psem.n
                        wge(GP, sw3, sw3.n)
                        wge(GP, smm, p3_brow_rd[0])
                        sw3.inc(GP.dma_start(
                            out=brow_bf[0:1, 0:256],
                            in_=b_proj.unsqueeze(0)[
                                0:1, hp * 256:(hp + 1) * 256]), 16)
                        BROWLD = sw3.n
                        wge(TN, psem, PANLD)
                        wge(TN, sw3, BROWLD)
                        wge(TN, sld, po_ld)
                        for tl in range(TPB):
                            pe_acquire(psG)
                            for h in range(H):
                                buf = kt_sb2[par] if h < 8 else v_sb2[par]
                                hh = h % 8
                                TN.matmul(
                                    psG[:, 0:256],
                                    buf[:, hh * 512 + tl * 128:
                                        hh * 512 + (tl + 1) * 128],
                                    pbuf[:, h * 256:(h + 1) * 256],
                                    start=(h == 0), stop=False)
                            last = TN.matmul(psG[:, 0:256], ones_row_bf[:],
                                             brow_bf[0:1, 0:256],
                                             start=False, stop=True)
                            pe_done(psG, smm.inc(last))
                            rd_acquire(V, psG)
                            ad = V.tensor_tensor(
                                out=xr[b][tl][:, hp * 256:(hp + 1) * 256],
                                in0=psG[:, 0:256],
                                in1=xr[b][tl][:, hp * 256:(hp + 1) * 256],
                                op=ALU.add)
                            rd_done(psG, sdve, sdve.inc(ad))
                        p3_pan_used[hp % 2] = smm.n
                        p3_brow_rd[0] = smm.n
                    po_used[0] = smm.n
                    kv_used[par] = smm.n

                emit_pair_loads(0)
                for p in range(8):
                    b, j = p >> 1, p & 1
                    if p + 1 < 8:
                        emit_pair_loads(p + 1)
                    kt_sb, v_sb, qt_sb = kt_sb2[p % 2], v_sb2[p % 2], qt_sb2[p % 2]
                    wge(TN, sld, kv_ld[p % 2])
                    pend = None
                    for qc in range(8):
                        nk = 4 * qc + 4
                        ob = obank[qc % 2]
                        exp_cnt = {}
                        msk_cnt = {}
                        s_cnt = {}

                        def emit_S(j2):
                            m = j2 - 4 * qc
                            coloff = 0 if m < 0 else m * 128
                            sb = sbanks[j2 % 4]
                            pe_acquire(sb)
                            inst = TN.matmul(
                                sb[:, coloff:512],
                                kt_sb[:, j2 * 128:(j2 + 1) * 128],
                                qt_sb[:, qc * 512 + coloff:(qc + 1) * 512],
                                start=True, stop=True)
                            s_cnt[j2] = smm.inc(inst)
                            pe_done(sb, s_cnt[j2])

                        def emit_exp(j2):
                            m = j2 - 4 * qc
                            coloff = 0 if m < 0 else m * 128
                            sb = sbanks[j2 % 4]
                            wge(SC, smm, s_cnt[j2])
                            wge(SC, smm, pt_pv[j2 % 4])
                            ex = SC.activation(out=pt_sb[j2 % 4][:, coloff:512],
                                               in_=sb[:, coloff:512],
                                               func=AF.Exp, scale=SCALE, bias=SHIFT)
                            cnt = sact.inc(ex)
                            rd_done(sb, sact, cnt)
                            exp_cnt[j2] = cnt
                            if m >= 0:
                                wge(V, sact, cnt)
                                mk = V.tensor_tensor(
                                    out=pt_sb[j2 % 4][:, coloff:coloff + 128],
                                    in0=pt_sb[j2 % 4][:, coloff:coloff + 128],
                                    in1=triu_bf[:], op=ALU.mult)
                                msk_cnt[j2] = sdve.inc(mk)

                        def emit_PV(j2):
                            m = j2 - 4 * qc
                            coloff = 0 if m < 0 else m * 128
                            wge(TN, sact, exp_cnt[j2])
                            if m >= 0:
                                wge(TN, sdve, msk_cnt[j2])
                            if j2 == 0:
                                pe_acquire(ob)
                                pe_acquire(ps_row)
                            TN.matmul(ob[:, coloff:512],
                                      v_sb[:, j2 * 128:(j2 + 1) * 128],
                                      pt_sb[j2 % 4][:, coloff:512],
                                      start=(j2 == 0), stop=(j2 == nk - 1))
                            dnm = TN.matmul(ps_row[0:1, coloff:512],
                                            ones_col_bf[:],
                                            pt_sb[j2 % 4][:, coloff:512],
                                            start=(j2 == 0), stop=(j2 == nk - 1))
                            pt_pv[j2 % 4] = smm.inc(dnm)

                        def emit_epilogue_front(st8):
                            qc0, ob0 = st8
                            rd_acquire(V, ps_row)
                            if pend_bc[0] is not None:
                                wge(V, smm, pend_bc[0])
                            rc = V.reciprocal(out=attn_d[:], in_=ps_row[0:1, :])
                            rd_done(ps_row, sdve, sdve.inc(rc))
                            return sdve.n

                        def emit_epilogue_tail(st8, rc_cnt):
                            qc0, ob0 = st8
                            bb = sbanks[(4 * qc0 + 4 + 2) % 4]
                            pe_acquire(bb)
                            wge(TN, sdve, rc_cnt)
                            bc = TN.matmul(bb[:], ones_row_f32[:], attn_d[:],
                                           start=True, stop=True)
                            pend_bc[0] = smm.inc(bc)
                            pe_done(bb, pend_bc[0])
                            rd_acquire(SC, bb)
                            wge(SC, sdve, pend_ml[0])
                            cp = SC.copy(attn_db[:], bb[:])
                            rd_done(bb, sact, sact.inc(cp))
                            wge(V, sact, sact.n)
                            wge(V, sst, stage_o[qc0 % 2])
                            rd_acquire(V, ob0)
                            ml = V.tensor_tensor(out=stage_bf[qc0 % 2][:],
                                                 in0=ob0[:], in1=attn_db[:],
                                                 op=ALU.mult)
                            OEP = sdve.inc(ml)
                            pend_ml[0] = OEP
                            rd_done(ob0, sdve, OEP)
                            wge(SY, sdve, OEP)
                            wge(SY, sst, sst.n)
                            stage_o[qc0 % 2] = sst.inc(SY.dma_start(
                                out=ag2_in[b][j].ap()[0, :,
                                                      qc0 * 512:(qc0 + 1) * 512],
                                in_=stage_bf[qc0 % 2][:]), 16)

                        emit_S(0)
                        emit_exp(0)
                        if nk > 1:
                            emit_S(1)
                            emit_exp(1)
                        if pend is not None:
                            emit_epilogue_tail(*pend)
                            pend = None
                        for j2 in range(nk):
                            if j2 + 2 < nk:
                                emit_S(j2 + 2)
                                emit_exp(j2 + 2)
                            emit_PV(j2)
                        pe_done(ob, smm.n)
                        pe_done(ps_row, smm.n)
                        rcc = emit_epilogue_front((qc, ob))
                        pend = ((qc, ob), rcc)
                    emit_epilogue_tail(*pend)
                    pend = None
                    kv_used[p % 2] = smm.n
                    if j == 0 and b >= 1:
                        emit_proj(b - 1, 0)
                    wge(GP, sst, sst.n)
                    allgather(ag2_in[b][j].ap(), ag2_out[b][j].ap())
                    if j == 1:
                        AG2_done[b] = sag.n
                emit_proj(3, 1)

        def emit_ffn(tag, final=False):
            with ExitStack() as cf:
                EF = cf.enter_context
                phase_fence()
                xlnt = EF(nc.sbuf_tensor(f"xlntF{tag}", [128, DT * SLAB], BF16))
                at_sb = EF(nc.sbuf_tensor(f"at{tag}", [128, DT * 128], BF16))
                rt_sb = EF(nc.sbuf_tensor(f"rt{tag}", [128, DT * 8], BF16))
                hgt_all = EF(nc.sbuf_tensor(f"hgt{tag}", [128, 16 * 128], BF16))
                r8 = EF(nc.sbuf_tensor(f"r8{tag}", [128, 8], F32))
                rmax = EF(nc.sbuf_tensor(f"rmax{tag}", [128, 1], F32))
                roh = EF(nc.sbuf_tensor(f"roh{tag}", [128, 8], F32))
                hg_sb = EF(nc.sbuf_tensor(f"hg{tag}", [128, 128], BF16))
                bwf_pre = EF(nc.sbuf_tensor(f"bwfp{tag}", [128, D], BF16))
                cfl = EF(ExitStack())
                ln_alloc(cfl.enter_context, "f" + tag)

                wge(GP, smm, smm.n)
                wge(GP, sw, sw.n)
                sw.inc(GP.dma_start(
                    out=at_sb[:].rearrange("p (i n) -> p i n", i=DT),
                    in_=a_T), 16)
                wge(GP, sw, sw.n)
                sw.inc(GP.dma_start(
                    out=rt_sb[:].rearrange("p (i n) -> p i n", i=DT),
                    in_=routerT), 16)
                wge(GP, sw, sw.n)
                sw.inc(GP.dma_start(out=bwf_pre[:], in_=bw_flat), 16)
                for b in range(B):
                    emit_ln_transpose_slab(b, xlnt, store_scr=True)
                    wge(TN, sw, sw.n)
                    wge(TN, sact, sact.n)
                    for tl in range(TPB):
                        pe_acquire(psG)
                        last = None
                        for i in range(DT):
                            last = TN.matmul(
                                psG[:, 0:8],
                                xlnt[:, i * 512 + tl * 128:i * 512 + (tl + 1) * 128],
                                rt_sb[:, i * 8:(i + 1) * 8],
                                start=(i == 0), stop=(i == DT - 1))
                        pe_done(psG, smm.inc(last))
                        rd_acquire(V, psG)
                        cpr = V.tensor_copy(out=r8[:], in_=psG[:, 0:8])
                        rd_done(psG, sdve, sdve.inc(cpr))
                        wge(V, sdve, sdve.n)
                        sdve.inc(V.tensor_reduce(out=rmax[:], in_=r8[:],
                                                 axis=mybir.AxisListType.X,
                                                 op=ALU.max))
                        wge(V, sdve, sdve.n)
                        sdve.inc(V.tensor_scalar(out=roh[:], in0=r8[:],
                                                 scalar1=rmax[:], scalar2=2.0,
                                                 op0=ALU.is_equal, op1=ALU.mult))
                        pe_acquire(psA)
                        last = None
                        for i in range(DT):
                            last = TN.matmul(
                                psA[:, 0:128],
                                xlnt[:, i * 512 + tl * 128:i * 512 + (tl + 1) * 128],
                                at_sb[:, i * 128:(i + 1) * 128],
                                start=(i == 0), stop=(i == DT - 1))
                        pe_done(psA, smm.inc(last))
                        rd_acquire(V, psA)
                        wge(V, sdve, sdve.n)
                        gt = V.scalar_tensor_tensor(
                            out=hg_sb[:].rearrange("p (a c2) -> p a c2", a=8),
                            in0=psA[:, 0:128].rearrange("p (a c2) -> p a c2", a=8),
                            scalar=1.0,
                            in1=roh[:].unsqueeze(2).broadcast_to([128, 8, 16]),
                            op0=ALU.mult, op1=ALU.mult)
                        gcnt = sdve.inc(gt)
                        rd_done(psA, sdve, gcnt)
                        wge(TN, sdve, gcnt)
                        pe_acquire(psB)
                        tr = TN.matmul(psB[:, 0:128], hg_sb[:], ident_bf[:],
                                       start=True, stop=True)
                        pe_done(psB, smm.inc(tr))
                        rd_acquire(SC, psB)
                        ev = SC.copy(hgt_all[:, (b * TPB + tl) * 128:
                                             (b * TPB + tl + 1) * 128],
                                     psB[:, 0:128])
                        rd_done(psB, sact, sact.inc(ev))
                cfl.close()
                with ExitStack() as cf2:
                    EF2 = cf2.enter_context
                    phase_fence()
                    whs = [EF2(nc.sbuf_tensor(f"wh{tag}{i}", [128, DT * 256],
                                              BF16)) for i in range(2)]
                    xl2 = [xlnt,
                           EF2(nc.sbuf_tensor(f"xlnt2{tag}",
                                              [128, DT * SLAB], BF16))]
                    bwf_sb = bwf_pre
                    pan_used = [smm.n, smm.n]
                    xl_used = [smm.n, smm.n]
                    xl_ld = [0, 0]
                    brow_rd = smm.n
                    for k in range(2):
                        wge(SY, sld, sld.n)
                        wge(SY, sst, sst.n)
                        sld.inc(SY.dma_start(out=xl2[k][:], in_=xlnt_scr[k]), 16)
                        xl_ld[k] = sld.n
                    seq = [(hp2, b2) for hp2 in range(8) for b2 in range(B)]
                    pbuf = None
                    for si, (hp, b) in enumerate(seq):
                        k = b % 2
                        if b == 0:
                            pbuf = whs[hp % 2]
                            psem = (sw, sw2)[hp % 2]
                            wge(GP, smm, pan_used[hp % 2])
                            wge(GP, psem, psem.n)
                            psem.inc(GP.dma_start(
                                out=pbuf[:].rearrange("p (i n) -> p i n", i=DT),
                                in_=w_baseT[hp]), 16)
                            PANLD = psem.n
                            wge(GP, sw3, sw3.n)
                            wge(GP, smm, brow_rd)
                            sw3.inc(GP.dma_start(
                                out=brow_bf[0:1, 0:256],
                                in_=b_base.unsqueeze(0)[
                                    0:1, hp * 256:(hp + 1) * 256]), 16)
                            BROWLD = sw3.n
                            wge(TN, psem, PANLD)
                            wge(TN, sw3, BROWLD)
                        wge(TN, sld, xl_ld[k])
                        for tl in range(TPB):
                            bank = (psC, psD, psE, psF)[tl % 4]
                            pe_acquire(bank)
                            for i in range(DT):
                                TN.matmul(
                                    bank[:, 0:256],
                                    xl2[k][:, i * 512 + tl * 128:
                                           i * 512 + (tl + 1) * 128],
                                    pbuf[:, i * 256:(i + 1) * 256],
                                    start=(i == 0), stop=False)
                            TN.matmul(bank[:, 0:256],
                                      hgt_all[:, (b * TPB + tl) * 128:
                                              (b * TPB + tl + 1) * 128],
                                      bwf_sb[:, hp * 256:(hp + 1) * 256],
                                      start=False, stop=False)
                            last = TN.matmul(bank[:, 0:256], ones_row_bf[:],
                                             brow_bf[0:1, 0:256],
                                             start=False, stop=True)
                            pe_done(bank, smm.inc(last))
                            rd_acquire(V, bank)
                            ad = V.tensor_tensor(
                                out=xr[b][tl][:, hp * 256:(hp + 1) * 256],
                                in0=bank[:, 0:256],
                                in1=xr[b][tl][:, hp * 256:(hp + 1) * 256],
                                op=ALU.add)
                            rd_done(bank, sdve, sdve.inc(ad))
                            if final and hp == 7:
                                wge(SY, sdve, sdve.n)
                                sout.inc(SY.dma_start(
                                    out=y_out[b, tl * 128:(tl + 1) * 128, :],
                                    in_=xr[b][tl][:]), 16)
                        xl_used[k] = smm.n
                        if b == B - 1:
                            pan_used[hp % 2] = smm.n
                            brow_rd = smm.n
                        if si + 2 < len(seq):
                            nb = seq[si + 2][1]
                            wge(SY, smm, xl_used[k])
                            wge(SY, sld, sld.n)
                            sld.inc(SY.dma_start(out=xl2[nb % 2][:],
                                                 in_=xlnt_scr[nb]), 16)
                            xl_ld[nb % 2] = sld.n

        if n_sl >= 2:
            emit_ffn("a")

        if n_sl >= 3:
            vh_expr = pid % 2
            bch_expr = pid // 2
            with ExitStack() as c5:
                E5 = c5.enter_context
                phase_fence()
                xlnt = E5(nc.sbuf_tensor("xlnt5", [128, DT * SLAB], BF16))
                c5i = E5(ExitStack())
                E5i = c5i.enter_context
                ln_alloc(E5i, "p5")
                wq_sb = E5i(nc.sbuf_tensor("wq_sb", [128, DT * 128], BF16))
                wk_sb = E5i(nc.sbuf_tensor("wk_sb", [128, DT * 128], BF16))
                gq_s = E5i(nc.sbuf_tensor("gq_s", [128, 512], F32))
                gk_s = E5i(nc.sbuf_tensor("gk_s", [128, 512], F32))
                gsp = E5i(nc.sbuf_tensor("gsp", [128, 512], F32))
                gb_cum = E5i(nc.sbuf_tensor("gb_cum", [128, 512], F32))
                gexp = E5i(nc.sbuf_tensor("gexp", [128, 512], F32))
                gnb = E5i(nc.sbuf_tensor("gnb", [128, GCS], F32))
                glam = E5i(nc.sbuf_tensor("glam", [128, GCS], F32))
                gabk = E5i(nc.sbuf_tensor("gabk", [128, 512], BF16))
                gkh = E5i(nc.sbuf_tensor("gkh", [128, 512], BF16))
                kh_tm = E5i(nc.sbuf_tensor("kh_tm", [32, GCS * 128], BF16))
                zeros32 = E5i(nc.sbuf_tensor("zeros32", [128, 32], F32))
                sdve.inc(V.memset(zeros32[:], 0.0))

                wge(GP, smm, smm.n)
                wge(GP, sw, sw.n)
                sw.inc(GP.dma_start(
                    out=wq_sb[:].rearrange("p (i n) -> p i n", i=DT),
                    in_=wqT), 16)
                wge(GP, sw, sw.n)
                sw.inc(GP.dma_start(
                    out=wk_sb[:].rearrange("p (i n) -> p i n", i=DT),
                    in_=wkT), 16)

                xln_buf5 = LN_BUFS["xln"]
                for b in range(B):
                    rs = emit_ln_slab(b)
                    for t in range(TPB):
                        wge(SY, sdve, rs[t])
                        wge(SY, sst, sst.n)
                        sst.inc(SY.dma_start(
                            out=ag3_v.ap()[b, t * 128:(t + 1) * 128, :],
                            in_=xln_buf5[t][:]), 16)
                wge(GP, sst, sst.n)
                allgather(ag3_v.ap(), ag3_v_o.ap())
                AG3V_done = sag.n
                TRANS_done = 0
                for b in range(B):
                    wge(GP, smm, TRANS_done)
                    wge(GP, sw, sw.n)
                    for t in range(TPB):
                        sw.inc(GP.dma_start(
                            out=xln_buf5[t][:],
                            in_=ag3_v.ap()[b, t * 128:(t + 1) * 128, :]), 16)
                    XLN_LD = sw.n
                    ST_prev = sst.n
                    for t in range(TPB):
                        transpose_tile_into(xlnt, t, t, XLN_LD,
                                            ready_ctr=sw, sst_guard=ST_prev)
                    TRANS_done = smm.n
                    wge(SY, sact, sact.n)
                    wge(SY, sst, sst.n)
                    sst.inc(SY.dma_start(out=xlnt_scr[b], in_=xlnt[:]), 16)
                    wge(TN, sw, sw.n)
                    wge(TN, sact, sact.n)
                    last = None
                    for i in range(DT):
                        last = TN.matmul(psC[:],
                                         wq_sb[:, i * 128:(i + 1) * 128],
                                         xlnt[:, i * 512:(i + 1) * 512],
                                         start=(i == 0), stop=(i == DT - 1))
                    smm.inc(last)
                    last = None
                    for i in range(DT):
                        last = TN.matmul(psD[:],
                                         wk_sb[:, i * 128:(i + 1) * 128],
                                         xlnt[:, i * 512:(i + 1) * 512],
                                         start=(i == 0), stop=(i == DT - 1))
                    smm.inc(last)
                    wge(SC, smm, smm.n)
                    sact.inc(SC.activation(out=gq_s[:], in_=psC[:], func=AF.Silu))
                    sact.inc(SC.activation(out=gk_s[:], in_=psD[:], func=AF.Sigmoid))
                    wge(SC, sact, sact.n)
                    sact.inc(SC.activation(out=gsp[:], in_=gk_s[:], func=AF.Exp))
                    wge(SC, sact, sact.n)
                    sact.inc(SC.activation(out=gsp[:], in_=gsp[:], func=AF.Ln,
                                           bias=1.0))
                    wge(V, sact, sact.n)
                    last = None
                    for cc in range(GCS):
                        last = V.tensor_tensor_scan(
                            out=gb_cum[:, cc * GC:(cc + 1) * GC],
                            data0=gsp[:, cc * GC:(cc + 1) * GC], data1=zeros32[:],
                            initial=0.0, op0=ALU.add, op1=ALU.add)
                    CUM = sdve.inc(last)
                    wge(SC, sdve, CUM)
                    sact.inc(SC.activation(out=glam[:], in_=gb_cum[:, GC - 1::GC],
                                           func=AF.Exp, scale=-1.0))
                    wge(V, sdve, CUM)
                    sdve.inc(V.tensor_scalar(out=gnb[:], in0=gb_cum[:, GC - 1::GC],
                                             scalar1=-1.0, scalar2=None,
                                             op0=ALU.mult))
                    wge(SY, sact, sact.n)
                    wge(SY, sst, sst.n)
                    sst.inc(SY.dma_start(out=ag3_lam.ap()[b], in_=glam[:]), 16)
                    sact.inc(SC.activation(out=gexp[:], in_=gb_cum[:], func=AF.Exp,
                                           scale=-1.0, bias=LNSC))
                    wge(V, sact, sact.n)
                    wge(V, sst, sst.n)
                    AW = sdve.inc(V.tensor_tensor(out=gabk[:], in0=gq_s[:],
                                                  in1=gexp[:], op=ALU.mult))
                    wge(SY, sdve, AW)
                    wge(SY, sst, sst.n)
                    sst.inc(SY.dma_start(
                        out=ag3_abk.ap()[b, 0].rearrange("c (p g) -> p c g", p=128),
                        in_=gabk[:].rearrange("p (c g) -> p c g", c=GCS)), 16)
                    wge(SC, sst, sst.n)
                    wge(SC, sdve, sdve.n)
                    sact.inc(SC.activation(out=gexp[:], in_=gb_cum[:], func=AF.Exp))
                    wge(V, sact, sact.n)
                    wge(V, sst, sst.n)
                    BW = sdve.inc(V.tensor_tensor(out=gabk[:], in0=gk_s[:],
                                                  in1=gexp[:], op=ALU.mult))
                    wge(SY, sdve, BW)
                    wge(SY, sst, sst.n)
                    sst.inc(SY.dma_start(
                        out=ag3_abk.ap()[b, 1].rearrange("c (p g) -> p c g", p=128),
                        in_=gabk[:].rearrange("p (c g) -> p c g", c=GCS)), 16)
                    wge(SC, sdve, sdve.n)
                    last = None
                    for cc in range(GCS):
                        last = SC.activation(out=gexp[:, cc * GC:(cc + 1) * GC],
                                             in_=gb_cum[:, cc * GC:(cc + 1) * GC],
                                             func=AF.Exp, bias=gnb[:, cc:cc + 1])
                    sact.inc(last)
                    wge(V, sact, sact.n)
                    KH = sdve.inc(V.tensor_tensor(out=gkh[:], in0=gk_s[:],
                                                  in1=gexp[:], op=ALU.mult))
                    wge(TN, sdve, KH)
                    for cc in range(GCS):
                        bank = (psA, psB)[cc % 2]
                        pe_acquire(bank)
                        tr = TN.matmul(bank[0:32, 0:128],
                                       gkh[:, cc * GC:(cc + 1) * GC],
                                       ident_bf[:], start=True, stop=True)
                        pe_done(bank, smm.inc(tr))
                        rd_acquire(SC, bank)
                        ev = SC.copy(kh_tm[:, cc * 128:(cc + 1) * 128],
                                     bank[0:32, 0:128])
                        rd_done(bank, sact, sact.inc(ev))
                    wge(SY, sact, sact.n)
                    wge(SY, sst, sst.n)
                    sst.inc(SY.dma_start(
                        out=ag3_abk.ap()[b, 2].rearrange("c (t v) -> t c v", t=GC),
                        in_=kh_tm[:].rearrange("t (c v) -> t c v", c=GCS)), 16)

                wge(GP, sst, sst.n)
                allgather(ag3_abk.ap(), ag3_abk_o.ap())
                allgather(ag3_lam.ap(), ag3_lam_o.ap())
                AG3_done = sag.n
                c5i.close()

            with ExitStack() as c6:
                E6 = c6.enter_context
                phase_fence()
                wge(GP, sw, sw.n)
                sw.inc(GP.dma_start(out=wog_bf.ap(), in_=wogT), 16)
                wge(GP, sw, sw.n)
                sw.inc(GP.dma_start(out=wig_bf.ap(), in_=wigT), 16)
                ch_ab = [E6(nc.sbuf_tensor(f"ch_ab{i}", [128, 2 * GCS * GC], BF16))
                         for i in range(2)]
                ch_kh = [E6(nc.sbuf_tensor(f"ch_kh{i}", [32, GCS * 128], BF16))
                         for i in range(2)]
                ch_v = [E6(nc.sbuf_tensor(f"ch_v{i}", [32, 4 * VH], BF16))
                        for i in range(2)]
                ch_lam = [E6(nc.sbuf_tensor(f"ch_lam{i}", [128, GCS], F32))
                          for i in range(2)]
                S_f32 = E6(nc.sbuf_tensor("S_f32", [128, VH], F32))
                S_bf = E6(nc.sbuf_tensor("S_bf", [128, VH], BF16))
                ptg_sb = [E6(nc.sbuf_tensor(f"ptg{i}", [32, 32], BF16))
                          for i in range(2)]
                oT_sb = E6(nc.sbuf_tensor("oT_sb", [128, 8 * 128], BF16))
                o2_st = E6(nc.sbuf_tensor("o2_st", [1, 128], F32))
                osq = E6(nc.sbuf_tensor("osq6", [128, 1024], BF16))

                wge(V, sag, AG3_done)
                sdve.inc(V.memset(S_f32[:], 0.0))
                wge(SC, sdve, sdve.n)
                sbf_cnt = sact.inc(SC.copy(S_bf[:], S_f32[:]))
                wge(SY, sag, AG3V_done)
                sld.inc(SY.dma_start(
                    out=v_scr.ap().unsqueeze(0),
                    in_=ag3_v_o.ap().rearrange(
                        "r b t d -> b r t d")[bch_slice, :, :, vh_cols]), 16)
                wge(SY, sld, sld.n)
                wge(SY, sag, AG3_done)
                sld.inc(SY.dma_start(
                    out=abk_scr.ap().unsqueeze(0),
                    in_=ag3_abk_o.ap().rearrange(
                        "r b k c m -> b r k c m")[bch_slice]), 16)
                wge(SY, sld, sld.n)
                sld.inc(SY.dma_start(
                    out=lam_scr.ap().unsqueeze(0),
                    in_=ag3_lam_o.ap().rearrange(
                        "r b p c -> b r p c")[bch_slice]), 16)
                wge(SY, sld, sld.n)

                ub = [(psE, psF), (psC, psD)]
                sec_used = [0, 0]
                v_used = [0, 0]
                ld_sec = [0, 0]
                ld_v = [0, 0]
                u_cnt = {}
                p_cnt = {}
                mk_cnt = {}
                OMM = smm.n
                SUPD = sdve.n

                def emit_loads(g):
                    r, cc = g >> 4, g & 15
                    blkk = g // 4
                    if cc == 0:
                        par = r % 2
                        wge(SY, smm, sec_used[par])
                        wge(SY, sld, sld.n)
                        sld.inc(SY.dma_start(
                            out=ch_ab[par][:, 0:GCS * GC].rearrange(
                                "p (c g2) -> p c g2", c=GCS),
                            in_=abk_scr.ap()[r, 0].rearrange(
                                "c (p g2) -> p c g2", p=128)), 16)
                        sld.inc(SY.dma_start(
                            out=ch_ab[par][:, GCS * GC:2 * GCS * GC].rearrange(
                                "p (c g2) -> p c g2", c=GCS),
                            in_=abk_scr.ap()[r, 1].rearrange(
                                "c (p g2) -> p c g2", p=128)), 16)
                        sld.inc(SY.dma_start(
                            out=ch_kh[par][:].rearrange("t (c v2) -> t c v2", c=GCS),
                            in_=abk_scr.ap()[r, 2].rearrange(
                                "c (t v2) -> t c v2", t=GC)), 16)
                        sld.inc(SY.dma_start(
                            out=ch_lam[par][:],
                            in_=lam_scr.ap()[r]), 16)
                        ld_sec[par] = sld.n
                    if g % 4 == 0:
                        r2, tloc = blkk >> 2, blkk & 3
                        bpar = blkk % 2
                        wge(SY, smm, v_used[bpar])
                        wge(SY, sld, sld.n)
                        sld.inc(SY.dma_start(
                            out=ch_v[bpar][:].rearrange("t (c v2) -> t c v2", c=4),
                            in_=v_scr.ap()[r2].rearrange(
                                "(b2 c t) v2 -> b2 t c v2", c=4, t=GC)[
                                tloc]), 16)
                        ld_v[bpar] = sld.n

                def emit_UP(g):
                    r, cc = g >> 4, g & 15
                    par = r % 2
                    bpar = (g // 4) % 2
                    kh_sl = ch_kh[par][:, cc * 128:(cc + 1) * 128]
                    v_sl = ch_v[bpar][:, (g % 4) * VH:(g % 4 + 1) * VH]
                    a_sl = ch_ab[par][:, cc * GC:(cc + 1) * GC]
                    b_sl = ch_ab[par][:, GCS * GC + cc * GC:
                                      GCS * GC + (cc + 1) * GC]
                    wge(TN, sld, max(ld_sec[par], ld_v[bpar]))
                    ua, ubk = ub[g % 2]
                    pe_acquire(ua, ubk)
                    TN.matmul(ua[:], kh_sl, v_sl[:, 0:512], start=True, stop=True)
                    u2 = TN.matmul(ubk[:], kh_sl, v_sl[:, 512:VH],
                                   start=True, stop=True)
                    u_cnt[g] = smm.inc(u2)
                    pe_done(ua, u_cnt[g])
                    pe_done(ubk, u_cnt[g])
                    pban = (psA, psB)[g % 2]
                    pe_acquire(pban)
                    pt = TN.matmul(pban[0:32, 0:32], b_sl, a_sl,
                                   start=True, stop=True)
                    p_cnt[g] = smm.inc(pt)
                    pe_done(pban, p_cnt[g])
                    rd_acquire(V, pban)
                    wge(V, smm, OMM)
                    mk = V.tensor_tensor(out=ptg_sb[g % 2][:],
                                         in0=pban[0:32, 0:32],
                                         in1=triu_bf[0:32, 0:32], op=ALU.mult)
                    mk_cnt[g] = sdve.inc(mk)
                    rd_done(pban, sdve, mk_cnt[g])

                for g in range(NGC):
                    if g == 0:
                        emit_loads(0)
                        emit_UP(0)
                    if g + 1 < NGC:
                        emit_loads(g + 1)
                        emit_UP(g + 1)
                    r, cc = g >> 4, g & 15
                    blkk = g // 4
                    par = r % 2
                    bpar = blkk % 2
                    a_sl = ch_ab[par][:, cc * GC:(cc + 1) * GC]
                    v_sl = ch_v[bpar][:, (g % 4) * VH:(g % 4 + 1) * VH]
                    wge(TN, sdve, mk_cnt[g])
                    wge(TN, sact, sbf_cnt)
                    pe_acquire(psG)
                    last = None
                    for vt in range(8):
                        TN.matmul(psG[:, vt * 32:(vt + 1) * 32],
                                  v_sl[:, vt * 128:(vt + 1) * 128],
                                  ptg_sb[g % 2][:], start=True, stop=False)
                        last = TN.matmul(psG[:, vt * 32:(vt + 1) * 32],
                                         S_bf[:, vt * 128:(vt + 1) * 128],
                                         a_sl, start=False, stop=True)
                    OMM = smm.inc(last)
                    pe_done(psG, OMM)
                    sec_used[par] = OMM
                    v_used[bpar] = OMM
                    ua, ubk = ub[g % 2]
                    rd_acquire(V, ua, ubk)
                    wge(V, sdve, SUPD)
                    wge(V, sact, sbf_cnt)
                    V.scalar_tensor_tensor(out=S_f32[:, 0:512],
                                           in0=S_f32[:, 0:512],
                                           scalar=ch_lam[par][:, cc:cc + 1],
                                           in1=ua[:],
                                           op0=ALU.mult, op1=ALU.add)
                    s2 = V.scalar_tensor_tensor(out=S_f32[:, 512:VH],
                                                in0=S_f32[:, 512:VH],
                                                scalar=ch_lam[par][:, cc:cc + 1],
                                                in1=ubk[:],
                                                op0=ALU.mult, op1=ALU.add)
                    SUPD = sdve.inc(s2)
                    rd_done(ua, sdve, SUPD)
                    rd_done(ubk, sdve, SUPD)
                    wge(SC, sdve, SUPD)
                    wge(SC, smm, OMM)
                    cpS = SC.copy(S_bf[:], S_f32[:])
                    sbf_cnt = sact.inc(cpS)
                    rd_acquire(SC, psG)
                    wge(SC, sst, sst.n)
                    ev = SC.copy(
                        oT_sb[:].rearrange("p (v2 t) -> p v2 t", v2=8)[
                            :, :, (g % 4) * 32:(g % 4) * 32 + 32],
                        psG[:].rearrange("p (v2 t) -> p v2 t", v2=8))
                    rd_done(psG, sact, sact.inc(ev))
                    if g % 4 == 3:
                        tb = blkk
                        wge(SC, sact, sact.n)
                        sq = SC.activation(out=osq[:], in_=oT_sb[:],
                                           func=AF.Square)
                        sact.inc(sq)
                        wge(TN, sact, sact.n)
                        pe_acquire(ps_row)
                        last = None
                        for vt in range(8):
                            last = TN.matmul(ps_row[0:1, 0:128], ones_col_bf[:],
                                             osq[:, vt * 128:(vt + 1) * 128],
                                             start=(vt == 0), stop=(vt == 7))
                        pe_done(ps_row, smm.inc(last))
                        rd_acquire(V, ps_row)
                        wge(V, sst, sst.n)
                        cpo = V.tensor_copy(out=o2_st[:], in_=ps_row[0:1, 0:128])
                        rd_done(ps_row, sdve, sdve.inc(cpo))
                        wge(SY, sdve, sdve.n)
                        wge(SY, sst, sst.n)
                        sst.inc(SY.dma_start(
                            out=ag4_o2.ap()[:, tb * 128:(tb + 1) * 128],
                            in_=o2_st[:]), 16)
                        wge(SY, sact, sact.n)
                        wge(SY, sst, sst.n)
                        sst.inc(SY.dma_start(
                            out=ag4_ot.ap()[:, :, tb * 128:(tb + 1) * 128]
                            .rearrange("v p t -> p v t"),
                            in_=oT_sb[:].rearrange("p (v2 t) -> p v2 t", v2=8)), 16)
                wge(GP, sst, sst.n)
                allgather(ag4_ot.ap(), ag4_ot_o.ap())
                allgather(ag4_o2.ap(), ag4_o2_o.ap())
                AG4_done = sag.n

            with ExitStack() as c7a:
                E7a = c7a.enter_context
                phase_fence()
                wogs = [E7a(nc.sbuf_tensor(f"wogs{i}", [128, DT * 256], BF16))
                        for i in range(2)]
                wigs = [E7a(nc.sbuf_tensor(f"wigs{i}", [128, DT * 256], BF16))
                        for i in range(2)]
                xlnt7 = E7a(nc.sbuf_tensor("xlnt7", [128, DT * SLAB], BF16))
                otp_all = [E7a(nc.sbuf_tensor(f"otp7_{t}", [128, D], BF16))
                           for t in range(TPB)]
                gw_bc = E7a(nc.sbuf_tensor("gw_bc", [128, D], BF16))
                ep_sil = [E7a(nc.sbuf_tensor(f"epsil{i}", [128, 256], BF16))
                          for i in range(2)]
                ep_t1c = E7a(nc.sbuf_tensor("ept1c", [128, 256], F32))
                rstd_c = E7a(nc.sbuf_tensor("rstdc", [128, 16], F32))
                ep_o2 = E7a(nc.sbuf_tensor("epo2", [128, 2], F32))

                for q4 in range(4):
                    wge(SY, smm, smm.n)
                    wge(SY, sld, sld.n)
                    sld.inc(SY.dma_start(
                        out=brow_buf[:],
                        in_=g_w.unsqueeze(0)[0:1, q4 * 512:(q4 + 1) * 512]), 16)
                    wge(TN, sld, sld.n)
                    bank = (psC, psD, psE, psF)[q4]
                    pe_acquire(bank)
                    gwm = TN.matmul(bank[:], ones_row_f32[:], brow_buf[:],
                                    start=True, stop=True)
                    pe_done(bank, smm.inc(gwm))
                    rd_acquire(SC, bank)
                    cpg = SC.copy(gw_bc[:, q4 * 512:(q4 + 1) * 512], bank[:])
                    rd_done(bank, sact, sact.inc(cpg))

                wge(SY, sag, AG4_done)
                wge(SY, sld, sld.n)
                sld.inc(SY.dma_start(
                    out=ep_ot_scr.ap(),
                    in_=ag4_ot_o.ap()[:, :, :, colslab]), 16)
                wge(SY, sld, sld.n)
                sld.inc(SY.dma_start(
                    out=ep_o2_scr.ap(),
                    in_=ag4_o2_o.ap()[:, :, colslab]), 16)
                wge(SY, sld, sld.n)
                o2_rows = ep_o2_scr.ap().rearrange("r a t -> (r a t)").unsqueeze(1)

                otp_used = smm.n
                xl_used = smm.n
                o2_used = sdve.n
                pan_used = [smm.n, smm.n]
                sil_used = [sdve.n, sdve.n]
                for b in range(B):
                    wge(SY, smm, xl_used)
                    wge(SY, sld, sld.n)
                    sld.inc(SY.dma_start(out=xlnt7[:], in_=xlnt_scr[b]), 16)
                    wge(SY, smm, otp_used)
                    for tl in range(TPB):
                        for vh2 in range(2):
                            rr = 2 * b + vh2
                            sld.inc(SY.dma_start(
                                out=otp_all[tl][:, vh2 * 1024:(vh2 + 1) * 1024]
                                .rearrange("p (v2 t) -> p v2 t", v2=8),
                                in_=ep_ot_scr.ap()[rr, :, :,
                                                   tl * 128:(tl + 1) * 128]
                                .rearrange("v p t -> p v t")), 16)
                    BLD = sld.n
                    for hp in range(8):
                        pbw = wogs[hp % 2]
                        pbi = wigs[hp % 2]
                        psem = (sw, sw2)[hp % 2]
                        wge(GP, smm, pan_used[hp % 2])
                        wge(GP, psem, psem.n)
                        psem.inc(GP.dma_start(
                            out=pbw[:].rearrange("p (i n) -> p i n", i=DT),
                            in_=wog_bf.ap()[hp]), 16)
                        psem.inc(GP.dma_start(
                            out=pbi[:].rearrange("p (i n) -> p i n", i=DT),
                            in_=wig_bf.ap()[hp]), 16)
                        PANLD = psem.n
                        wge(TN, psem, PANLD)
                        wge(TN, sld, BLD)
                        for tl in range(TPB):
                            idx = b * TPB + tl
                            par = (hp * TPB + tl) % 2
                            if hp == 0:
                                wge(SY, sdve, o2_used)
                                wge(SY, sld, sld.n)
                                for vh2 in range(2):
                                    rr = 2 * b + vh2
                                    sld.inc(SY.dma_start(
                                        out=ep_o2[:, vh2:vh2 + 1],
                                        in_=o2_rows[rr * 512 + tl * 128:
                                                    rr * 512 + (tl + 1) * 128,
                                                    :]), 16)
                                O2LD = sld.n
                                wge(V, sld, O2LD)
                                rd = V.tensor_reduce(
                                    out=rstd_c[:, idx:idx + 1], in_=ep_o2[:],
                                    axis=mybir.AxisListType.X, op=ALU.add)
                                o2_used = sdve.inc(rd)
                                wge(V, sdve, sdve.n)
                                sdve.inc(V.tensor_scalar(
                                    out=rstd_c[:, idx:idx + 1],
                                    in0=rstd_c[:, idx:idx + 1],
                                    scalar1=1.0 / D, scalar2=None, op0=ALU.mult))
                                wge(SC, sdve, sdve.n)
                                sact.inc(SC.activation(
                                    out=rstd_c[:, idx:idx + 1],
                                    in_=rstd_c[:, idx:idx + 1],
                                    func=AF.Ln, bias=EPS))
                                wge(SC, sact, sact.n)
                                sact.inc(SC.activation(
                                    out=rstd_c[:, idx:idx + 1],
                                    in_=rstd_c[:, idx:idx + 1],
                                    func=AF.Exp, scale=-0.5))
                            bank = (psC, psD)[tl % 2]
                            pe_acquire(bank)
                            last = None
                            for i in range(DT):
                                last = TN.matmul(
                                    bank[:, 0:256],
                                    xlnt7[:, i * 512 + tl * 128:
                                          i * 512 + (tl + 1) * 128],
                                    pbi[:, i * 256:(i + 1) * 256],
                                    start=(i == 0), stop=False)
                            for i in range(DT):
                                last = TN.matmul(
                                    bank[:, 0:256],
                                    otp_all[tl][:, i * 128:(i + 1) * 128],
                                    pbw[:, i * 256:(i + 1) * 256],
                                    start=False, stop=(i == DT - 1))
                            pe_done(bank, smm.inc(last))
                            rd_acquire(SC, bank)
                            wge(SC, sdve, sil_used[par])
                            sl = SC.activation(out=ep_sil[par][:], in_=bank[:, 0:256],
                                               func=AF.Silu)
                            rd_done(bank, sact, sact.inc(sl))
                            tb_bank = (psA, psB)[par]
                            pe_acquire(tb_bank)
                            tlast = None
                            for u in range(2):
                                i = hp * 2 + u
                                tlast = TN.matmul(
                                    tb_bank[:, u * 128:(u + 1) * 128],
                                    otp_all[tl][:, i * 128:(i + 1) * 128],
                                    ident_bf[:], start=True, stop=True)
                            pe_done(tb_bank, smm.inc(tlast))
                            rd_acquire(V, tb_bank)
                            wge(V, sact, sact.n)
                            wge(V, sdve, sdve.n)
                            t1 = V.scalar_tensor_tensor(
                                out=ep_t1c[:], in0=tb_bank[:, 0:256],
                                scalar=rstd_c[:, idx:idx + 1],
                                in1=gw_bc[:, hp * 256:(hp + 1) * 256],
                                op0=ALU.mult, op1=ALU.mult)
                            rd_done(tb_bank, sdve, sdve.inc(t1))
                            wge(V, sdve, sdve.n)
                            t2 = V.tensor_tensor(out=ep_t1c[:], in0=ep_t1c[:],
                                                 in1=ep_sil[par][:], op=ALU.mult)
                            sdve.inc(t2)
                            wge(V, sdve, sdve.n)
                            ad = V.tensor_tensor(
                                out=xr[b][tl][:, hp * 256:(hp + 1) * 256],
                                in0=ep_t1c[:],
                                in1=xr[b][tl][:, hp * 256:(hp + 1) * 256],
                                op=ALU.add)
                            sil_used[par] = sdve.inc(ad)
                        pan_used[hp % 2] = smm.n
                    otp_used = smm.n
                    xl_used = smm.n

        if n_sl >= 4:
            emit_ffn("b", final=True)
        else:
            wge(SY, sdve, sdve.n)
            wge(SY, sact, sact.n)
            wge(SY, smm, smm.n)
            for b in range(B):
                for t in range(TPB):
                    sout.inc(SY.dma_start(
                        out=y_out[b, t * 128:(t + 1) * 128, :],
                        in_=xr[b][t][:]), 16)

        wge(SY, sout, sout.n)

    return nc


def make_in_maps(inputs):
    x = np.asarray(inputs["L0_attn_a_x"])
    tri = np.triu(np.ones((128, 128), np.float32))
    ident = np.eye(128, dtype=np.float32)
    base = {
        "w_qkv": np.ascontiguousarray(
            np.asarray(inputs["L0_attn_a_W_qkv"]).reshape(
                16, 128, 24, 256).transpose(2, 1, 0, 3)),
        "b_qkv": np.asarray(inputs["L0_attn_a_b_qkv"]),
        "b_qkv_col": np.ascontiguousarray(
            np.asarray(inputs["L0_attn_a_b_qkv"]).reshape(48, 128).T),
        "w_proj": np.ascontiguousarray(
            np.asarray(inputs["L0_attn_a_W_proj"]).reshape(
                16, 128, 8, 256).transpose(2, 1, 0, 3)),
        "b_proj": np.asarray(inputs["L0_attn_a_b_proj"]),
        "w_baseT": np.ascontiguousarray(
            np.asarray(inputs["L0_ffn_W_base"]).T.reshape(
                16, 128, 8, 256).transpose(2, 1, 0, 3)),
        "b_base": np.asarray(inputs["L0_ffn_b_base"]),
        "routerT": np.ascontiguousarray(
            np.asarray(inputs["L0_ffn_router_W"]).T.reshape(
                16, 128, 8).transpose(1, 0, 2)),
        "a_T": np.ascontiguousarray(
            np.asarray(inputs["L0_ffn_A"]).reshape(128, D).T.reshape(
                16, 128, 128).transpose(1, 0, 2)),
        "bw_flat": np.ascontiguousarray(
            np.asarray(inputs["L0_ffn_Bw"]).transpose(0, 2, 1).reshape(128, D)),
        "wqT": np.ascontiguousarray(
            np.asarray(inputs["L0_attn_b_Wq"]).T.reshape(
                16, 128, 128).transpose(1, 0, 2)),
        "wkT": np.ascontiguousarray(
            np.asarray(inputs["L0_attn_b_Wk"]).T.reshape(
                16, 128, 128).transpose(1, 0, 2)),
        "wogT": np.ascontiguousarray(
            np.asarray(inputs["L0_attn_b_Wog"]).T.reshape(
                16, 128, 8, 256).transpose(2, 1, 0, 3)),
        "wigT": np.ascontiguousarray(
            np.asarray(inputs["L0_attn_b_Wig"]).T.reshape(
                16, 128, 8, 256).transpose(2, 1, 0, 3)),
        "g_w": np.asarray(inputs["L0_attn_b_g_weight"]),
        "triu_in": tri,
        "ident_in": ident,
    }
    in_maps = []
    for c in range(NR):
        m = dict(base)
        m["x_my"] = np.ascontiguousarray(x[:, c * SLAB:(c + 1) * SLAB, :])
        in_maps.append(m)
    return in_maps


def kernel(**inputs):
    nc = build(N_SL)
    in_maps = make_in_maps(inputs)
    res = run_bass_kernel_spmd(nc, in_maps, core_ids=list(range(NR)))
    y = np.empty((B, T, D), np.float32)
    for c in range(NR):
        y[:, c * SLAB:(c + 1) * SLAB, :] = np.asarray(res.results[c]["y_my"])
    return y


if __name__ == "__main__":
    nc = build(1)
    print("built OK; instructions:",
          sum(len(b.instructions) for b in nc.m.functions[0].blocks))


# version 17
# speedup vs baseline: 1.0055x; 1.0055x over previous
import math
import numpy as np
import ml_dtypes
import concourse.bass as bass
import concourse.mybir as mybir
from concourse.bass_utils import run_bass_kernel_spmd
from contextlib import ExitStack

F32 = mybir.dt.float32
BF16 = mybir.dt.bfloat16
AF = mybir.ActivationFunctionType
ALU = mybir.AluOpType

B, T, D, H, HD = 4, 4096, 2048, 16, 128
NR = 8
SLAB = T // NR
TPB = SLAB // 128
DT = D // 128
GC = 32
GCS = SLAB // GC
NGC = T // GC
VH = D // 2
SCALE = HD ** -0.5
LNSC = math.log(SCALE)
SHIFT = -12.0
EPS = 1e-5
N_SL = 4


class Ctr:
    def __init__(self, ctx, nc, name):
        self.h = ctx.enter_context(nc.semaphore(name))
        self.n = 0

    def inc(self, instr, k=1):
        instr.then_inc(self.h, k)
        self.n += k
        return self.n


def build(n_sl=N_SL):
    nc = bass.Bass("TRN2", target_bir_lowering=False, debug=False)

    x_in = nc.dram_tensor("x_my", [B, SLAB, D], F32, kind="ExternalInput").ap()
    w_qkv = nc.dram_tensor("w_qkv", [24, 128, DT, 256], F32,
                           kind="ExternalInput").ap()
    b_qkv = nc.dram_tensor("b_qkv", [3 * D], F32, kind="ExternalInput").ap()
    b_qkv_col = nc.dram_tensor("b_qkv_col", [128, 48], F32, kind="ExternalInput").ap()
    w_proj = nc.dram_tensor("w_proj", [8, 128, DT, 256], F32,
                            kind="ExternalInput").ap()
    b_proj = nc.dram_tensor("b_proj", [D], F32, kind="ExternalInput").ap()
    w_baseT = nc.dram_tensor("w_baseT", [8, 128, DT, 256], F32,
                             kind="ExternalInput").ap()
    b_base = nc.dram_tensor("b_base", [D], F32, kind="ExternalInput").ap()
    routerT = nc.dram_tensor("routerT", [128, DT, 8], F32,
                             kind="ExternalInput").ap()
    a_T = nc.dram_tensor("a_T", [128, DT, 128], F32,
                         kind="ExternalInput").ap()
    bw_flat = nc.dram_tensor("bw_flat", [128, D], F32, kind="ExternalInput").ap()
    wqT = nc.dram_tensor("wqT", [128, DT, HD], F32,
                         kind="ExternalInput").ap()
    wkT = nc.dram_tensor("wkT", [128, DT, HD], F32,
                         kind="ExternalInput").ap()
    wogT = nc.dram_tensor("wogT", [8, 128, DT, 256], F32,
                          kind="ExternalInput").ap()
    wigT = nc.dram_tensor("wigT", [8, 128, DT, 256], F32,
                          kind="ExternalInput").ap()
    g_w = nc.dram_tensor("g_w", [D], F32, kind="ExternalInput").ap()
    triu_in = nc.dram_tensor("triu_in", [128, 128], F32, kind="ExternalInput").ap()
    ident_in = nc.dram_tensor("ident_in", [128, 128], F32, kind="ExternalInput").ap()
    y_out = nc.dram_tensor("y_my", [B, SLAB, D], F32, kind="ExternalOutput").ap()

    SEC = 128 * 512
    ag1_in = [nc.dram_tensor(f"ag1_in{b}", [H, 3, SEC], BF16) for b in range(B)]
    ag1_out = [nc.dram_tensor(f"ag1_out{b}", [NR, H, 3, SEC], BF16,
                              addr_space="Shared") for b in range(B)]
    ag2_in = [[nc.dram_tensor(f"ag2_in{b}_{j}", [1, 128, T], BF16)
               for j in range(2)] for b in range(B)]
    ag2_out = [[nc.dram_tensor(f"ag2_out{b}_{j}", [NR, 1, 128, T], BF16,
                               addr_space="Shared") for j in range(2)]
               for b in range(B)]
    ag2_out_unused = [nc.dram_tensor(f"ag2_ou{b}", [NR, 2, 128, T], BF16,
                              addr_space="Shared") for b in range(B)]
    pair_scr = [nc.dram_tensor(f"pair_scr{i}", [NR, 3, SEC], BF16)
                for i in range(2)]
    proj_scr = nc.dram_tensor("proj_scr", [B, NR, 2, 128, 512], BF16)
    ag3_abk = nc.dram_tensor("ag3_abk", [B, 3, GCS, 128 * GC], BF16)
    ag3_abk_o = nc.dram_tensor("ag3_abk_o", [NR, B, 3, GCS, 128 * GC], BF16,
                               addr_space="Shared")
    ag3_lam = nc.dram_tensor("ag3_lam", [B, 128, GCS], F32)
    ag3_lam_o = nc.dram_tensor("ag3_lam_o", [NR, B, 128, GCS], F32,
                               addr_space="Shared")
    ag3_v = nc.dram_tensor("ag3_v", [B, SLAB, D], BF16)
    ag3_v_o = nc.dram_tensor("ag3_v_o", [NR, B, SLAB, D], BF16,
                             addr_space="Shared")
    ag4_ot = nc.dram_tensor("ag4_ot", [8, 128, T], BF16)
    ag4_ot_o = nc.dram_tensor("ag4_ot_o", [NR, 8, 128, T], BF16,
                              addr_space="Shared")
    ag4_o2 = nc.dram_tensor("ag4_o2", [1, T], F32)
    ag4_o2_o = nc.dram_tensor("ag4_o2_o", [NR, 1, T], F32, addr_space="Shared")
    go_scr = nc.dram_tensor("go_scr", [B, SLAB, D], F32).ap()
    wog_bf = nc.dram_tensor("wog_bf", [8, 128, DT, 256], BF16)
    wig_bf = nc.dram_tensor("wig_bf", [8, 128, DT, 256], BF16)
    abk_scr = nc.dram_tensor("abk_scr", [NR, 3, GCS, 128 * GC], BF16)
    lam_scr = nc.dram_tensor("lam_scr", [NR, 128, GCS], F32)
    v_scr = nc.dram_tensor("v_scr", [NR, SLAB, VH], BF16)
    ep_ot_scr = nc.dram_tensor("ep_ot_scr", [NR, 8, 128, 512], BF16)
    ep_o2_scr = nc.dram_tensor("ep_o2_scr", [NR, 1, 512], F32)
    xlnt_scr = nc.dram_tensor("xlnt_scr", [B, 128, DT * SLAB], BF16).ap()

    with ExitStack() as ctx:
        E = ctx.enter_context

        xr = [[E(nc.sbuf_tensor(f"xr{b}_{t}", [128, D], F32)) for t in range(TPB)]
              for b in range(B)]
        triu_bf = E(nc.sbuf_tensor("triu_bf", [128, 128], BF16))
        ident_bf = E(nc.sbuf_tensor("ident_bf", [128, 128], BF16))
        ones_col_bf = E(nc.sbuf_tensor("ones_col_bf", [128, 1], BF16))
        ones_row_f32 = E(nc.sbuf_tensor("ones_row_f32", [1, 128], F32))
        consts = E(nc.sbuf_tensor("consts", [128, 16], F32))
        bias_qkv_col = E(nc.sbuf_tensor("bias_qkv_col", [128, 48], F32))
        brow_buf = E(nc.sbuf_tensor("brow_buf", [1, 512], F32))
        brow_bf = E(nc.sbuf_tensor("brow_bf", [1, 512], BF16))
        ones_row_bf = E(nc.sbuf_tensor("ones_row_bf", [1, 128], BF16))
        ln_sum = E(nc.sbuf_tensor("ln_sum", [128, 4], F32))
        ln_sq = E(nc.sbuf_tensor("ln_sq", [128, 4], F32))
        ln_m = E(nc.sbuf_tensor("ln_m", [128, 4], F32))
        ln_v = E(nc.sbuf_tensor("ln_v", [128, 4], F32))
        LN_BUFS = {}
        stage_bf = [E(nc.sbuf_tensor(f"stgbf{i}", [128, 512], BF16)) for i in range(2)]

        psA = E(nc.psum_tensor("psA", [128, 512], F32))
        psB = E(nc.psum_tensor("psB", [128, 512], F32))
        psC = E(nc.psum_tensor("psC", [128, 512], F32))
        psD = E(nc.psum_tensor("psD", [128, 512], F32))
        psE = E(nc.psum_tensor("psE", [128, 512], F32))
        psF = E(nc.psum_tensor("psF", [128, 512], F32))
        psG = E(nc.psum_tensor("psG", [128, 256], F32))
        ps_row = E(nc.psum_tensor("ps_row", [1, 512], F32))

        sx = Ctr(ctx, nc, "sx")
        sw = Ctr(ctx, nc, "sw")
        sld = Ctr(ctx, nc, "sld")
        sst = Ctr(ctx, nc, "sst")
        sag = Ctr(ctx, nc, "sag")
        smm = Ctr(ctx, nc, "smm")
        sact = Ctr(ctx, nc, "sact")
        sdve = Ctr(ctx, nc, "sdve")
        sout = Ctr(ctx, nc, "sout")
        sw2 = Ctr(ctx, nc, "sw2")
        st1s = [Ctr(ctx, nc, f"st1_{i}") for i in range(4)]
        sw3 = Ctr(ctx, nc, "sw3")
        spre = Ctr(ctx, nc, "spre")

        blk = E(nc.Block())

        CONST_VALUES = sorted({SHIFT, SCALE, LNSC, -1.0, -0.5, 2.0, 1.0 / D,
                               EPS, 0.5, 1.0, 1.0 / HD})
        for i, val in enumerate(CONST_VALUES):
            nc.const_aps.register(F32, float(val), consts[:, i:i + 1])

        pid = nc.partition_id()
        V, SC, TN, SY, GP = nc.vector, nc.scalar, nc.tensor, nc.sync, nc.gpsimd
        LAST_WAIT = {}

        def wge(eng, ctr, count):
            key = (id(eng), id(ctr))
            if LAST_WAIT.get(key, -1) >= count:
                return
            eng.wait_ge(ctr.h, count)
            LAST_WAIT[key] = count
        pid_s = nc.snap(pid)
        h_slice = [bass.ts(nc.snap(pid * 2 + j), 1) for j in range(2)]
        bch_slice = bass.ts(nc.snap(pid // 2), 1)
        vh_s = nc.snap(pid % 2)
        vh_cols = bass.ts(vh_s, VH)
        colslab = bass.ts(pid_s, 512)
        slab_slice = bass.ts(pid_s, 1)

        for b in range(B):
            for t in range(TPB):
                sx.inc(SY.dma_start(out=xr[b][t][:],
                                    in_=x_in[b, t * 128:(t + 1) * 128, :]), 16)
        sx.inc(SY.dma_start(out=bias_qkv_col[:], in_=b_qkv_col), 16)
        wge(GP, sw, sw.n)
        sw.inc(GP.dma_start(out=triu_bf[:], in_=triu_in), 16)
        wge(GP, sw, sw.n)
        sw.inc(GP.dma_start(out=ident_bf[:], in_=ident_in), 16)
        for i, val in enumerate(CONST_VALUES):
            V.memset(consts[:, i:i + 1], float(val))
        V.memset(ones_col_bf[:], 1.0)
        V.memset(ones_row_bf[:], 1.0)
        sdve.inc(V.memset(ones_row_f32[:], 1.0))

        for eng in (V, SC, TN, SY, GP):
            wge(eng, sx, sx.n)
            wge(eng, sw, sw.n)
            wge(eng, sdve, sdve.n)

        BANK_WR = {}
        BANK_RD = {}

        def pe_acquire(*banks):
            for bank in banks:
                c = BANK_RD.get(bank.name)
                if c is not None:
                    wge(TN, c[0], c[1])

        def pe_done(bank, cnt):
            BANK_WR[bank.name] = cnt

        def rd_acquire(eng, *banks):
            for bank in banks:
                c = BANK_WR.get(bank.name)
                if c is not None:
                    wge(eng, smm, c)

        def rd_done(bank, ctr, cnt):
            BANK_RD[bank.name] = (ctr, cnt)

        def emit_ln_slab(b, xln_cons=None, sst_cons=None):
            wge(SC, sdve, sdve.n)
            wge(V, sst, sst.n if sst_cons is None else sst_cons)
            wge(V, smm, smm.n if xln_cons is None else xln_cons)
            ln_sqs, xln_buf = LN_BUFS["sqs"], LN_BUFS["xln"]
            for t in range(TPB):
                i1 = V.tensor_scalar(out=xln_buf[t][:], in0=xr[b][t][:], scalar1=0.0,
                                     scalar2=0.0, op0=ALU.add, op1=ALU.add,
                                     accum_out=ln_sum[:, t:t + 1])
                sdve.inc(i1)
                if t < TPB - 1:
                    wge(V, sdve, sdve.n)
            for t in range(TPB):
                i2 = SC.activation(out=ln_sqs[:], in_=xr[b][t][:], func=AF.Square,
                                   accum_out=ln_sq[:, t:t + 1])
                sact.inc(i2)
                if t < TPB - 1:
                    wge(SC, sact, sact.n)
            wge(V, sdve, sdve.n)
            wge(V, sact, sact.n)
            sdve.inc(V.tensor_scalar(out=ln_m[:], in0=ln_sum[:], scalar1=1.0 / D,
                                     scalar2=None, op0=ALU.mult))
            wge(V, sdve, sdve.n)
            sdve.inc(V.scalar_tensor_tensor(out=ln_v[:], in0=ln_m[:], scalar=-1.0,
                                            in1=ln_m[:], op0=ALU.mult,
                                            op1=ALU.mult))
            wge(V, sdve, sdve.n)
            sdve.inc(V.scalar_tensor_tensor(out=ln_v[:], in0=ln_sq[:],
                                            scalar=1.0 / D, in1=ln_v[:],
                                            op0=ALU.mult, op1=ALU.add))
            wge(SC, sdve, sdve.n)
            sact.inc(SC.activation(out=ln_v[:], in_=ln_v[:], func=AF.Ln, bias=EPS))
            wge(SC, sact, sact.n)
            sact.inc(SC.activation(out=ln_v[:], in_=ln_v[:], func=AF.Exp,
                                   scale=-0.5))
            wge(V, sact, sact.n)
            outs = []
            for t in range(TPB):
                i7 = V.tensor_scalar(out=xln_buf[t][:], in0=xr[b][t][:],
                                     scalar1=ln_m[:, t:t + 1],
                                     scalar2=ln_v[:, t:t + 1],
                                     op0=ALU.subtract, op1=ALU.mult)
                outs.append(sdve.inc(i7))
            return outs

        def transpose_tile_into(xlnt_sb, t, xi, xln_ready, xlnt_guard=None,
                                ready_ctr=None, sst_guard=None):
            xln_buf = LN_BUFS["xln"]
            wge(TN, ready_ctr if ready_ctr is not None else sdve, xln_ready)
            for grp in range(4):
                bank = (psA, psB)[grp % 2]
                pe_acquire(bank)
                last = None
                for u in range(4):
                    i = grp * 4 + u
                    last = TN.matmul(bank[:, u * 128:(u + 1) * 128],
                                     xln_buf[xi][:, i * 128:(i + 1) * 128],
                                     ident_bf[:], start=True, stop=True)
                pe_done(bank, smm.inc(last))
                rd_acquire(SC, bank)
                if xlnt_guard is not None:
                    wge(SC, smm, xlnt_guard)
                if sst_guard is not None:
                    wge(SC, sst, sst_guard)
                dst = xlnt_sb[:].rearrange("p (i n) -> p i n", i=DT)[
                    :, grp * 4:(grp + 1) * 4, t * 128:(t + 1) * 128]
                ev = SC.copy(dst, bank[:].rearrange("p (u n) -> p u n", u=4))
                rd_done(bank, sact, sact.inc(ev))

        def ln_alloc(EE, tag):

            LN_BUFS["sqs"] = EE(nc.sbuf_tensor(f"lnsqs{tag}", [128, D], BF16))
            LN_BUFS["xln"] = [EE(nc.sbuf_tensor(f"xln{tag}{i}", [128, D], BF16))
                              for i in range(4)]

        def emit_ln_transpose_slab(b, xlnt_sb, store_scr=False, v_dst=None):
            rs = emit_ln_slab(b)
            for t in range(TPB):
                transpose_tile_into(xlnt_sb, t, t, rs[t])
                if v_dst is not None:
                    wge(SY, sdve, rs[t])
                    wge(SY, sst, sst.n)
                    sst.inc(SY.dma_start(
                        out=v_dst[b, t * 128:(t + 1) * 128, :],
                        in_=LN_BUFS["xln"][t][:]), 16)
            if store_scr:
                wge(SY, sact, sact.n)
                wge(SY, sst, sst.n)
                sst.inc(SY.dma_start(out=xlnt_scr[b], in_=xlnt_sb[:]), 16)

        def load_panel(buf_sb, src_cols_ap, ncols):
            nt = DT
            return GP.dma_start(
                out=buf_sb[:].rearrange("p (i n) -> p i n", i=nt),
                in_=src_cols_ap.rearrange("(i p) n -> p i n", p=128))

        def allgather(in_ap, out_ap):
            return sag.inc(GP.collective_compute(
                "AllGather", ALU.bypass, replica_groups=[list(range(NR))],
                ins=[in_ap.opt()], outs=[out_ap.opt()]))

        def phase_fence():
            for eng in (V, SC, TN, SY, GP):
                for c in (sld, sst, smm, sact, sdve, sw, sw2, sw3, spre,
                          st1s[0], st1s[1], st1s[2], st1s[3]):
                    wge(eng, c, c.n)

        if n_sl >= 1:
            with ExitStack() as c1:
                E1 = c1.enter_context
                xlnt2 = [E1(nc.sbuf_tensor(f"xlnt1_{i}", [128, DT * SLAB], BF16))
                         for i in range(2)]
                ln_alloc(E1, "p1")
                wpan = [E1(nc.sbuf_tensor(f"wpan1_{i}", [128, DT * 256], BF16))
                        for i in range(2)]
                stg1 = [E1(nc.sbuf_tensor(f"stg1_{i}", [128, 512], BF16))
                        for i in range(4)]
                AG1_done = []
                panel_used = [0, 0]
                stage_st = [0, 0, 0, 0]
                stg_i = [0]
                brow_rd = 0
                QEND = 0
                p1_sst0 = sst.n
                prev_trans = 0
                for pr in range(2):
                    for bb in range(2):
                        b = 2 * pr + bb
                        rs = emit_ln_slab(b, xln_cons=prev_trans,
                                          sst_cons=p1_sst0)
                        for t in range(TPB):
                            transpose_tile_into(xlnt2[bb], t, t, rs[t],
                                                xlnt_guard=QEND if pr else None)
                        prev_trans = smm.n
                    trans_done = sact.n
                    for pan in range(24):
                        pbuf = wpan[pan % 2]
                        psem = (sw, sw2)[pan % 2]
                        GP.wait_ge(smm.h, panel_used[pan % 2])
                        wge(GP, psem, psem.n)
                        psem.inc(GP.dma_start(
                            out=pbuf[:].rearrange("p (i n) -> p i n", i=DT),
                            in_=w_qkv[pan]), 16)
                        PANLD = psem.n
                        if pan >= 16:
                            pv = pan - 16
                            wge(GP, sw3, sw3.n)
                            wge(GP, smm, brow_rd)
                            sw3.inc(GP.dma_start(
                                out=brow_bf[0:1, 0:256],
                                in_=b_qkv.unsqueeze(0)[
                                    0:1, 2 * D + pv * 256:
                                    2 * D + (pv + 1) * 256]), 16)
                            BROWLD = sw3.n
                        wge(TN, psem, PANLD)
                        if pan == 0:
                            wge(TN, sact, trans_done)
                        if pan < 16:
                            for su in range(2):
                                ot = pan * 2 + su
                                for bb in range(2):
                                    b = 2 * pr + bb
                                    bank = (psC, psD, psE, psF)[
                                        (su % 2) * 2 + bb]
                                    pe_acquire(bank)
                                    last = None
                                    for i in range(DT):
                                        last = TN.matmul(
                                            bank[:],
                                            pbuf[:, i * 256 + su * 128:
                                                 i * 256 + (su + 1) * 128],
                                            xlnt2[bb][:, i * 512:(i + 1) * 512],
                                            start=(i == 0), stop=(i == DT - 1))
                                    pe_done(bank, smm.inc(last))
                                    rd_acquire(SC, bank)
                                    sl = stg_i[0] % 4
                                    stg_i[0] += 1
                                    wge(SC, st1s[sl], stage_st[sl])
                                    st = stg1[sl]
                                    ev = SC.activation(
                                        out=st[:], in_=bank[:],
                                        func=AF.Identity,
                                        bias=bias_qkv_col[:, ot:ot + 1])
                                    rd_done(bank, sact, sact.inc(ev))
                                    wge(SY, sact, sact.n)
                                    hh, kind = (ot, 0) if ot < 16 else (ot - 16, 1)
                                    dst = ag1_in[b].ap()[hh, kind].rearrange(
                                        "(p n) -> p n", n=512)
                                    stage_st[sl] = st1s[sl].inc(
                                        SY.dma_start(out=dst, in_=st[:]), 16)
                        else:
                            pv = pan - 16
                            wge(TN, sw3, BROWLD)
                            for t in range(TPB):
                                for bb in range(2):
                                    b = 2 * pr + bb
                                    bank = (psC, psD, psE, psF)[
                                        (t % 2) * 2 + bb]
                                    pe_acquire(bank)
                                    last = None
                                    for i in range(DT):
                                        last = TN.matmul(
                                            bank[:, 0:256],
                                            xlnt2[bb][:, i * 512 + t * 128:
                                                      i * 512 + (t + 1) * 128],
                                            pbuf[:, i * 256:(i + 1) * 256],
                                            start=(i == 0), stop=False)
                                    last = TN.matmul(bank[:, 0:256],
                                                     ones_row_bf[:],
                                                     brow_bf[0:1, 0:256],
                                                     start=False, stop=True)
                                    pe_done(bank, smm.inc(last))
                                    brow_rd = smm.n
                                    rd_acquire(SC, bank)
                                    sl = stg_i[0] % 4
                                    stg_i[0] += 1
                                    wge(SC, st1s[sl], stage_st[sl])
                                    st = stg1[sl]
                                    ev = SC.copy(st[:, 0:256], bank[:, 0:256])
                                    rd_done(bank, sact, sact.inc(ev))
                                    wge(SY, sact, sact.n)
                                    dst = ag1_in[b].ap()[
                                        2 * pv:2 * pv + 2, 2].rearrange(
                                        "h (t2 v) -> t2 h v", v=128)[
                                        t * 128:(t + 1) * 128, :, :]
                                    srcv = st[:, 0:256].rearrange(
                                        "p (h v) -> p h v", h=2)
                                    stage_st[sl] = st1s[sl].inc(
                                        SY.dma_start(out=dst, in_=srcv), 16)
                        panel_used[pan % 2] = smm.n
                    QEND = smm.n
                    if pr == 1:
                        for j in range(2):
                            wge(GP, sag, AG1_done[0])
                            if j:
                                wge(GP, spre, 16)
                            gsrc0 = ag1_out[0].ap().rearrange(
                                "r h k m -> h r k m")[h_slice[j]]
                            spre.inc(GP.dma_start(
                                out=pair_scr[j].ap().unsqueeze(0),
                                in_=gsrc0), 16)
                    for bb in range(2):
                        b = 2 * pr + bb
                        wge(GP, sst, sst.n)
                        for k in range(4):
                            wge(GP, st1s[k], st1s[k].n)
                        allgather(ag1_in[b].ap(), ag1_out[b].ap())
                        AG1_done.append(sag.n)


            with ExitStack() as c2:
                E2 = c2.enter_context
                phase_fence()
                kt_sb2 = [E2(nc.sbuf_tensor(f"kt_sb{i}", [128, T], BF16))
                          for i in range(2)]
                v_sb2 = [E2(nc.sbuf_tensor(f"v_sb{i}", [128, T], BF16))
                         for i in range(2)]
                qt_sb2 = [E2(nc.sbuf_tensor(f"qt_sb{i}", [128, T], BF16))
                          for i in range(2)]
                pt_sb = [E2(nc.sbuf_tensor(f"pt{i}", [128, 512], BF16))
                         for i in range(4)]
                attn_d = E2(nc.sbuf_tensor("attn_d", [1, 512], F32))
                attn_db = E2(nc.sbuf_tensor("attn_db", [128, 512], F32))
                wpan3 = [E2(nc.sbuf_tensor(f"wpan3_{i}", [128, DT * 256], BF16))
                         for i in range(2)]

                sbanks = [psA, psB, psE, psF]
                obank = [psC, psD]
                kv_used = [smm.n, smm.n]
                kv_ld = [0, 0]
                pt_pv = [0, 0, 0, 0]
                stage_o = [sst.n, sst.n]
                pend_bc = [None]
                pend_ml = [0]

                def emit_pair_loads(p):
                    b, j = p >> 1, p & 1
                    wge(SY, smm, kv_used[p % 2])
                    if p >= 2:
                        wge(SY, sag, AG1_done[b])
                        wge(SY, sld, sld.n)
                        gsrc = ag1_out[b].ap().rearrange(
                            "r h k m -> h r k m")[h_slice[j]]
                        sld.inc(SY.dma_start(
                            out=pair_scr[p % 2].ap().unsqueeze(0),
                            in_=gsrc), 16)
                        wge(SY, sld, sld.n)
                    else:
                        wge(SY, sld, sld.n)
                        wge(SY, spre, 16 * (p + 1))
                    rows_pn = pair_scr[p % 2].ap().rearrange(
                        "r k (p n) -> (r k p) n", n=512)
                    rows_v = pair_scr[p % 2].ap().rearrange(
                        "r k (j2 p v) -> (r k j2 p) v", j2=4, p=128)
                    for r in range(NR):
                        base = r * 3
                        sld.inc(SY.dma_start(
                            out=qt_sb2[p % 2][:, r * 512:(r + 1) * 512],
                            in_=rows_pn[(base + 0) * 128:(base + 1) * 128, :]), 16)
                        sld.inc(SY.dma_start(
                            out=kt_sb2[p % 2][:, r * 512:(r + 1) * 512],
                            in_=rows_pn[(base + 1) * 128:(base + 2) * 128, :]), 16)
                        for j2l in range(4):
                            rr = (base + 2) * 4 + j2l
                            sld.inc(SY.dma_start(
                                out=v_sb2[p % 2][:, (4 * r + j2l) * 128:
                                                 (4 * r + j2l + 1) * 128],
                                in_=rows_v[rr * 128:(rr + 1) * 128, :]), 16)
                    kv_ld[p % 2] = sld.n

                AG2_done = {}
                po_used = [smm.n]
                p3_pan_used = [smm.n, smm.n]
                p3_brow_rd = [smm.n]

                def emit_proj(b, par):
                    wge(SY, sag, AG2_done[b])
                    wge(SY, sld, sld.n)
                    for jj2 in range(2):
                        sld.inc(SY.dma_start(
                            out=proj_scr.ap()[b, :, jj2],
                            in_=ag2_out[b][jj2].ap()[:, 0, :, colslab]), 16)
                    wge(SY, sld, sld.n)
                    wge(SY, smm, kv_used[par])
                    wge(SY, smm, po_used[0])
                    for h in range(H):
                        r, jj = h >> 1, h & 1
                        buf = kt_sb2[par] if h < 8 else v_sb2[par]
                        hh = h % 8
                        sld.inc(SY.dma_start(
                            out=buf[:, hh * 512:(hh + 1) * 512],
                            in_=proj_scr.ap()[b, r, jj]), 16)
                    po_ld = sld.n
                    for hp in range(8):
                        pbuf = wpan3[hp % 2]
                        psem = (sw, sw2)[hp % 2]
                        wge(GP, smm, p3_pan_used[hp % 2])
                        wge(GP, psem, psem.n)
                        psem.inc(GP.dma_start(
                            out=pbuf[:].rearrange("p (i n) -> p i n", i=DT),
                            in_=w_proj[hp]), 16)
                        PANLD = psem.n
                        wge(GP, sw3, sw3.n)
                        wge(GP, smm, p3_brow_rd[0])
                        sw3.inc(GP.dma_start(
                            out=brow_bf[0:1, 0:256],
                            in_=b_proj.unsqueeze(0)[
                                0:1, hp * 256:(hp + 1) * 256]), 16)
                        BROWLD = sw3.n
                        wge(TN, psem, PANLD)
                        wge(TN, sw3, BROWLD)
                        wge(TN, sld, po_ld)
                        for tl in range(TPB):
                            pe_acquire(psG)
                            for h in range(H):
                                buf = kt_sb2[par] if h < 8 else v_sb2[par]
                                hh = h % 8
                                TN.matmul(
                                    psG[:, 0:256],
                                    buf[:, hh * 512 + tl * 128:
                                        hh * 512 + (tl + 1) * 128],
                                    pbuf[:, h * 256:(h + 1) * 256],
                                    start=(h == 0), stop=False)
                            last = TN.matmul(psG[:, 0:256], ones_row_bf[:],
                                             brow_bf[0:1, 0:256],
                                             start=False, stop=True)
                            pe_done(psG, smm.inc(last))
                            rd_acquire(V, psG)
                            ad = V.tensor_tensor(
                                out=xr[b][tl][:, hp * 256:(hp + 1) * 256],
                                in0=psG[:, 0:256],
                                in1=xr[b][tl][:, hp * 256:(hp + 1) * 256],
                                op=ALU.add)
                            rd_done(psG, sdve, sdve.inc(ad))
                        p3_pan_used[hp % 2] = smm.n
                        p3_brow_rd[0] = smm.n
                    po_used[0] = smm.n
                    kv_used[par] = smm.n

                emit_pair_loads(0)
                for p in range(8):
                    b, j = p >> 1, p & 1
                    if p + 1 < 8:
                        emit_pair_loads(p + 1)
                    kt_sb, v_sb, qt_sb = kt_sb2[p % 2], v_sb2[p % 2], qt_sb2[p % 2]
                    wge(TN, sld, kv_ld[p % 2])
                    pend = None
                    for qc in range(8):
                        nk = 4 * qc + 4
                        ob = obank[qc % 2]
                        exp_cnt = {}
                        msk_cnt = {}
                        s_cnt = {}

                        def emit_S(j2):
                            m = j2 - 4 * qc
                            coloff = 0 if m < 0 else m * 128
                            sb = sbanks[j2 % 4]
                            pe_acquire(sb)
                            inst = TN.matmul(
                                sb[:, coloff:512],
                                kt_sb[:, j2 * 128:(j2 + 1) * 128],
                                qt_sb[:, qc * 512 + coloff:(qc + 1) * 512],
                                start=True, stop=True)
                            s_cnt[j2] = smm.inc(inst)
                            pe_done(sb, s_cnt[j2])

                        def emit_exp(j2):
                            m = j2 - 4 * qc
                            coloff = 0 if m < 0 else m * 128
                            sb = sbanks[j2 % 4]
                            wge(SC, smm, s_cnt[j2])
                            wge(SC, smm, pt_pv[j2 % 4])
                            ex = SC.activation(out=pt_sb[j2 % 4][:, coloff:512],
                                               in_=sb[:, coloff:512],
                                               func=AF.Exp, scale=SCALE, bias=SHIFT)
                            cnt = sact.inc(ex)
                            rd_done(sb, sact, cnt)
                            exp_cnt[j2] = cnt
                            if m >= 0:
                                wge(V, sact, cnt)
                                mk = V.tensor_tensor(
                                    out=pt_sb[j2 % 4][:, coloff:coloff + 128],
                                    in0=pt_sb[j2 % 4][:, coloff:coloff + 128],
                                    in1=triu_bf[:], op=ALU.mult)
                                msk_cnt[j2] = sdve.inc(mk)

                        def emit_PV(j2):
                            m = j2 - 4 * qc
                            coloff = 0 if m < 0 else m * 128
                            wge(TN, sact, exp_cnt[j2])
                            if m >= 0:
                                wge(TN, sdve, msk_cnt[j2])
                            if j2 == 0:
                                pe_acquire(ob)
                                pe_acquire(ps_row)
                            TN.matmul(ob[:, coloff:512],
                                      v_sb[:, j2 * 128:(j2 + 1) * 128],
                                      pt_sb[j2 % 4][:, coloff:512],
                                      start=(j2 == 0), stop=(j2 == nk - 1))
                            dnm = TN.matmul(ps_row[0:1, coloff:512],
                                            ones_col_bf[:],
                                            pt_sb[j2 % 4][:, coloff:512],
                                            start=(j2 == 0), stop=(j2 == nk - 1))
                            pt_pv[j2 % 4] = smm.inc(dnm)

                        def emit_epilogue_front(st8):
                            qc0, ob0 = st8
                            rd_acquire(V, ps_row)
                            if pend_bc[0] is not None:
                                wge(V, smm, pend_bc[0])
                            rc = V.reciprocal(out=attn_d[:], in_=ps_row[0:1, :])
                            rd_done(ps_row, sdve, sdve.inc(rc))
                            return sdve.n

                        def emit_epilogue_tail(st8, rc_cnt):
                            qc0, ob0 = st8
                            bb = sbanks[(4 * qc0 + 4 + 2) % 4]
                            pe_acquire(bb)
                            wge(TN, sdve, rc_cnt)
                            bc = TN.matmul(bb[:], ones_row_f32[:], attn_d[:],
                                           start=True, stop=True)
                            pend_bc[0] = smm.inc(bc)
                            pe_done(bb, pend_bc[0])
                            rd_acquire(SC, bb)
                            wge(SC, sdve, pend_ml[0])
                            cp = SC.copy(attn_db[:], bb[:])
                            rd_done(bb, sact, sact.inc(cp))
                            wge(V, sact, sact.n)
                            wge(V, sst, stage_o[qc0 % 2])
                            rd_acquire(V, ob0)
                            ml = V.tensor_tensor(out=stage_bf[qc0 % 2][:],
                                                 in0=ob0[:], in1=attn_db[:],
                                                 op=ALU.mult)
                            OEP = sdve.inc(ml)
                            pend_ml[0] = OEP
                            rd_done(ob0, sdve, OEP)
                            wge(SY, sdve, OEP)
                            wge(SY, sst, sst.n)
                            stage_o[qc0 % 2] = sst.inc(SY.dma_start(
                                out=ag2_in[b][j].ap()[0, :,
                                                      qc0 * 512:(qc0 + 1) * 512],
                                in_=stage_bf[qc0 % 2][:]), 16)

                        emit_S(0)
                        emit_exp(0)
                        if nk > 1:
                            emit_S(1)
                            emit_exp(1)
                        if pend is not None:
                            emit_epilogue_tail(*pend)
                            pend = None
                        for j2 in range(nk):
                            if j2 + 2 < nk:
                                emit_S(j2 + 2)
                                emit_exp(j2 + 2)
                            emit_PV(j2)
                        pe_done(ob, smm.n)
                        pe_done(ps_row, smm.n)
                        rcc = emit_epilogue_front((qc, ob))
                        pend = ((qc, ob), rcc)
                    emit_epilogue_tail(*pend)
                    pend = None
                    kv_used[p % 2] = smm.n
                    if j == 0 and b >= 1:
                        emit_proj(b - 1, 0)
                    wge(GP, sst, sst.n)
                    allgather(ag2_in[b][j].ap(), ag2_out[b][j].ap())
                    if j == 1:
                        AG2_done[b] = sag.n
                emit_proj(3, 1)

        def emit_ffn(tag, final=False):
            with ExitStack() as cf:
                EF = cf.enter_context
                phase_fence()
                xlnt = EF(nc.sbuf_tensor(f"xlntF{tag}", [128, DT * SLAB], BF16))
                at_sb = EF(nc.sbuf_tensor(f"at{tag}", [128, DT * 128], BF16))
                rt_sb = EF(nc.sbuf_tensor(f"rt{tag}", [128, DT * 8], BF16))
                hgt_all = EF(nc.sbuf_tensor(f"hgt{tag}", [128, 16 * 128], BF16))
                r8 = EF(nc.sbuf_tensor(f"r8{tag}", [128, 8], F32))
                rmax = EF(nc.sbuf_tensor(f"rmax{tag}", [128, 1], F32))
                roh = EF(nc.sbuf_tensor(f"roh{tag}", [128, 8], F32))
                hg_sb = EF(nc.sbuf_tensor(f"hg{tag}", [128, 128], BF16))
                bwf_pre = EF(nc.sbuf_tensor(f"bwfp{tag}", [128, D], BF16))
                cfl = EF(ExitStack())
                ln_alloc(cfl.enter_context, "f" + tag)

                wge(GP, smm, smm.n)
                wge(GP, sw, sw.n)
                sw.inc(GP.dma_start(
                    out=at_sb[:].rearrange("p (i n) -> p i n", i=DT),
                    in_=a_T), 16)
                wge(GP, sw, sw.n)
                sw.inc(GP.dma_start(
                    out=rt_sb[:].rearrange("p (i n) -> p i n", i=DT),
                    in_=routerT), 16)
                wge(GP, sw, sw.n)
                sw.inc(GP.dma_start(out=bwf_pre[:], in_=bw_flat), 16)
                for b in range(B):
                    emit_ln_transpose_slab(b, xlnt, store_scr=True)
                    wge(TN, sw, sw.n)
                    wge(TN, sact, sact.n)
                    for tl in range(TPB):
                        pe_acquire(psG)
                        last = None
                        for i in range(DT):
                            last = TN.matmul(
                                psG[:, 0:8],
                                xlnt[:, i * 512 + tl * 128:i * 512 + (tl + 1) * 128],
                                rt_sb[:, i * 8:(i + 1) * 8],
                                start=(i == 0), stop=(i == DT - 1))
                        pe_done(psG, smm.inc(last))
                        rd_acquire(V, psG)
                        cpr = V.tensor_copy(out=r8[:], in_=psG[:, 0:8])
                        rd_done(psG, sdve, sdve.inc(cpr))
                        wge(V, sdve, sdve.n)
                        sdve.inc(V.tensor_reduce(out=rmax[:], in_=r8[:],
                                                 axis=mybir.AxisListType.X,
                                                 op=ALU.max))
                        wge(V, sdve, sdve.n)
                        sdve.inc(V.tensor_scalar(out=roh[:], in0=r8[:],
                                                 scalar1=rmax[:], scalar2=2.0,
                                                 op0=ALU.is_equal, op1=ALU.mult))
                        pe_acquire(psA)
                        last = None
                        for i in range(DT):
                            last = TN.matmul(
                                psA[:, 0:128],
                                xlnt[:, i * 512 + tl * 128:i * 512 + (tl + 1) * 128],
                                at_sb[:, i * 128:(i + 1) * 128],
                                start=(i == 0), stop=(i == DT - 1))
                        pe_done(psA, smm.inc(last))
                        rd_acquire(V, psA)
                        wge(V, sdve, sdve.n)
                        gt = V.scalar_tensor_tensor(
                            out=hg_sb[:].rearrange("p (a c2) -> p a c2", a=8),
                            in0=psA[:, 0:128].rearrange("p (a c2) -> p a c2", a=8),
                            scalar=1.0,
                            in1=roh[:].unsqueeze(2).broadcast_to([128, 8, 16]),
                            op0=ALU.mult, op1=ALU.mult)
                        gcnt = sdve.inc(gt)
                        rd_done(psA, sdve, gcnt)
                        wge(TN, sdve, gcnt)
                        pe_acquire(psB)
                        tr = TN.matmul(psB[:, 0:128], hg_sb[:], ident_bf[:],
                                       start=True, stop=True)
                        pe_done(psB, smm.inc(tr))
                        rd_acquire(SC, psB)
                        ev = SC.copy(hgt_all[:, (b * TPB + tl) * 128:
                                             (b * TPB + tl + 1) * 128],
                                     psB[:, 0:128])
                        rd_done(psB, sact, sact.inc(ev))
                cfl.close()
                with ExitStack() as cf2:
                    EF2 = cf2.enter_context
                    phase_fence()
                    whs = [EF2(nc.sbuf_tensor(f"wh{tag}{i}", [128, DT * 256],
                                              BF16)) for i in range(2)]
                    xl2 = [xlnt,
                           EF2(nc.sbuf_tensor(f"xlnt2{tag}",
                                              [128, DT * SLAB], BF16))]
                    bwf_sb = bwf_pre
                    pan_used = [smm.n, smm.n]
                    xl_used = [smm.n, smm.n]
                    xl_ld = [0, 0]
                    brow_rd = smm.n
                    for k in range(2):
                        wge(SY, sld, sld.n)
                        wge(SY, sst, sst.n)
                        sld.inc(SY.dma_start(out=xl2[k][:], in_=xlnt_scr[k]), 16)
                        xl_ld[k] = sld.n
                    seq = [(hp2, b2) for hp2 in range(8) for b2 in range(B)]
                    pbuf = None
                    for si, (hp, b) in enumerate(seq):
                        k = b % 2
                        if b == 0:
                            pbuf = whs[hp % 2]
                            psem = (sw, sw2)[hp % 2]
                            wge(GP, smm, pan_used[hp % 2])
                            wge(GP, psem, psem.n)
                            psem.inc(GP.dma_start(
                                out=pbuf[:].rearrange("p (i n) -> p i n", i=DT),
                                in_=w_baseT[hp]), 16)
                            PANLD = psem.n
                            wge(GP, sw3, sw3.n)
                            wge(GP, smm, brow_rd)
                            sw3.inc(GP.dma_start(
                                out=brow_bf[0:1, 0:256],
                                in_=b_base.unsqueeze(0)[
                                    0:1, hp * 256:(hp + 1) * 256]), 16)
                            BROWLD = sw3.n
                            wge(TN, psem, PANLD)
                            wge(TN, sw3, BROWLD)
                        wge(TN, sld, xl_ld[k])
                        for tl in range(TPB):
                            bank = (psC, psD, psE, psF)[tl % 4]
                            pe_acquire(bank)
                            for i in range(DT):
                                TN.matmul(
                                    bank[:, 0:256],
                                    xl2[k][:, i * 512 + tl * 128:
                                           i * 512 + (tl + 1) * 128],
                                    pbuf[:, i * 256:(i + 1) * 256],
                                    start=(i == 0), stop=False)
                            TN.matmul(bank[:, 0:256],
                                      hgt_all[:, (b * TPB + tl) * 128:
                                              (b * TPB + tl + 1) * 128],
                                      bwf_sb[:, hp * 256:(hp + 1) * 256],
                                      start=False, stop=False)
                            last = TN.matmul(bank[:, 0:256], ones_row_bf[:],
                                             brow_bf[0:1, 0:256],
                                             start=False, stop=True)
                            pe_done(bank, smm.inc(last))
                            rd_acquire(V, bank)
                            ad = V.tensor_tensor(
                                out=xr[b][tl][:, hp * 256:(hp + 1) * 256],
                                in0=bank[:, 0:256],
                                in1=xr[b][tl][:, hp * 256:(hp + 1) * 256],
                                op=ALU.add)
                            rd_done(bank, sdve, sdve.inc(ad))
                            if final and hp == 7:
                                wge(SY, sdve, sdve.n)
                                sout.inc(SY.dma_start(
                                    out=y_out[b, tl * 128:(tl + 1) * 128, :],
                                    in_=xr[b][tl][:]), 16)
                        xl_used[k] = smm.n
                        if b == B - 1:
                            pan_used[hp % 2] = smm.n
                            brow_rd = smm.n
                        if si + 2 < len(seq):
                            nb = seq[si + 2][1]
                            wge(SY, smm, xl_used[k])
                            wge(SY, sld, sld.n)
                            sld.inc(SY.dma_start(out=xl2[nb % 2][:],
                                                 in_=xlnt_scr[nb]), 16)
                            xl_ld[nb % 2] = sld.n

        if n_sl >= 2:
            emit_ffn("a")

        if n_sl >= 3:
            vh_expr = pid % 2
            bch_expr = pid // 2
            with ExitStack() as c5:
                E5 = c5.enter_context
                phase_fence()
                xlnt = E5(nc.sbuf_tensor("xlnt5", [128, DT * SLAB], BF16))
                ln16_s = E5(nc.sbuf_tensor("ln16s", [128, 16], F32))
                ln16_q = E5(nc.sbuf_tensor("ln16q", [128, 16], F32))
                ln16_m = E5(nc.sbuf_tensor("ln16m", [128, 16], F32))
                ln16_v = E5(nc.sbuf_tensor("ln16v", [128, 16], F32))
                c5i = E5(ExitStack())
                E5i = c5i.enter_context
                ln_alloc(E5i, "p5")
                wq_sb = E5i(nc.sbuf_tensor("wq_sb", [128, DT * 128], BF16))
                wk_sb = E5i(nc.sbuf_tensor("wk_sb", [128, DT * 128], BF16))
                gq_s = E5i(nc.sbuf_tensor("gq_s", [128, 512], F32))
                gk_s = E5i(nc.sbuf_tensor("gk_s", [128, 512], F32))
                gsp = E5i(nc.sbuf_tensor("gsp", [128, 512], F32))
                gb_cum = E5i(nc.sbuf_tensor("gb_cum", [128, 512], F32))
                gexp = E5i(nc.sbuf_tensor("gexp", [128, 512], F32))
                gnb = E5i(nc.sbuf_tensor("gnb", [128, GCS], F32))
                glam = E5i(nc.sbuf_tensor("glam", [128, GCS], F32))
                gabk = E5i(nc.sbuf_tensor("gabk", [128, 512], BF16))
                gkh = E5i(nc.sbuf_tensor("gkh", [128, 512], BF16))
                kh_tm = E5i(nc.sbuf_tensor("kh_tm", [32, GCS * 128], BF16))
                zeros32 = E5i(nc.sbuf_tensor("zeros32", [128, 32], F32))
                sdve.inc(V.memset(zeros32[:], 0.0))

                wge(GP, smm, smm.n)
                wge(GP, sw, sw.n)
                sw.inc(GP.dma_start(
                    out=wq_sb[:].rearrange("p (i n) -> p i n", i=DT),
                    in_=wqT), 16)
                wge(GP, sw, sw.n)
                sw.inc(GP.dma_start(
                    out=wk_sb[:].rearrange("p (i n) -> p i n", i=DT),
                    in_=wkT), 16)

                xln_buf5 = LN_BUFS["xln"]
                wge(SC, sdve, sdve.n)
                wge(V, sst, sst.n)
                wge(V, smm, smm.n)
                for b in range(B):
                    for t in range(TPB):
                        cix = b * TPB + t
                        sdve.inc(V.tensor_scalar(
                            out=xln_buf5[t][:], in0=xr[b][t][:],
                            scalar1=0.0, scalar2=0.0,
                            op0=ALU.add, op1=ALU.add,
                            accum_out=ln16_s[:, cix:cix + 1]))
                        if cix < 15:
                            wge(V, sdve, sdve.n)
                for b in range(B):
                    for t in range(TPB):
                        cix = b * TPB + t
                        sact.inc(SC.activation(
                            out=LN_BUFS["sqs"][:], in_=xr[b][t][:],
                            func=AF.Square,
                            accum_out=ln16_q[:, cix:cix + 1]))
                        if cix < 15:
                            wge(SC, sact, sact.n)
                wge(V, sdve, sdve.n)
                wge(V, sact, sact.n)
                sdve.inc(V.tensor_scalar(out=ln16_m[:], in0=ln16_s[:],
                                         scalar1=1.0 / D, scalar2=None,
                                         op0=ALU.mult))
                wge(V, sdve, sdve.n)
                sdve.inc(V.scalar_tensor_tensor(out=ln16_v[:], in0=ln16_m[:],
                                                scalar=-1.0, in1=ln16_m[:],
                                                op0=ALU.mult, op1=ALU.mult))
                wge(V, sdve, sdve.n)
                sdve.inc(V.scalar_tensor_tensor(out=ln16_v[:], in0=ln16_q[:],
                                                scalar=1.0 / D, in1=ln16_v[:],
                                                op0=ALU.mult, op1=ALU.add))
                wge(SC, sdve, sdve.n)
                sact.inc(SC.activation(out=ln16_v[:], in_=ln16_v[:],
                                       func=AF.Ln, bias=EPS))
                wge(SC, sact, sact.n)
                sact.inc(SC.activation(out=ln16_v[:], in_=ln16_v[:],
                                       func=AF.Exp, scale=-0.5))
                wge(V, sact, sact.n)
                vst5 = [0, 0, 0, 0]
                for b in range(B):
                    for t in range(TPB):
                        cix = b * TPB + t
                        wge(V, sst, vst5[t])
                        AP7 = sdve.inc(V.tensor_scalar(
                            out=xln_buf5[t][:], in0=xr[b][t][:],
                            scalar1=ln16_m[:, cix:cix + 1],
                            scalar2=ln16_v[:, cix:cix + 1],
                            op0=ALU.subtract, op1=ALU.mult))
                        wge(SY, sdve, AP7)
                        wge(SY, sst, sst.n)
                        vst5[t] = sst.inc(SY.dma_start(
                            out=ag3_v.ap()[b, t * 128:(t + 1) * 128, :],
                            in_=xln_buf5[t][:]), 16)
                wge(GP, sst, sst.n)
                allgather(ag3_v.ap(), ag3_v_o.ap())
                AG3V_done = sag.n
                TRANS_done = 0
                for b in range(B):
                    wge(GP, smm, TRANS_done)
                    wge(GP, sw, sw.n)
                    for t in range(TPB):
                        sw.inc(GP.dma_start(
                            out=xln_buf5[t][:],
                            in_=ag3_v.ap()[b, t * 128:(t + 1) * 128, :]), 16)
                    XLN_LD = sw.n
                    ST_prev = sst.n
                    for t in range(TPB):
                        transpose_tile_into(xlnt, t, t, XLN_LD,
                                            ready_ctr=sw, sst_guard=ST_prev)
                    TRANS_done = smm.n
                    wge(SY, sact, sact.n)
                    wge(SY, sst, sst.n)
                    sst.inc(SY.dma_start(out=xlnt_scr[b], in_=xlnt[:]), 16)
                    wge(TN, sw, sw.n)
                    wge(TN, sact, sact.n)
                    last = None
                    for i in range(DT):
                        last = TN.matmul(psC[:],
                                         wq_sb[:, i * 128:(i + 1) * 128],
                                         xlnt[:, i * 512:(i + 1) * 512],
                                         start=(i == 0), stop=(i == DT - 1))
                    smm.inc(last)
                    last = None
                    for i in range(DT):
                        last = TN.matmul(psD[:],
                                         wk_sb[:, i * 128:(i + 1) * 128],
                                         xlnt[:, i * 512:(i + 1) * 512],
                                         start=(i == 0), stop=(i == DT - 1))
                    smm.inc(last)
                    wge(SC, smm, smm.n)
                    sact.inc(SC.activation(out=gq_s[:], in_=psC[:], func=AF.Silu))
                    sact.inc(SC.activation(out=gk_s[:], in_=psD[:], func=AF.Sigmoid))
                    wge(SC, sact, sact.n)
                    sact.inc(SC.activation(out=gsp[:], in_=gk_s[:], func=AF.Exp))
                    wge(SC, sact, sact.n)
                    sact.inc(SC.activation(out=gsp[:], in_=gsp[:], func=AF.Ln,
                                           bias=1.0))
                    wge(V, sact, sact.n)
                    last = None
                    for cc in range(GCS):
                        last = V.tensor_tensor_scan(
                            out=gb_cum[:, cc * GC:(cc + 1) * GC],
                            data0=gsp[:, cc * GC:(cc + 1) * GC], data1=zeros32[:],
                            initial=0.0, op0=ALU.add, op1=ALU.add)
                    CUM = sdve.inc(last)
                    wge(SC, sdve, CUM)
                    sact.inc(SC.activation(out=glam[:], in_=gb_cum[:, GC - 1::GC],
                                           func=AF.Exp, scale=-1.0))
                    wge(V, sdve, CUM)
                    sdve.inc(V.tensor_scalar(out=gnb[:], in0=gb_cum[:, GC - 1::GC],
                                             scalar1=-1.0, scalar2=None,
                                             op0=ALU.mult))
                    wge(SY, sact, sact.n)
                    wge(SY, sst, sst.n)
                    sst.inc(SY.dma_start(out=ag3_lam.ap()[b], in_=glam[:]), 16)
                    sact.inc(SC.activation(out=gexp[:], in_=gb_cum[:], func=AF.Exp,
                                           scale=-1.0, bias=LNSC))
                    wge(V, sact, sact.n)
                    wge(V, sst, sst.n)
                    AW = sdve.inc(V.tensor_tensor(out=gabk[:], in0=gq_s[:],
                                                  in1=gexp[:], op=ALU.mult))
                    wge(SY, sdve, AW)
                    wge(SY, sst, sst.n)
                    sst.inc(SY.dma_start(
                        out=ag3_abk.ap()[b, 0].rearrange("c (p g) -> p c g", p=128),
                        in_=gabk[:].rearrange("p (c g) -> p c g", c=GCS)), 16)
                    wge(SC, sst, sst.n)
                    wge(SC, sdve, sdve.n)
                    sact.inc(SC.activation(out=gexp[:], in_=gb_cum[:], func=AF.Exp))
                    wge(V, sact, sact.n)
                    wge(V, sst, sst.n)
                    BW = sdve.inc(V.tensor_tensor(out=gabk[:], in0=gk_s[:],
                                                  in1=gexp[:], op=ALU.mult))
                    wge(SY, sdve, BW)
                    wge(SY, sst, sst.n)
                    sst.inc(SY.dma_start(
                        out=ag3_abk.ap()[b, 1].rearrange("c (p g) -> p c g", p=128),
                        in_=gabk[:].rearrange("p (c g) -> p c g", c=GCS)), 16)
                    wge(SC, sdve, sdve.n)
                    last = None
                    for cc in range(GCS):
                        last = SC.activation(out=gexp[:, cc * GC:(cc + 1) * GC],
                                             in_=gb_cum[:, cc * GC:(cc + 1) * GC],
                                             func=AF.Exp, bias=gnb[:, cc:cc + 1])
                    sact.inc(last)
                    wge(V, sact, sact.n)
                    KH = sdve.inc(V.tensor_tensor(out=gkh[:], in0=gk_s[:],
                                                  in1=gexp[:], op=ALU.mult))
                    wge(TN, sdve, KH)
                    for cc in range(GCS):
                        bank = (psA, psB)[cc % 2]
                        pe_acquire(bank)
                        tr = TN.matmul(bank[0:32, 0:128],
                                       gkh[:, cc * GC:(cc + 1) * GC],
                                       ident_bf[:], start=True, stop=True)
                        pe_done(bank, smm.inc(tr))
                        rd_acquire(SC, bank)
                        ev = SC.copy(kh_tm[:, cc * 128:(cc + 1) * 128],
                                     bank[0:32, 0:128])
                        rd_done(bank, sact, sact.inc(ev))
                    wge(SY, sact, sact.n)
                    wge(SY, sst, sst.n)
                    sst.inc(SY.dma_start(
                        out=ag3_abk.ap()[b, 2].rearrange("c (t v) -> t c v", t=GC),
                        in_=kh_tm[:].rearrange("t (c v) -> t c v", c=GCS)), 16)

                wge(GP, sst, sst.n)
                allgather(ag3_abk.ap(), ag3_abk_o.ap())
                allgather(ag3_lam.ap(), ag3_lam_o.ap())
                AG3_done = sag.n
                c5i.close()

            with ExitStack() as c6:
                E6 = c6.enter_context
                phase_fence()
                wge(GP, sw, sw.n)
                sw.inc(GP.dma_start(out=wog_bf.ap(), in_=wogT), 16)
                wge(GP, sw, sw.n)
                sw.inc(GP.dma_start(out=wig_bf.ap(), in_=wigT), 16)
                ch_ab = [E6(nc.sbuf_tensor(f"ch_ab{i}", [128, 2 * GCS * GC], BF16))
                         for i in range(2)]
                ch_kh = [E6(nc.sbuf_tensor(f"ch_kh{i}", [32, GCS * 128], BF16))
                         for i in range(2)]
                ch_v = [E6(nc.sbuf_tensor(f"ch_v{i}", [32, 4 * VH], BF16))
                        for i in range(2)]
                ch_lam = [E6(nc.sbuf_tensor(f"ch_lam{i}", [128, GCS], F32))
                          for i in range(2)]
                S_f32 = E6(nc.sbuf_tensor("S_f32", [128, VH], F32))
                S_bf = E6(nc.sbuf_tensor("S_bf", [128, VH], BF16))
                ptg_sb = [E6(nc.sbuf_tensor(f"ptg{i}", [32, 32], BF16))
                          for i in range(2)]
                oT_sb = E6(nc.sbuf_tensor("oT_sb", [128, 8 * 128], BF16))
                o2_st = E6(nc.sbuf_tensor("o2_st", [1, 128], F32))
                osq = E6(nc.sbuf_tensor("osq6", [128, 1024], BF16))

                wge(V, sag, AG3_done)
                sdve.inc(V.memset(S_f32[:], 0.0))
                wge(SC, sdve, sdve.n)
                sbf_cnt = sact.inc(SC.copy(S_bf[:], S_f32[:]))
                wge(SY, sag, AG3V_done)
                sld.inc(SY.dma_start(
                    out=v_scr.ap().unsqueeze(0),
                    in_=ag3_v_o.ap().rearrange(
                        "r b t d -> b r t d")[bch_slice, :, :, vh_cols]), 16)
                wge(SY, sld, sld.n)
                wge(SY, sag, AG3_done)
                sld.inc(SY.dma_start(
                    out=abk_scr.ap().unsqueeze(0),
                    in_=ag3_abk_o.ap().rearrange(
                        "r b k c m -> b r k c m")[bch_slice]), 16)
                wge(SY, sld, sld.n)
                sld.inc(SY.dma_start(
                    out=lam_scr.ap().unsqueeze(0),
                    in_=ag3_lam_o.ap().rearrange(
                        "r b p c -> b r p c")[bch_slice]), 16)
                wge(SY, sld, sld.n)

                ub = [(psE, psF), (psC, psD)]
                sec_used = [0, 0]
                v_used = [0, 0]
                ld_sec = [0, 0]
                ld_v = [0, 0]
                u_cnt = {}
                p_cnt = {}
                mk_cnt = {}
                OMM = smm.n
                SUPD = sdve.n

                def emit_loads(g):
                    r, cc = g >> 4, g & 15
                    blkk = g // 4
                    if cc == 0:
                        par = r % 2
                        wge(SY, smm, sec_used[par])
                        wge(SY, sld, sld.n)
                        sld.inc(SY.dma_start(
                            out=ch_ab[par][:, 0:GCS * GC].rearrange(
                                "p (c g2) -> p c g2", c=GCS),
                            in_=abk_scr.ap()[r, 0].rearrange(
                                "c (p g2) -> p c g2", p=128)), 16)
                        sld.inc(SY.dma_start(
                            out=ch_ab[par][:, GCS * GC:2 * GCS * GC].rearrange(
                                "p (c g2) -> p c g2", c=GCS),
                            in_=abk_scr.ap()[r, 1].rearrange(
                                "c (p g2) -> p c g2", p=128)), 16)
                        sld.inc(SY.dma_start(
                            out=ch_kh[par][:].rearrange("t (c v2) -> t c v2", c=GCS),
                            in_=abk_scr.ap()[r, 2].rearrange(
                                "c (t v2) -> t c v2", t=GC)), 16)
                        sld.inc(SY.dma_start(
                            out=ch_lam[par][:],
                            in_=lam_scr.ap()[r]), 16)
                        ld_sec[par] = sld.n
                    if g % 4 == 0:
                        r2, tloc = blkk >> 2, blkk & 3
                        bpar = blkk % 2
                        wge(SY, smm, v_used[bpar])
                        wge(SY, sld, sld.n)
                        sld.inc(SY.dma_start(
                            out=ch_v[bpar][:].rearrange("t (c v2) -> t c v2", c=4),
                            in_=v_scr.ap()[r2].rearrange(
                                "(b2 c t) v2 -> b2 t c v2", c=4, t=GC)[
                                tloc]), 16)
                        ld_v[bpar] = sld.n

                def emit_UP(g):
                    r, cc = g >> 4, g & 15
                    par = r % 2
                    bpar = (g // 4) % 2
                    kh_sl = ch_kh[par][:, cc * 128:(cc + 1) * 128]
                    v_sl = ch_v[bpar][:, (g % 4) * VH:(g % 4 + 1) * VH]
                    a_sl = ch_ab[par][:, cc * GC:(cc + 1) * GC]
                    b_sl = ch_ab[par][:, GCS * GC + cc * GC:
                                      GCS * GC + (cc + 1) * GC]
                    wge(TN, sld, max(ld_sec[par], ld_v[bpar]))
                    ua, ubk = ub[g % 2]
                    pe_acquire(ua, ubk)
                    TN.matmul(ua[:], kh_sl, v_sl[:, 0:512], start=True, stop=True)
                    u2 = TN.matmul(ubk[:], kh_sl, v_sl[:, 512:VH],
                                   start=True, stop=True)
                    u_cnt[g] = smm.inc(u2)
                    pe_done(ua, u_cnt[g])
                    pe_done(ubk, u_cnt[g])
                    pban = (psA, psB)[g % 2]
                    pe_acquire(pban)
                    pt = TN.matmul(pban[0:32, 0:32], b_sl, a_sl,
                                   start=True, stop=True)
                    p_cnt[g] = smm.inc(pt)
                    pe_done(pban, p_cnt[g])
                    rd_acquire(V, pban)
                    wge(V, smm, OMM)
                    mk = V.tensor_tensor(out=ptg_sb[g % 2][:],
                                         in0=pban[0:32, 0:32],
                                         in1=triu_bf[0:32, 0:32], op=ALU.mult)
                    mk_cnt[g] = sdve.inc(mk)
                    rd_done(pban, sdve, mk_cnt[g])

                for g in range(NGC):
                    if g == 0:
                        emit_loads(0)
                        emit_UP(0)
                    if g + 1 < NGC:
                        emit_loads(g + 1)
                        emit_UP(g + 1)
                    r, cc = g >> 4, g & 15
                    blkk = g // 4
                    par = r % 2
                    bpar = blkk % 2
                    a_sl = ch_ab[par][:, cc * GC:(cc + 1) * GC]
                    v_sl = ch_v[bpar][:, (g % 4) * VH:(g % 4 + 1) * VH]
                    wge(TN, sdve, mk_cnt[g])
                    wge(TN, sact, sbf_cnt)
                    pe_acquire(psG)
                    last = None
                    for vt in range(8):
                        TN.matmul(psG[:, vt * 32:(vt + 1) * 32],
                                  v_sl[:, vt * 128:(vt + 1) * 128],
                                  ptg_sb[g % 2][:], start=True, stop=False)
                        last = TN.matmul(psG[:, vt * 32:(vt + 1) * 32],
                                         S_bf[:, vt * 128:(vt + 1) * 128],
                                         a_sl, start=False, stop=True)
                    OMM = smm.inc(last)
                    pe_done(psG, OMM)
                    sec_used[par] = OMM
                    v_used[bpar] = OMM
                    ua, ubk = ub[g % 2]
                    rd_acquire(V, ua, ubk)
                    wge(V, sdve, SUPD)
                    wge(V, sact, sbf_cnt)
                    V.scalar_tensor_tensor(out=S_f32[:, 0:512],
                                           in0=S_f32[:, 0:512],
                                           scalar=ch_lam[par][:, cc:cc + 1],
                                           in1=ua[:],
                                           op0=ALU.mult, op1=ALU.add)
                    s2 = V.scalar_tensor_tensor(out=S_f32[:, 512:VH],
                                                in0=S_f32[:, 512:VH],
                                                scalar=ch_lam[par][:, cc:cc + 1],
                                                in1=ubk[:],
                                                op0=ALU.mult, op1=ALU.add)
                    SUPD = sdve.inc(s2)
                    rd_done(ua, sdve, SUPD)
                    rd_done(ubk, sdve, SUPD)
                    wge(SC, sdve, SUPD)
                    wge(SC, smm, OMM)
                    cpS = SC.copy(S_bf[:], S_f32[:])
                    sbf_cnt = sact.inc(cpS)
                    rd_acquire(SC, psG)
                    wge(SC, sst, sst.n)
                    ev = SC.copy(
                        oT_sb[:].rearrange("p (v2 t) -> p v2 t", v2=8)[
                            :, :, (g % 4) * 32:(g % 4) * 32 + 32],
                        psG[:].rearrange("p (v2 t) -> p v2 t", v2=8))
                    rd_done(psG, sact, sact.inc(ev))
                    if g % 4 == 3:
                        tb = blkk
                        wge(SC, sact, sact.n)
                        sq = SC.activation(out=osq[:], in_=oT_sb[:],
                                           func=AF.Square)
                        sact.inc(sq)
                        wge(TN, sact, sact.n)
                        pe_acquire(ps_row)
                        last = None
                        for vt in range(8):
                            last = TN.matmul(ps_row[0:1, 0:128], ones_col_bf[:],
                                             osq[:, vt * 128:(vt + 1) * 128],
                                             start=(vt == 0), stop=(vt == 7))
                        pe_done(ps_row, smm.inc(last))
                        rd_acquire(V, ps_row)
                        wge(V, sst, sst.n)
                        cpo = V.tensor_copy(out=o2_st[:], in_=ps_row[0:1, 0:128])
                        rd_done(ps_row, sdve, sdve.inc(cpo))
                        wge(SY, sdve, sdve.n)
                        wge(SY, sst, sst.n)
                        sst.inc(SY.dma_start(
                            out=ag4_o2.ap()[:, tb * 128:(tb + 1) * 128],
                            in_=o2_st[:]), 16)
                        wge(SY, sact, sact.n)
                        wge(SY, sst, sst.n)
                        sst.inc(SY.dma_start(
                            out=ag4_ot.ap()[:, :, tb * 128:(tb + 1) * 128]
                            .rearrange("v p t -> p v t"),
                            in_=oT_sb[:].rearrange("p (v2 t) -> p v2 t", v2=8)), 16)
                wge(GP, sst, sst.n)
                allgather(ag4_ot.ap(), ag4_ot_o.ap())
                allgather(ag4_o2.ap(), ag4_o2_o.ap())
                AG4_done = sag.n

            with ExitStack() as c7a:
                E7a = c7a.enter_context
                phase_fence()
                wogs = [E7a(nc.sbuf_tensor(f"wogs{i}", [128, DT * 256], BF16))
                        for i in range(2)]
                wigs = [E7a(nc.sbuf_tensor(f"wigs{i}", [128, DT * 256], BF16))
                        for i in range(2)]
                xlnt7 = E7a(nc.sbuf_tensor("xlnt7", [128, DT * SLAB], BF16))
                otp_all = [E7a(nc.sbuf_tensor(f"otp7_{t}", [128, D], BF16))
                           for t in range(TPB)]
                gw_bc = E7a(nc.sbuf_tensor("gw_bc", [128, D], BF16))
                ep_sil = [E7a(nc.sbuf_tensor(f"epsil{i}", [128, 256], BF16))
                          for i in range(2)]
                ep_t1c = E7a(nc.sbuf_tensor("ept1c", [128, 256], F32))
                rstd_c = E7a(nc.sbuf_tensor("rstdc", [128, 16], F32))
                ep_o2 = E7a(nc.sbuf_tensor("epo2", [128, 2], F32))

                for q4 in range(4):
                    wge(SY, smm, smm.n)
                    wge(SY, sld, sld.n)
                    sld.inc(SY.dma_start(
                        out=brow_buf[:],
                        in_=g_w.unsqueeze(0)[0:1, q4 * 512:(q4 + 1) * 512]), 16)
                    wge(TN, sld, sld.n)
                    bank = (psC, psD, psE, psF)[q4]
                    pe_acquire(bank)
                    gwm = TN.matmul(bank[:], ones_row_f32[:], brow_buf[:],
                                    start=True, stop=True)
                    pe_done(bank, smm.inc(gwm))
                    rd_acquire(SC, bank)
                    cpg = SC.copy(gw_bc[:, q4 * 512:(q4 + 1) * 512], bank[:])
                    rd_done(bank, sact, sact.inc(cpg))

                wge(SY, sag, AG4_done)
                wge(SY, sld, sld.n)
                sld.inc(SY.dma_start(
                    out=ep_ot_scr.ap(),
                    in_=ag4_ot_o.ap()[:, :, :, colslab]), 16)
                wge(SY, sld, sld.n)
                sld.inc(SY.dma_start(
                    out=ep_o2_scr.ap(),
                    in_=ag4_o2_o.ap()[:, :, colslab]), 16)
                wge(SY, sld, sld.n)
                o2_rows = ep_o2_scr.ap().rearrange("r a t -> (r a t)").unsqueeze(1)

                otp_used = smm.n
                xl_used = smm.n
                o2_used = sdve.n
                pan_used = [smm.n, smm.n]
                sil_used = [sdve.n, sdve.n]
                for b in range(B):
                    wge(SY, smm, xl_used)
                    wge(SY, sld, sld.n)
                    sld.inc(SY.dma_start(out=xlnt7[:], in_=xlnt_scr[b]), 16)
                    wge(SY, smm, otp_used)
                    for tl in range(TPB):
                        for vh2 in range(2):
                            rr = 2 * b + vh2
                            sld.inc(SY.dma_start(
                                out=otp_all[tl][:, vh2 * 1024:(vh2 + 1) * 1024]
                                .rearrange("p (v2 t) -> p v2 t", v2=8),
                                in_=ep_ot_scr.ap()[rr, :, :,
                                                   tl * 128:(tl + 1) * 128]
                                .rearrange("v p t -> p v t")), 16)
                    BLD = sld.n
                    for hp in range(8):
                        pbw = wogs[hp % 2]
                        pbi = wigs[hp % 2]
                        psem = (sw, sw2)[hp % 2]
                        wge(GP, smm, pan_used[hp % 2])
                        wge(GP, psem, psem.n)
                        psem.inc(GP.dma_start(
                            out=pbw[:].rearrange("p (i n) -> p i n", i=DT),
                            in_=wog_bf.ap()[hp]), 16)
                        psem.inc(GP.dma_start(
                            out=pbi[:].rearrange("p (i n) -> p i n", i=DT),
                            in_=wig_bf.ap()[hp]), 16)
                        PANLD = psem.n
                        wge(TN, psem, PANLD)
                        wge(TN, sld, BLD)
                        for tl in range(TPB):
                            idx = b * TPB + tl
                            par = (hp * TPB + tl) % 2
                            if hp == 0:
                                wge(SY, sdve, o2_used)
                                wge(SY, sld, sld.n)
                                for vh2 in range(2):
                                    rr = 2 * b + vh2
                                    sld.inc(SY.dma_start(
                                        out=ep_o2[:, vh2:vh2 + 1],
                                        in_=o2_rows[rr * 512 + tl * 128:
                                                    rr * 512 + (tl + 1) * 128,
                                                    :]), 16)
                                O2LD = sld.n
                                wge(V, sld, O2LD)
                                rd = V.tensor_reduce(
                                    out=rstd_c[:, idx:idx + 1], in_=ep_o2[:],
                                    axis=mybir.AxisListType.X, op=ALU.add)
                                o2_used = sdve.inc(rd)
                                wge(V, sdve, sdve.n)
                                sdve.inc(V.tensor_scalar(
                                    out=rstd_c[:, idx:idx + 1],
                                    in0=rstd_c[:, idx:idx + 1],
                                    scalar1=1.0 / D, scalar2=None, op0=ALU.mult))
                                wge(SC, sdve, sdve.n)
                                sact.inc(SC.activation(
                                    out=rstd_c[:, idx:idx + 1],
                                    in_=rstd_c[:, idx:idx + 1],
                                    func=AF.Ln, bias=EPS))
                                wge(SC, sact, sact.n)
                                sact.inc(SC.activation(
                                    out=rstd_c[:, idx:idx + 1],
                                    in_=rstd_c[:, idx:idx + 1],
                                    func=AF.Exp, scale=-0.5))
                            bank = (psC, psD)[tl % 2]
                            pe_acquire(bank)
                            last = None
                            for i in range(DT):
                                last = TN.matmul(
                                    bank[:, 0:256],
                                    xlnt7[:, i * 512 + tl * 128:
                                          i * 512 + (tl + 1) * 128],
                                    pbi[:, i * 256:(i + 1) * 256],
                                    start=(i == 0), stop=False)
                            for i in range(DT):
                                last = TN.matmul(
                                    bank[:, 0:256],
                                    otp_all[tl][:, i * 128:(i + 1) * 128],
                                    pbw[:, i * 256:(i + 1) * 256],
                                    start=False, stop=(i == DT - 1))
                            pe_done(bank, smm.inc(last))
                            rd_acquire(SC, bank)
                            wge(SC, sdve, sil_used[par])
                            sl = SC.activation(out=ep_sil[par][:], in_=bank[:, 0:256],
                                               func=AF.Silu)
                            rd_done(bank, sact, sact.inc(sl))
                            tb_bank = (psA, psB)[par]
                            pe_acquire(tb_bank)
                            tlast = None
                            for u in range(2):
                                i = hp * 2 + u
                                tlast = TN.matmul(
                                    tb_bank[:, u * 128:(u + 1) * 128],
                                    otp_all[tl][:, i * 128:(i + 1) * 128],
                                    ident_bf[:], start=True, stop=True)
                            pe_done(tb_bank, smm.inc(tlast))
                            rd_acquire(V, tb_bank)
                            wge(V, sact, sact.n)
                            wge(V, sdve, sdve.n)
                            t1 = V.scalar_tensor_tensor(
                                out=ep_t1c[:], in0=tb_bank[:, 0:256],
                                scalar=rstd_c[:, idx:idx + 1],
                                in1=gw_bc[:, hp * 256:(hp + 1) * 256],
                                op0=ALU.mult, op1=ALU.mult)
                            rd_done(tb_bank, sdve, sdve.inc(t1))
                            wge(V, sdve, sdve.n)
                            t2 = V.tensor_tensor(out=ep_t1c[:], in0=ep_t1c[:],
                                                 in1=ep_sil[par][:], op=ALU.mult)
                            sdve.inc(t2)
                            wge(V, sdve, sdve.n)
                            ad = V.tensor_tensor(
                                out=xr[b][tl][:, hp * 256:(hp + 1) * 256],
                                in0=ep_t1c[:],
                                in1=xr[b][tl][:, hp * 256:(hp + 1) * 256],
                                op=ALU.add)
                            sil_used[par] = sdve.inc(ad)
                        pan_used[hp % 2] = smm.n
                    otp_used = smm.n
                    xl_used = smm.n

        if n_sl >= 4:
            emit_ffn("b", final=True)
        else:
            wge(SY, sdve, sdve.n)
            wge(SY, sact, sact.n)
            wge(SY, smm, smm.n)
            for b in range(B):
                for t in range(TPB):
                    sout.inc(SY.dma_start(
                        out=y_out[b, t * 128:(t + 1) * 128, :],
                        in_=xr[b][t][:]), 16)

        wge(SY, sout, sout.n)

    return nc


def make_in_maps(inputs):
    x = np.asarray(inputs["L0_attn_a_x"])
    tri = np.triu(np.ones((128, 128), np.float32))
    ident = np.eye(128, dtype=np.float32)
    base = {
        "w_qkv": np.ascontiguousarray(
            np.asarray(inputs["L0_attn_a_W_qkv"]).reshape(
                16, 128, 24, 256).transpose(2, 1, 0, 3)),
        "b_qkv": np.asarray(inputs["L0_attn_a_b_qkv"]),
        "b_qkv_col": np.ascontiguousarray(
            np.asarray(inputs["L0_attn_a_b_qkv"]).reshape(48, 128).T),
        "w_proj": np.ascontiguousarray(
            np.asarray(inputs["L0_attn_a_W_proj"]).reshape(
                16, 128, 8, 256).transpose(2, 1, 0, 3)),
        "b_proj": np.asarray(inputs["L0_attn_a_b_proj"]),
        "w_baseT": np.ascontiguousarray(
            np.asarray(inputs["L0_ffn_W_base"]).T.reshape(
                16, 128, 8, 256).transpose(2, 1, 0, 3)),
        "b_base": np.asarray(inputs["L0_ffn_b_base"]),
        "routerT": np.ascontiguousarray(
            np.asarray(inputs["L0_ffn_router_W"]).T.reshape(
                16, 128, 8).transpose(1, 0, 2)),
        "a_T": np.ascontiguousarray(
            np.asarray(inputs["L0_ffn_A"]).reshape(128, D).T.reshape(
                16, 128, 128).transpose(1, 0, 2)),
        "bw_flat": np.ascontiguousarray(
            np.asarray(inputs["L0_ffn_Bw"]).transpose(0, 2, 1).reshape(128, D)),
        "wqT": np.ascontiguousarray(
            np.asarray(inputs["L0_attn_b_Wq"]).T.reshape(
                16, 128, 128).transpose(1, 0, 2)),
        "wkT": np.ascontiguousarray(
            np.asarray(inputs["L0_attn_b_Wk"]).T.reshape(
                16, 128, 128).transpose(1, 0, 2)),
        "wogT": np.ascontiguousarray(
            np.asarray(inputs["L0_attn_b_Wog"]).T.reshape(
                16, 128, 8, 256).transpose(2, 1, 0, 3)),
        "wigT": np.ascontiguousarray(
            np.asarray(inputs["L0_attn_b_Wig"]).T.reshape(
                16, 128, 8, 256).transpose(2, 1, 0, 3)),
        "g_w": np.asarray(inputs["L0_attn_b_g_weight"]),
        "triu_in": tri,
        "ident_in": ident,
    }
    in_maps = []
    for c in range(NR):
        m = dict(base)
        m["x_my"] = np.ascontiguousarray(x[:, c * SLAB:(c + 1) * SLAB, :])
        in_maps.append(m)
    return in_maps


def kernel(**inputs):
    nc = build(N_SL)
    in_maps = make_in_maps(inputs)
    res = run_bass_kernel_spmd(nc, in_maps, core_ids=list(range(NR)))
    y = np.empty((B, T, D), np.float32)
    for c in range(NR):
        y[:, c * SLAB:(c + 1) * SLAB, :] = np.asarray(res.results[c]["y_my"])
    return y


if __name__ == "__main__":
    nc = build(1)
    print("built OK; instructions:",
          sum(len(b.instructions) for b in nc.m.functions[0].blocks))


# version 18 (final)
# speedup vs baseline: 1.0087x; 1.0031x over previous
import math
import numpy as np
import ml_dtypes
import concourse.bass as bass
import concourse.mybir as mybir
from concourse.bass_utils import run_bass_kernel_spmd
from contextlib import ExitStack

F32 = mybir.dt.float32
BF16 = mybir.dt.bfloat16
AF = mybir.ActivationFunctionType
ALU = mybir.AluOpType

B, T, D, H, HD = 4, 4096, 2048, 16, 128
NR = 8
SLAB = T // NR
TPB = SLAB // 128
DT = D // 128
GC = 32
GCS = SLAB // GC
NGC = T // GC
VH = D // 2
SCALE = HD ** -0.5
LNSC = math.log(SCALE)
SHIFT = -12.0
EPS = 1e-5
N_SL = 4


class Ctr:
    def __init__(self, ctx, nc, name):
        self.h = ctx.enter_context(nc.semaphore(name))
        self.n = 0

    def inc(self, instr, k=1):
        instr.then_inc(self.h, k)
        self.n += k
        return self.n


def build(n_sl=N_SL):
    nc = bass.Bass("TRN2", target_bir_lowering=False, debug=False)

    x_in = nc.dram_tensor("x_my", [B, SLAB, D], F32, kind="ExternalInput").ap()
    w_qkv = nc.dram_tensor("w_qkv", [24, 128, DT, 256], F32,
                           kind="ExternalInput").ap()
    b_qkv = nc.dram_tensor("b_qkv", [3 * D], F32, kind="ExternalInput").ap()
    b_qkv_col = nc.dram_tensor("b_qkv_col", [128, 48], F32, kind="ExternalInput").ap()
    w_proj = nc.dram_tensor("w_proj", [8, 128, DT, 256], F32,
                            kind="ExternalInput").ap()
    b_proj = nc.dram_tensor("b_proj", [D], F32, kind="ExternalInput").ap()
    w_baseT = nc.dram_tensor("w_baseT", [8, 128, DT, 256], F32,
                             kind="ExternalInput").ap()
    b_base = nc.dram_tensor("b_base", [D], F32, kind="ExternalInput").ap()
    routerT = nc.dram_tensor("routerT", [128, DT, 8], F32,
                             kind="ExternalInput").ap()
    a_T = nc.dram_tensor("a_T", [128, DT, 128], F32,
                         kind="ExternalInput").ap()
    bw_flat = nc.dram_tensor("bw_flat", [128, D], F32, kind="ExternalInput").ap()
    wqT = nc.dram_tensor("wqT", [128, DT, HD], F32,
                         kind="ExternalInput").ap()
    wkT = nc.dram_tensor("wkT", [128, DT, HD], F32,
                         kind="ExternalInput").ap()
    wogT = nc.dram_tensor("wogT", [8, 128, DT, 256], F32,
                          kind="ExternalInput").ap()
    wigT = nc.dram_tensor("wigT", [8, 128, DT, 256], F32,
                          kind="ExternalInput").ap()
    g_w = nc.dram_tensor("g_w", [D], F32, kind="ExternalInput").ap()
    triu_in = nc.dram_tensor("triu_in", [128, 128], F32, kind="ExternalInput").ap()
    ident_in = nc.dram_tensor("ident_in", [128, 128], F32, kind="ExternalInput").ap()
    y_out = nc.dram_tensor("y_my", [B, SLAB, D], F32, kind="ExternalOutput").ap()

    SEC = 128 * 512
    ag1_in = [nc.dram_tensor(f"ag1_in{b}", [H, 3, SEC], BF16) for b in range(B)]
    ag1_out = [nc.dram_tensor(f"ag1_out{b}", [NR, H, 3, SEC], BF16,
                              addr_space="Shared") for b in range(B)]
    ag2_in = [[nc.dram_tensor(f"ag2_in{b}_{j}", [1, 128, T], BF16)
               for j in range(2)] for b in range(B)]
    ag2_out = [[nc.dram_tensor(f"ag2_out{b}_{j}", [NR, 1, 128, T], BF16,
                               addr_space="Shared") for j in range(2)]
               for b in range(B)]
    ag2_out_unused = [nc.dram_tensor(f"ag2_ou{b}", [NR, 2, 128, T], BF16,
                              addr_space="Shared") for b in range(B)]
    pair_scr = [nc.dram_tensor(f"pair_scr{i}", [NR, 3, SEC], BF16)
                for i in range(2)]
    proj_scr = nc.dram_tensor("proj_scr", [B, NR, 2, 128, 512], BF16)
    ag3_abk = nc.dram_tensor("ag3_abk", [B, 3, GCS, 128 * GC], BF16)
    ag3_abk_o = nc.dram_tensor("ag3_abk_o", [NR, B, 3, GCS, 128 * GC], BF16,
                               addr_space="Shared")
    ag3_lam = nc.dram_tensor("ag3_lam", [B, 128, GCS], F32)
    ag3_lam_o = nc.dram_tensor("ag3_lam_o", [NR, B, 128, GCS], F32,
                               addr_space="Shared")
    ag3_v = nc.dram_tensor("ag3_v", [B, SLAB, D], BF16)
    ag3_v_o = nc.dram_tensor("ag3_v_o", [NR, B, SLAB, D], BF16,
                             addr_space="Shared")
    ag4_ot = nc.dram_tensor("ag4_ot", [8, 128, T], BF16)
    ag4_ot_o = nc.dram_tensor("ag4_ot_o", [NR, 8, 128, T], BF16,
                              addr_space="Shared")
    ag4_o2 = nc.dram_tensor("ag4_o2", [1, T], F32)
    ag4_o2_o = nc.dram_tensor("ag4_o2_o", [NR, 1, T], F32, addr_space="Shared")
    go_scr = nc.dram_tensor("go_scr", [B, SLAB, D], F32).ap()
    wog_bf = nc.dram_tensor("wog_bf", [8, 128, DT, 256], BF16)
    wig_bf = nc.dram_tensor("wig_bf", [8, 128, DT, 256], BF16)
    abk_scr = nc.dram_tensor("abk_scr", [NR, 3, GCS, 128 * GC], BF16)
    lam_scr = nc.dram_tensor("lam_scr", [NR, 128, GCS], F32)
    v_scr = nc.dram_tensor("v_scr", [NR, SLAB, VH], BF16)
    ep_ot_scr = nc.dram_tensor("ep_ot_scr", [NR, 8, 128, 512], BF16)
    ep_o2_scr = nc.dram_tensor("ep_o2_scr", [NR, 1, 512], F32)
    xlnt_scr = nc.dram_tensor("xlnt_scr", [B, 128, DT * SLAB], BF16).ap()

    with ExitStack() as ctx:
        E = ctx.enter_context

        xr = [[E(nc.sbuf_tensor(f"xr{b}_{t}", [128, D], F32)) for t in range(TPB)]
              for b in range(B)]
        triu_bf = E(nc.sbuf_tensor("triu_bf", [128, 128], BF16))
        ident_bf = E(nc.sbuf_tensor("ident_bf", [128, 128], BF16))
        ones_col_bf = E(nc.sbuf_tensor("ones_col_bf", [128, 1], BF16))
        ones_row_f32 = E(nc.sbuf_tensor("ones_row_f32", [1, 128], F32))
        consts = E(nc.sbuf_tensor("consts", [128, 16], F32))
        bias_qkv_col = E(nc.sbuf_tensor("bias_qkv_col", [128, 48], F32))
        brow_buf = E(nc.sbuf_tensor("brow_buf", [1, 512], F32))
        brow_bf = E(nc.sbuf_tensor("brow_bf", [1, 512], BF16))
        ones_row_bf = E(nc.sbuf_tensor("ones_row_bf", [1, 128], BF16))
        ln_sum = E(nc.sbuf_tensor("ln_sum", [128, 4], F32))
        ln_sq = E(nc.sbuf_tensor("ln_sq", [128, 4], F32))
        ln_m = E(nc.sbuf_tensor("ln_m", [128, 4], F32))
        ln_v = E(nc.sbuf_tensor("ln_v", [128, 4], F32))
        LN_BUFS = {}
        stage_bf = [E(nc.sbuf_tensor(f"stgbf{i}", [128, 512], BF16)) for i in range(2)]

        psA = E(nc.psum_tensor("psA", [128, 512], F32))
        psB = E(nc.psum_tensor("psB", [128, 512], F32))
        psC = E(nc.psum_tensor("psC", [128, 512], F32))
        psD = E(nc.psum_tensor("psD", [128, 512], F32))
        psE = E(nc.psum_tensor("psE", [128, 512], F32))
        psF = E(nc.psum_tensor("psF", [128, 512], F32))
        psG = E(nc.psum_tensor("psG", [128, 256], F32))
        ps_row = E(nc.psum_tensor("ps_row", [1, 512], F32))

        sx = Ctr(ctx, nc, "sx")
        sw = Ctr(ctx, nc, "sw")
        sld = Ctr(ctx, nc, "sld")
        sst = Ctr(ctx, nc, "sst")
        sag = Ctr(ctx, nc, "sag")
        smm = Ctr(ctx, nc, "smm")
        sact = Ctr(ctx, nc, "sact")
        sdve = Ctr(ctx, nc, "sdve")
        sout = Ctr(ctx, nc, "sout")
        sw2 = Ctr(ctx, nc, "sw2")
        st1s = [Ctr(ctx, nc, f"st1_{i}") for i in range(4)]
        sw3 = Ctr(ctx, nc, "sw3")
        spre = Ctr(ctx, nc, "spre")

        blk = E(nc.Block())

        CONST_VALUES = sorted({SHIFT, SCALE, LNSC, -1.0, -0.5, 2.0, 1.0 / D,
                               EPS, 0.5, 1.0, 1.0 / HD})
        for i, val in enumerate(CONST_VALUES):
            nc.const_aps.register(F32, float(val), consts[:, i:i + 1])

        pid = nc.partition_id()
        V, SC, TN, SY, GP = nc.vector, nc.scalar, nc.tensor, nc.sync, nc.gpsimd
        LAST_WAIT = {}

        def wge(eng, ctr, count):
            key = (id(eng), id(ctr))
            if LAST_WAIT.get(key, -1) >= count:
                return
            eng.wait_ge(ctr.h, count)
            LAST_WAIT[key] = count
        pid_s = nc.snap(pid)
        h_slice = [bass.ts(nc.snap(pid * 2 + j), 1) for j in range(2)]
        bch_slice = bass.ts(nc.snap(pid // 2), 1)
        vh_s = nc.snap(pid % 2)
        vh_cols = bass.ts(vh_s, VH)
        colslab = bass.ts(pid_s, 512)
        slab_slice = bass.ts(pid_s, 1)

        for b in range(B):
            for t in range(TPB):
                sx.inc(SY.dma_start(out=xr[b][t][:],
                                    in_=x_in[b, t * 128:(t + 1) * 128, :]), 16)
        sx.inc(SY.dma_start(out=bias_qkv_col[:], in_=b_qkv_col), 16)
        wge(GP, sw, sw.n)
        sw.inc(GP.dma_start(out=triu_bf[:], in_=triu_in), 16)
        wge(GP, sw, sw.n)
        sw.inc(GP.dma_start(out=ident_bf[:], in_=ident_in), 16)
        for i, val in enumerate(CONST_VALUES):
            V.memset(consts[:, i:i + 1], float(val))
        V.memset(ones_col_bf[:], 1.0)
        V.memset(ones_row_bf[:], 1.0)
        sdve.inc(V.memset(ones_row_f32[:], 1.0))

        for eng in (V, SC, TN, SY, GP):
            wge(eng, sx, sx.n)
            wge(eng, sw, sw.n)
            wge(eng, sdve, sdve.n)

        BANK_WR = {}
        BANK_RD = {}

        def pe_acquire(*banks):
            for bank in banks:
                c = BANK_RD.get(bank.name)
                if c is not None:
                    wge(TN, c[0], c[1])

        def pe_done(bank, cnt):
            BANK_WR[bank.name] = cnt

        def rd_acquire(eng, *banks):
            for bank in banks:
                c = BANK_WR.get(bank.name)
                if c is not None:
                    wge(eng, smm, c)

        def rd_done(bank, ctr, cnt):
            BANK_RD[bank.name] = (ctr, cnt)

        def emit_ln_slab(b, xln_cons=None, sst_cons=None):
            wge(SC, sdve, sdve.n)
            wge(V, sst, sst.n if sst_cons is None else sst_cons)
            wge(V, smm, smm.n if xln_cons is None else xln_cons)
            ln_sqs, xln_buf = LN_BUFS["sqs"], LN_BUFS["xln"]
            for t in range(TPB):
                i1 = V.tensor_scalar(out=xln_buf[t][:], in0=xr[b][t][:], scalar1=0.0,
                                     scalar2=0.0, op0=ALU.add, op1=ALU.add,
                                     accum_out=ln_sum[:, t:t + 1])
                sdve.inc(i1)
                if t < TPB - 1:
                    wge(V, sdve, sdve.n)
            for t in range(TPB):
                i2 = SC.activation(out=ln_sqs[:], in_=xr[b][t][:], func=AF.Square,
                                   accum_out=ln_sq[:, t:t + 1])
                sact.inc(i2)
                if t < TPB - 1:
                    wge(SC, sact, sact.n)
            wge(V, sdve, sdve.n)
            wge(V, sact, sact.n)
            sdve.inc(V.tensor_scalar(out=ln_m[:], in0=ln_sum[:], scalar1=1.0 / D,
                                     scalar2=None, op0=ALU.mult))
            wge(V, sdve, sdve.n)
            sdve.inc(V.scalar_tensor_tensor(out=ln_v[:], in0=ln_m[:], scalar=-1.0,
                                            in1=ln_m[:], op0=ALU.mult,
                                            op1=ALU.mult))
            wge(V, sdve, sdve.n)
            sdve.inc(V.scalar_tensor_tensor(out=ln_v[:], in0=ln_sq[:],
                                            scalar=1.0 / D, in1=ln_v[:],
                                            op0=ALU.mult, op1=ALU.add))
            wge(SC, sdve, sdve.n)
            sact.inc(SC.activation(out=ln_v[:], in_=ln_v[:], func=AF.Ln, bias=EPS))
            wge(SC, sact, sact.n)
            sact.inc(SC.activation(out=ln_v[:], in_=ln_v[:], func=AF.Exp,
                                   scale=-0.5))
            wge(V, sact, sact.n)
            outs = []
            for t in range(TPB):
                i7 = V.tensor_scalar(out=xln_buf[t][:], in0=xr[b][t][:],
                                     scalar1=ln_m[:, t:t + 1],
                                     scalar2=ln_v[:, t:t + 1],
                                     op0=ALU.subtract, op1=ALU.mult)
                outs.append(sdve.inc(i7))
            return outs

        def transpose_tile_into(xlnt_sb, t, xi, xln_ready, xlnt_guard=None,
                                ready_ctr=None, sst_guard=None):
            xln_buf = LN_BUFS["xln"]
            wge(TN, ready_ctr if ready_ctr is not None else sdve, xln_ready)
            for grp in range(4):
                bank = (psA, psB)[grp % 2]
                pe_acquire(bank)
                last = None
                for u in range(4):
                    i = grp * 4 + u
                    last = TN.matmul(bank[:, u * 128:(u + 1) * 128],
                                     xln_buf[xi][:, i * 128:(i + 1) * 128],
                                     ident_bf[:], start=True, stop=True)
                pe_done(bank, smm.inc(last))
                rd_acquire(SC, bank)
                if xlnt_guard is not None:
                    wge(SC, smm, xlnt_guard)
                if sst_guard is not None:
                    wge(SC, sst, sst_guard)
                dst = xlnt_sb[:].rearrange("p (i n) -> p i n", i=DT)[
                    :, grp * 4:(grp + 1) * 4, t * 128:(t + 1) * 128]
                ev = SC.copy(dst, bank[:].rearrange("p (u n) -> p u n", u=4))
                rd_done(bank, sact, sact.inc(ev))

        def ln_alloc(EE, tag):

            LN_BUFS["sqs"] = EE(nc.sbuf_tensor(f"lnsqs{tag}", [128, D], BF16))
            LN_BUFS["xln"] = [EE(nc.sbuf_tensor(f"xln{tag}{i}", [128, D], BF16))
                              for i in range(4)]

        def emit_ln_transpose_slab(b, xlnt_sb, store_scr=False, v_dst=None):
            rs = emit_ln_slab(b)
            for t in range(TPB):
                transpose_tile_into(xlnt_sb, t, t, rs[t])
                if v_dst is not None:
                    wge(SY, sdve, rs[t])
                    wge(SY, sst, sst.n)
                    sst.inc(SY.dma_start(
                        out=v_dst[b, t * 128:(t + 1) * 128, :],
                        in_=LN_BUFS["xln"][t][:]), 16)
            if store_scr:
                wge(SY, sact, sact.n)
                wge(SY, sst, sst.n)
                sst.inc(SY.dma_start(out=xlnt_scr[b], in_=xlnt_sb[:]), 16)

        def load_panel(buf_sb, src_cols_ap, ncols):
            nt = DT
            return GP.dma_start(
                out=buf_sb[:].rearrange("p (i n) -> p i n", i=nt),
                in_=src_cols_ap.rearrange("(i p) n -> p i n", p=128))

        def allgather(in_ap, out_ap):
            return sag.inc(GP.collective_compute(
                "AllGather", ALU.bypass, replica_groups=[list(range(NR))],
                ins=[in_ap.opt()], outs=[out_ap.opt()]))

        def phase_fence():
            for eng in (V, SC, TN, SY, GP):
                for c in (sld, sst, smm, sact, sdve, sw, sw2, sw3, spre,
                          st1s[0], st1s[1], st1s[2], st1s[3]):
                    wge(eng, c, c.n)

        if n_sl >= 1:
            with ExitStack() as c1:
                E1 = c1.enter_context
                xlnt2 = [E1(nc.sbuf_tensor(f"xlnt1_{i}", [128, DT * SLAB], BF16))
                         for i in range(2)]
                ln_alloc(E1, "p1")
                wpan = [E1(nc.sbuf_tensor(f"wpan1_{i}", [128, DT * 256], BF16))
                        for i in range(2)]
                stg1 = [E1(nc.sbuf_tensor(f"stg1_{i}", [128, 512], BF16))
                        for i in range(4)]
                AG1_done = []
                panel_used = [0, 0]
                stage_st = [0, 0, 0, 0]
                stg_i = [0]
                brow_rd = 0
                QEND = 0
                p1_sst0 = sst.n
                prev_trans = 0
                for pr in range(2):
                    for bb in range(2):
                        b = 2 * pr + bb
                        rs = emit_ln_slab(b, xln_cons=prev_trans,
                                          sst_cons=p1_sst0)
                        for t in range(TPB):
                            transpose_tile_into(xlnt2[bb], t, t, rs[t],
                                                xlnt_guard=QEND if pr else None)
                        prev_trans = smm.n
                    trans_done = sact.n
                    for pan in range(24):
                        pbuf = wpan[pan % 2]
                        psem = (sw, sw2)[pan % 2]
                        GP.wait_ge(smm.h, panel_used[pan % 2])
                        wge(GP, psem, psem.n)
                        psem.inc(GP.dma_start(
                            out=pbuf[:].rearrange("p (i n) -> p i n", i=DT),
                            in_=w_qkv[pan]), 16)
                        PANLD = psem.n
                        if pan >= 16:
                            pv = pan - 16
                            wge(GP, sw3, sw3.n)
                            wge(GP, smm, brow_rd)
                            sw3.inc(GP.dma_start(
                                out=brow_bf[0:1, 0:256],
                                in_=b_qkv.unsqueeze(0)[
                                    0:1, 2 * D + pv * 256:
                                    2 * D + (pv + 1) * 256]), 16)
                            BROWLD = sw3.n
                        wge(TN, psem, PANLD)
                        if pan == 0:
                            wge(TN, sact, trans_done)
                        if pan < 16:
                            for su in range(2):
                                ot = pan * 2 + su
                                for bb in range(2):
                                    b = 2 * pr + bb
                                    bank = (psC, psD, psE, psF)[
                                        (su % 2) * 2 + bb]
                                    pe_acquire(bank)
                                    last = None
                                    for i in range(DT):
                                        last = TN.matmul(
                                            bank[:],
                                            pbuf[:, i * 256 + su * 128:
                                                 i * 256 + (su + 1) * 128],
                                            xlnt2[bb][:, i * 512:(i + 1) * 512],
                                            start=(i == 0), stop=(i == DT - 1))
                                    pe_done(bank, smm.inc(last))
                                    rd_acquire(SC, bank)
                                    sl = stg_i[0] % 4
                                    stg_i[0] += 1
                                    wge(SC, st1s[sl], stage_st[sl])
                                    st = stg1[sl]
                                    ev = SC.activation(
                                        out=st[:], in_=bank[:],
                                        func=AF.Identity,
                                        bias=bias_qkv_col[:, ot:ot + 1])
                                    rd_done(bank, sact, sact.inc(ev))
                                    wge(SY, sact, sact.n)
                                    hh, kind = (ot, 0) if ot < 16 else (ot - 16, 1)
                                    dst = ag1_in[b].ap()[hh, kind].rearrange(
                                        "(p n) -> p n", n=512)
                                    stage_st[sl] = st1s[sl].inc(
                                        SY.dma_start(out=dst, in_=st[:]), 16)
                        else:
                            pv = pan - 16
                            wge(TN, sw3, BROWLD)
                            for t in range(TPB):
                                for bb in range(2):
                                    b = 2 * pr + bb
                                    bank = (psC, psD, psE, psF)[
                                        (t % 2) * 2 + bb]
                                    pe_acquire(bank)
                                    last = None
                                    for i in range(DT):
                                        last = TN.matmul(
                                            bank[:, 0:256],
                                            xlnt2[bb][:, i * 512 + t * 128:
                                                      i * 512 + (t + 1) * 128],
                                            pbuf[:, i * 256:(i + 1) * 256],
                                            start=(i == 0), stop=False)
                                    last = TN.matmul(bank[:, 0:256],
                                                     ones_row_bf[:],
                                                     brow_bf[0:1, 0:256],
                                                     start=False, stop=True)
                                    pe_done(bank, smm.inc(last))
                                    brow_rd = smm.n
                                    rd_acquire(SC, bank)
                                    sl = stg_i[0] % 4
                                    stg_i[0] += 1
                                    wge(SC, st1s[sl], stage_st[sl])
                                    st = stg1[sl]
                                    ev = SC.copy(st[:, 0:256], bank[:, 0:256])
                                    rd_done(bank, sact, sact.inc(ev))
                                    wge(SY, sact, sact.n)
                                    dst = ag1_in[b].ap()[
                                        2 * pv:2 * pv + 2, 2].rearrange(
                                        "h (t2 v) -> t2 h v", v=128)[
                                        t * 128:(t + 1) * 128, :, :]
                                    srcv = st[:, 0:256].rearrange(
                                        "p (h v) -> p h v", h=2)
                                    stage_st[sl] = st1s[sl].inc(
                                        SY.dma_start(out=dst, in_=srcv), 16)
                        panel_used[pan % 2] = smm.n
                    QEND = smm.n
                    if pr == 1:
                        for j in range(2):
                            wge(GP, sag, AG1_done[0])
                            if j:
                                wge(GP, spre, 16)
                            gsrc0 = ag1_out[0].ap().rearrange(
                                "r h k m -> h r k m")[h_slice[j]]
                            spre.inc(GP.dma_start(
                                out=pair_scr[j].ap().unsqueeze(0),
                                in_=gsrc0), 16)
                    for bb in range(2):
                        b = 2 * pr + bb
                        wge(GP, sst, sst.n)
                        for k in range(4):
                            wge(GP, st1s[k], st1s[k].n)
                        allgather(ag1_in[b].ap(), ag1_out[b].ap())
                        AG1_done.append(sag.n)


            with ExitStack() as c2:
                E2 = c2.enter_context
                phase_fence()
                kt_sb2 = [E2(nc.sbuf_tensor(f"kt_sb{i}", [128, T], BF16))
                          for i in range(2)]
                v_sb2 = [E2(nc.sbuf_tensor(f"v_sb{i}", [128, T], BF16))
                         for i in range(2)]
                qt_sb2 = [E2(nc.sbuf_tensor(f"qt_sb{i}", [128, T], BF16))
                          for i in range(2)]
                pt_sb = [E2(nc.sbuf_tensor(f"pt{i}", [128, 512], BF16))
                         for i in range(4)]
                attn_d = E2(nc.sbuf_tensor("attn_d", [1, 512], F32))
                attn_db = E2(nc.sbuf_tensor("attn_db", [128, 512], F32))
                wpan3 = [E2(nc.sbuf_tensor(f"wpan3_{i}", [128, DT * 256], BF16))
                         for i in range(2)]

                sbanks = [psA, psB, psE, psF]
                obank = [psC, psD]
                kv_used = [smm.n, smm.n]
                kv_ld = [0, 0]
                pt_pv = [0, 0, 0, 0]
                stage_o = [sst.n, sst.n]
                pend_bc = [None]
                pend_ml = [0]

                def emit_pair_loads(p):
                    b, j = p >> 1, p & 1
                    wge(SY, smm, kv_used[p % 2])
                    if p >= 2:
                        wge(SY, sag, AG1_done[b])
                        wge(SY, sld, sld.n)
                        gsrc = ag1_out[b].ap().rearrange(
                            "r h k m -> h r k m")[h_slice[j]]
                        sld.inc(SY.dma_start(
                            out=pair_scr[p % 2].ap().unsqueeze(0),
                            in_=gsrc), 16)
                        wge(SY, sld, sld.n)
                    else:
                        wge(SY, sld, sld.n)
                        wge(SY, spre, 16 * (p + 1))
                    rows_pn = pair_scr[p % 2].ap().rearrange(
                        "r k (p n) -> (r k p) n", n=512)
                    rows_v = pair_scr[p % 2].ap().rearrange(
                        "r k (j2 p v) -> (r k j2 p) v", j2=4, p=128)
                    for r in range(NR):
                        base = r * 3
                        sld.inc(SY.dma_start(
                            out=qt_sb2[p % 2][:, r * 512:(r + 1) * 512],
                            in_=rows_pn[(base + 0) * 128:(base + 1) * 128, :]), 16)
                        sld.inc(SY.dma_start(
                            out=kt_sb2[p % 2][:, r * 512:(r + 1) * 512],
                            in_=rows_pn[(base + 1) * 128:(base + 2) * 128, :]), 16)
                        for j2l in range(4):
                            rr = (base + 2) * 4 + j2l
                            sld.inc(SY.dma_start(
                                out=v_sb2[p % 2][:, (4 * r + j2l) * 128:
                                                 (4 * r + j2l + 1) * 128],
                                in_=rows_v[rr * 128:(rr + 1) * 128, :]), 16)
                    kv_ld[p % 2] = sld.n

                AG2_done = {}
                po_used = [smm.n]
                p3_pan_used = [smm.n, smm.n]
                p3_brow_rd = [smm.n]

                def emit_proj(b, par):
                    wge(SY, sag, AG2_done[b])
                    wge(SY, sld, sld.n)
                    for jj2 in range(2):
                        sld.inc(SY.dma_start(
                            out=proj_scr.ap()[b, :, jj2],
                            in_=ag2_out[b][jj2].ap()[:, 0, :, colslab]), 16)
                    wge(SY, sld, sld.n)
                    wge(SY, smm, kv_used[par])
                    wge(SY, smm, po_used[0])
                    for h in range(H):
                        r, jj = h >> 1, h & 1
                        buf = kt_sb2[par] if h < 8 else v_sb2[par]
                        hh = h % 8
                        sld.inc(SY.dma_start(
                            out=buf[:, hh * 512:(hh + 1) * 512],
                            in_=proj_scr.ap()[b, r, jj]), 16)
                    po_ld = sld.n
                    for hp in range(8):
                        pbuf = wpan3[hp % 2]
                        psem = (sw, sw2)[hp % 2]
                        wge(GP, smm, p3_pan_used[hp % 2])
                        wge(GP, psem, psem.n)
                        psem.inc(GP.dma_start(
                            out=pbuf[:].rearrange("p (i n) -> p i n", i=DT),
                            in_=w_proj[hp]), 16)
                        PANLD = psem.n
                        wge(GP, sw3, sw3.n)
                        wge(GP, smm, p3_brow_rd[0])
                        sw3.inc(GP.dma_start(
                            out=brow_bf[0:1, 0:256],
                            in_=b_proj.unsqueeze(0)[
                                0:1, hp * 256:(hp + 1) * 256]), 16)
                        BROWLD = sw3.n
                        wge(TN, psem, PANLD)
                        wge(TN, sw3, BROWLD)
                        wge(TN, sld, po_ld)
                        for tl in range(TPB):
                            pe_acquire(psG)
                            for h in range(H):
                                buf = kt_sb2[par] if h < 8 else v_sb2[par]
                                hh = h % 8
                                TN.matmul(
                                    psG[:, 0:256],
                                    buf[:, hh * 512 + tl * 128:
                                        hh * 512 + (tl + 1) * 128],
                                    pbuf[:, h * 256:(h + 1) * 256],
                                    start=(h == 0), stop=False)
                            last = TN.matmul(psG[:, 0:256], ones_row_bf[:],
                                             brow_bf[0:1, 0:256],
                                             start=False, stop=True)
                            pe_done(psG, smm.inc(last))
                            rd_acquire(V, psG)
                            ad = V.tensor_tensor(
                                out=xr[b][tl][:, hp * 256:(hp + 1) * 256],
                                in0=psG[:, 0:256],
                                in1=xr[b][tl][:, hp * 256:(hp + 1) * 256],
                                op=ALU.add)
                            rd_done(psG, sdve, sdve.inc(ad))
                        p3_pan_used[hp % 2] = smm.n
                        p3_brow_rd[0] = smm.n
                    po_used[0] = smm.n
                    kv_used[par] = smm.n

                emit_pair_loads(0)
                for p in range(8):
                    b, j = p >> 1, p & 1
                    if p + 1 < 8:
                        emit_pair_loads(p + 1)
                    kt_sb, v_sb, qt_sb = kt_sb2[p % 2], v_sb2[p % 2], qt_sb2[p % 2]
                    wge(TN, sld, kv_ld[p % 2])
                    pend = None
                    for qc in range(8):
                        nk = 4 * qc + 4
                        ob = obank[qc % 2]
                        exp_cnt = {}
                        msk_cnt = {}
                        s_cnt = {}

                        def emit_S(j2):
                            m = j2 - 4 * qc
                            coloff = 0 if m < 0 else m * 128
                            sb = sbanks[j2 % 4]
                            pe_acquire(sb)
                            inst = TN.matmul(
                                sb[:, coloff:512],
                                kt_sb[:, j2 * 128:(j2 + 1) * 128],
                                qt_sb[:, qc * 512 + coloff:(qc + 1) * 512],
                                start=True, stop=True)
                            s_cnt[j2] = smm.inc(inst)
                            pe_done(sb, s_cnt[j2])

                        def emit_exp(j2):
                            m = j2 - 4 * qc
                            coloff = 0 if m < 0 else m * 128
                            sb = sbanks[j2 % 4]
                            wge(SC, smm, s_cnt[j2])
                            wge(SC, smm, pt_pv[j2 % 4])
                            ex = SC.activation(out=pt_sb[j2 % 4][:, coloff:512],
                                               in_=sb[:, coloff:512],
                                               func=AF.Exp, scale=SCALE, bias=SHIFT)
                            cnt = sact.inc(ex)
                            rd_done(sb, sact, cnt)
                            exp_cnt[j2] = cnt
                            if m >= 0:
                                wge(V, sact, cnt)
                                mk = V.tensor_tensor(
                                    out=pt_sb[j2 % 4][:, coloff:coloff + 128],
                                    in0=pt_sb[j2 % 4][:, coloff:coloff + 128],
                                    in1=triu_bf[:], op=ALU.mult)
                                msk_cnt[j2] = sdve.inc(mk)

                        def emit_PV(j2):
                            m = j2 - 4 * qc
                            coloff = 0 if m < 0 else m * 128
                            wge(TN, sact, exp_cnt[j2])
                            if m >= 0:
                                wge(TN, sdve, msk_cnt[j2])
                            if j2 == 0:
                                pe_acquire(ob)
                                pe_acquire(ps_row)
                            TN.matmul(ob[:, coloff:512],
                                      v_sb[:, j2 * 128:(j2 + 1) * 128],
                                      pt_sb[j2 % 4][:, coloff:512],
                                      start=(j2 == 0), stop=(j2 == nk - 1))
                            dnm = TN.matmul(ps_row[0:1, coloff:512],
                                            ones_col_bf[:],
                                            pt_sb[j2 % 4][:, coloff:512],
                                            start=(j2 == 0), stop=(j2 == nk - 1))
                            pt_pv[j2 % 4] = smm.inc(dnm)

                        def emit_epilogue_front(st8):
                            qc0, ob0 = st8
                            rd_acquire(V, ps_row)
                            if pend_bc[0] is not None:
                                wge(V, smm, pend_bc[0])
                            rc = V.reciprocal(out=attn_d[:], in_=ps_row[0:1, :])
                            rd_done(ps_row, sdve, sdve.inc(rc))
                            return sdve.n

                        def emit_epilogue_tail(st8, rc_cnt):
                            qc0, ob0 = st8
                            bb = sbanks[(4 * qc0 + 4 + 2) % 4]
                            pe_acquire(bb)
                            wge(TN, sdve, rc_cnt)
                            bc = TN.matmul(bb[:], ones_row_f32[:], attn_d[:],
                                           start=True, stop=True)
                            pend_bc[0] = smm.inc(bc)
                            pe_done(bb, pend_bc[0])
                            rd_acquire(SC, bb)
                            wge(SC, sdve, pend_ml[0])
                            cp = SC.copy(attn_db[:], bb[:])
                            rd_done(bb, sact, sact.inc(cp))
                            wge(V, sact, sact.n)
                            wge(V, sst, stage_o[qc0 % 2])
                            rd_acquire(V, ob0)
                            ml = V.tensor_tensor(out=stage_bf[qc0 % 2][:],
                                                 in0=ob0[:], in1=attn_db[:],
                                                 op=ALU.mult)
                            OEP = sdve.inc(ml)
                            pend_ml[0] = OEP
                            rd_done(ob0, sdve, OEP)
                            wge(SY, sdve, OEP)
                            wge(SY, sst, sst.n)
                            stage_o[qc0 % 2] = sst.inc(SY.dma_start(
                                out=ag2_in[b][j].ap()[0, :,
                                                      qc0 * 512:(qc0 + 1) * 512],
                                in_=stage_bf[qc0 % 2][:]), 16)

                        emit_S(0)
                        emit_exp(0)
                        if nk > 1:
                            emit_S(1)
                            emit_exp(1)
                        for j2 in range(nk):
                            if j2 + 2 < nk:
                                emit_S(j2 + 2)
                                emit_exp(j2 + 2)
                            emit_PV(j2)
                        if pend is not None:
                            emit_epilogue_tail(*pend)
                            pend = None
                        pe_done(ob, smm.n)
                        pe_done(ps_row, smm.n)
                        rcc = emit_epilogue_front((qc, ob))
                        pend = ((qc, ob), rcc)
                    emit_epilogue_tail(*pend)
                    pend = None
                    kv_used[p % 2] = smm.n
                    if j == 0 and b >= 1:
                        emit_proj(b - 1, 0)
                    wge(GP, sst, sst.n)
                    allgather(ag2_in[b][j].ap(), ag2_out[b][j].ap())
                    if j == 1:
                        AG2_done[b] = sag.n
                emit_proj(3, 1)

        def emit_ffn(tag, final=False):
            with ExitStack() as cf:
                EF = cf.enter_context
                phase_fence()
                xlnt = EF(nc.sbuf_tensor(f"xlntF{tag}", [128, DT * SLAB], BF16))
                at_sb = EF(nc.sbuf_tensor(f"at{tag}", [128, DT * 128], BF16))
                rt_sb = EF(nc.sbuf_tensor(f"rt{tag}", [128, DT * 8], BF16))
                hgt_all = EF(nc.sbuf_tensor(f"hgt{tag}", [128, 16 * 128], BF16))
                r8 = EF(nc.sbuf_tensor(f"r8{tag}", [128, 8], F32))
                rmax = EF(nc.sbuf_tensor(f"rmax{tag}", [128, 1], F32))
                roh = EF(nc.sbuf_tensor(f"roh{tag}", [128, 8], F32))
                hg_sb = EF(nc.sbuf_tensor(f"hg{tag}", [128, 128], BF16))
                bwf_pre = EF(nc.sbuf_tensor(f"bwfp{tag}", [128, D], BF16))
                cfl = EF(ExitStack())
                ln_alloc(cfl.enter_context, "f" + tag)

                wge(GP, smm, smm.n)
                wge(GP, sw, sw.n)
                sw.inc(GP.dma_start(
                    out=at_sb[:].rearrange("p (i n) -> p i n", i=DT),
                    in_=a_T), 16)
                wge(GP, sw, sw.n)
                sw.inc(GP.dma_start(
                    out=rt_sb[:].rearrange("p (i n) -> p i n", i=DT),
                    in_=routerT), 16)
                wge(GP, sw, sw.n)
                sw.inc(GP.dma_start(out=bwf_pre[:], in_=bw_flat), 16)
                for b in range(B):
                    emit_ln_transpose_slab(b, xlnt, store_scr=True)
                    wge(TN, sw, sw.n)
                    wge(TN, sact, sact.n)
                    for tl in range(TPB):
                        pe_acquire(psG)
                        last = None
                        for i in range(DT):
                            last = TN.matmul(
                                psG[:, 0:8],
                                xlnt[:, i * 512 + tl * 128:i * 512 + (tl + 1) * 128],
                                rt_sb[:, i * 8:(i + 1) * 8],
                                start=(i == 0), stop=(i == DT - 1))
                        pe_done(psG, smm.inc(last))
                        rd_acquire(V, psG)
                        cpr = V.tensor_copy(out=r8[:], in_=psG[:, 0:8])
                        rd_done(psG, sdve, sdve.inc(cpr))
                        wge(V, sdve, sdve.n)
                        sdve.inc(V.tensor_reduce(out=rmax[:], in_=r8[:],
                                                 axis=mybir.AxisListType.X,
                                                 op=ALU.max))
                        wge(V, sdve, sdve.n)
                        sdve.inc(V.tensor_scalar(out=roh[:], in0=r8[:],
                                                 scalar1=rmax[:], scalar2=2.0,
                                                 op0=ALU.is_equal, op1=ALU.mult))
                        pe_acquire(psA)
                        last = None
                        for i in range(DT):
                            last = TN.matmul(
                                psA[:, 0:128],
                                xlnt[:, i * 512 + tl * 128:i * 512 + (tl + 1) * 128],
                                at_sb[:, i * 128:(i + 1) * 128],
                                start=(i == 0), stop=(i == DT - 1))
                        pe_done(psA, smm.inc(last))
                        rd_acquire(V, psA)
                        wge(V, sdve, sdve.n)
                        gt = V.scalar_tensor_tensor(
                            out=hg_sb[:].rearrange("p (a c2) -> p a c2", a=8),
                            in0=psA[:, 0:128].rearrange("p (a c2) -> p a c2", a=8),
                            scalar=1.0,
                            in1=roh[:].unsqueeze(2).broadcast_to([128, 8, 16]),
                            op0=ALU.mult, op1=ALU.mult)
                        gcnt = sdve.inc(gt)
                        rd_done(psA, sdve, gcnt)
                        wge(TN, sdve, gcnt)
                        pe_acquire(psB)
                        tr = TN.matmul(psB[:, 0:128], hg_sb[:], ident_bf[:],
                                       start=True, stop=True)
                        pe_done(psB, smm.inc(tr))
                        rd_acquire(SC, psB)
                        ev = SC.copy(hgt_all[:, (b * TPB + tl) * 128:
                                             (b * TPB + tl + 1) * 128],
                                     psB[:, 0:128])
                        rd_done(psB, sact, sact.inc(ev))
                cfl.close()
                with ExitStack() as cf2:
                    EF2 = cf2.enter_context
                    phase_fence()
                    whs = [EF2(nc.sbuf_tensor(f"wh{tag}{i}", [128, DT * 256],
                                              BF16)) for i in range(2)]
                    xl2 = [xlnt,
                           EF2(nc.sbuf_tensor(f"xlnt2{tag}",
                                              [128, DT * SLAB], BF16))]
                    bwf_sb = bwf_pre
                    pan_used = [smm.n, smm.n]
                    xl_used = [smm.n, smm.n]
                    xl_ld = [0, 0]
                    brow_rd = smm.n
                    for k in range(2):
                        wge(SY, sld, sld.n)
                        wge(SY, sst, sst.n)
                        sld.inc(SY.dma_start(out=xl2[k][:], in_=xlnt_scr[k]), 16)
                        xl_ld[k] = sld.n
                    seq = [(hp2, b2) for hp2 in range(8) for b2 in range(B)]
                    pbuf = None
                    for si, (hp, b) in enumerate(seq):
                        k = b % 2
                        if b == 0:
                            pbuf = whs[hp % 2]
                            psem = (sw, sw2)[hp % 2]
                            wge(GP, smm, pan_used[hp % 2])
                            wge(GP, psem, psem.n)
                            psem.inc(GP.dma_start(
                                out=pbuf[:].rearrange("p (i n) -> p i n", i=DT),
                                in_=w_baseT[hp]), 16)
                            PANLD = psem.n
                            wge(GP, sw3, sw3.n)
                            wge(GP, smm, brow_rd)
                            sw3.inc(GP.dma_start(
                                out=brow_bf[0:1, 0:256],
                                in_=b_base.unsqueeze(0)[
                                    0:1, hp * 256:(hp + 1) * 256]), 16)
                            BROWLD = sw3.n
                            wge(TN, psem, PANLD)
                            wge(TN, sw3, BROWLD)
                        wge(TN, sld, xl_ld[k])
                        for tl in range(TPB):
                            bank = (psC, psD, psE, psF)[tl % 4]
                            pe_acquire(bank)
                            for i in range(DT):
                                TN.matmul(
                                    bank[:, 0:256],
                                    xl2[k][:, i * 512 + tl * 128:
                                           i * 512 + (tl + 1) * 128],
                                    pbuf[:, i * 256:(i + 1) * 256],
                                    start=(i == 0), stop=False)
                            TN.matmul(bank[:, 0:256],
                                      hgt_all[:, (b * TPB + tl) * 128:
                                              (b * TPB + tl + 1) * 128],
                                      bwf_sb[:, hp * 256:(hp + 1) * 256],
                                      start=False, stop=False)
                            last = TN.matmul(bank[:, 0:256], ones_row_bf[:],
                                             brow_bf[0:1, 0:256],
                                             start=False, stop=True)
                            pe_done(bank, smm.inc(last))
                            rd_acquire(V, bank)
                            ad = V.tensor_tensor(
                                out=xr[b][tl][:, hp * 256:(hp + 1) * 256],
                                in0=bank[:, 0:256],
                                in1=xr[b][tl][:, hp * 256:(hp + 1) * 256],
                                op=ALU.add)
                            rd_done(bank, sdve, sdve.inc(ad))
                            if final and hp == 7:
                                wge(SY, sdve, sdve.n)
                                sout.inc(SY.dma_start(
                                    out=y_out[b, tl * 128:(tl + 1) * 128, :],
                                    in_=xr[b][tl][:]), 16)
                        xl_used[k] = smm.n
                        if b == B - 1:
                            pan_used[hp % 2] = smm.n
                            brow_rd = smm.n
                        if si + 2 < len(seq):
                            nb = seq[si + 2][1]
                            wge(SY, smm, xl_used[k])
                            wge(SY, sld, sld.n)
                            sld.inc(SY.dma_start(out=xl2[nb % 2][:],
                                                 in_=xlnt_scr[nb]), 16)
                            xl_ld[nb % 2] = sld.n

        if n_sl >= 2:
            emit_ffn("a")

        if n_sl >= 3:
            vh_expr = pid % 2
            bch_expr = pid // 2
            with ExitStack() as c5:
                E5 = c5.enter_context
                phase_fence()
                xlnt = E5(nc.sbuf_tensor("xlnt5", [128, DT * SLAB], BF16))
                ln16_s = E5(nc.sbuf_tensor("ln16s", [128, 16], F32))
                ln16_q = E5(nc.sbuf_tensor("ln16q", [128, 16], F32))
                ln16_m = E5(nc.sbuf_tensor("ln16m", [128, 16], F32))
                ln16_v = E5(nc.sbuf_tensor("ln16v", [128, 16], F32))
                c5i = E5(ExitStack())
                E5i = c5i.enter_context
                ln_alloc(E5i, "p5")
                wq_sb = E5i(nc.sbuf_tensor("wq_sb", [128, DT * 128], BF16))
                wk_sb = E5i(nc.sbuf_tensor("wk_sb", [128, DT * 128], BF16))
                gq_s = E5i(nc.sbuf_tensor("gq_s", [128, 512], F32))
                gk_s = E5i(nc.sbuf_tensor("gk_s", [128, 512], F32))
                gsp = E5i(nc.sbuf_tensor("gsp", [128, 512], F32))
                gb_cum = E5i(nc.sbuf_tensor("gb_cum", [128, 512], F32))
                gexp = E5i(nc.sbuf_tensor("gexp", [128, 512], F32))
                gnb = E5i(nc.sbuf_tensor("gnb", [128, GCS], F32))
                glam = E5i(nc.sbuf_tensor("glam", [128, GCS], F32))
                gabk = E5i(nc.sbuf_tensor("gabk", [128, 512], BF16))
                gkh = E5i(nc.sbuf_tensor("gkh", [128, 512], BF16))
                kh_tm = E5i(nc.sbuf_tensor("kh_tm", [32, GCS * 128], BF16))
                zeros32 = E5i(nc.sbuf_tensor("zeros32", [128, 32], F32))
                sdve.inc(V.memset(zeros32[:], 0.0))

                wge(GP, smm, smm.n)
                wge(GP, sw, sw.n)
                sw.inc(GP.dma_start(
                    out=wq_sb[:].rearrange("p (i n) -> p i n", i=DT),
                    in_=wqT), 16)
                wge(GP, sw, sw.n)
                sw.inc(GP.dma_start(
                    out=wk_sb[:].rearrange("p (i n) -> p i n", i=DT),
                    in_=wkT), 16)

                xln_buf5 = LN_BUFS["xln"]
                wge(SC, sdve, sdve.n)
                wge(V, sst, sst.n)
                wge(V, smm, smm.n)
                for b in range(B):
                    for t in range(TPB):
                        cix = b * TPB + t
                        sdve.inc(V.tensor_scalar(
                            out=xln_buf5[t][:], in0=xr[b][t][:],
                            scalar1=0.0, scalar2=0.0,
                            op0=ALU.add, op1=ALU.add,
                            accum_out=ln16_s[:, cix:cix + 1]))
                        if cix < 15:
                            wge(V, sdve, sdve.n)
                for b in range(B):
                    for t in range(TPB):
                        cix = b * TPB + t
                        sact.inc(SC.activation(
                            out=LN_BUFS["sqs"][:], in_=xr[b][t][:],
                            func=AF.Square,
                            accum_out=ln16_q[:, cix:cix + 1]))
                        if cix < 15:
                            wge(SC, sact, sact.n)
                wge(V, sdve, sdve.n)
                wge(V, sact, sact.n)
                sdve.inc(V.tensor_scalar(out=ln16_m[:], in0=ln16_s[:],
                                         scalar1=1.0 / D, scalar2=None,
                                         op0=ALU.mult))
                wge(V, sdve, sdve.n)
                sdve.inc(V.scalar_tensor_tensor(out=ln16_v[:], in0=ln16_m[:],
                                                scalar=-1.0, in1=ln16_m[:],
                                                op0=ALU.mult, op1=ALU.mult))
                wge(V, sdve, sdve.n)
                sdve.inc(V.scalar_tensor_tensor(out=ln16_v[:], in0=ln16_q[:],
                                                scalar=1.0 / D, in1=ln16_v[:],
                                                op0=ALU.mult, op1=ALU.add))
                wge(SC, sdve, sdve.n)
                sact.inc(SC.activation(out=ln16_v[:], in_=ln16_v[:],
                                       func=AF.Ln, bias=EPS))
                wge(SC, sact, sact.n)
                sact.inc(SC.activation(out=ln16_v[:], in_=ln16_v[:],
                                       func=AF.Exp, scale=-0.5))
                wge(V, sact, sact.n)
                vst5 = [0, 0, 0, 0]
                for b in range(B):
                    for t in range(TPB):
                        cix = b * TPB + t
                        wge(V, sst, vst5[t])
                        AP7 = sdve.inc(V.tensor_scalar(
                            out=xln_buf5[t][:], in0=xr[b][t][:],
                            scalar1=ln16_m[:, cix:cix + 1],
                            scalar2=ln16_v[:, cix:cix + 1],
                            op0=ALU.subtract, op1=ALU.mult))
                        wge(SY, sdve, AP7)
                        wge(SY, sst, sst.n)
                        vst5[t] = sst.inc(SY.dma_start(
                            out=ag3_v.ap()[b, t * 128:(t + 1) * 128, :],
                            in_=xln_buf5[t][:]), 16)
                wge(GP, sst, sst.n)
                allgather(ag3_v.ap(), ag3_v_o.ap())
                AG3V_done = sag.n
                TRANS_done = 0
                for b in range(B):
                    wge(GP, smm, TRANS_done)
                    wge(GP, sw, sw.n)
                    for t in range(TPB):
                        sw.inc(GP.dma_start(
                            out=xln_buf5[t][:],
                            in_=ag3_v.ap()[b, t * 128:(t + 1) * 128, :]), 16)
                    XLN_LD = sw.n
                    ST_prev = sst.n
                    for t in range(TPB):
                        transpose_tile_into(xlnt, t, t, XLN_LD,
                                            ready_ctr=sw, sst_guard=ST_prev)
                    TRANS_done = smm.n
                    wge(SY, sact, sact.n)
                    wge(SY, sst, sst.n)
                    sst.inc(SY.dma_start(out=xlnt_scr[b], in_=xlnt[:]), 16)
                    wge(TN, sw, sw.n)
                    wge(TN, sact, sact.n)
                    last = None
                    for i in range(DT):
                        last = TN.matmul(psC[:],
                                         wq_sb[:, i * 128:(i + 1) * 128],
                                         xlnt[:, i * 512:(i + 1) * 512],
                                         start=(i == 0), stop=(i == DT - 1))
                    smm.inc(last)
                    last = None
                    for i in range(DT):
                        last = TN.matmul(psD[:],
                                         wk_sb[:, i * 128:(i + 1) * 128],
                                         xlnt[:, i * 512:(i + 1) * 512],
                                         start=(i == 0), stop=(i == DT - 1))
                    smm.inc(last)
                    wge(SC, smm, smm.n)
                    sact.inc(SC.activation(out=gq_s[:], in_=psC[:], func=AF.Silu))
                    sact.inc(SC.activation(out=gk_s[:], in_=psD[:], func=AF.Sigmoid))
                    wge(SC, sact, sact.n)
                    sact.inc(SC.activation(out=gsp[:], in_=gk_s[:], func=AF.Exp))
                    wge(SC, sact, sact.n)
                    sact.inc(SC.activation(out=gsp[:], in_=gsp[:], func=AF.Ln,
                                           bias=1.0))
                    wge(V, sact, sact.n)
                    last = None
                    for cc in range(GCS):
                        last = V.tensor_tensor_scan(
                            out=gb_cum[:, cc * GC:(cc + 1) * GC],
                            data0=gsp[:, cc * GC:(cc + 1) * GC], data1=zeros32[:],
                            initial=0.0, op0=ALU.add, op1=ALU.add)
                    CUM = sdve.inc(last)
                    wge(SC, sdve, CUM)
                    sact.inc(SC.activation(out=glam[:], in_=gb_cum[:, GC - 1::GC],
                                           func=AF.Exp, scale=-1.0))
                    wge(V, sdve, CUM)
                    sdve.inc(V.tensor_scalar(out=gnb[:], in0=gb_cum[:, GC - 1::GC],
                                             scalar1=-1.0, scalar2=None,
                                             op0=ALU.mult))
                    wge(SY, sact, sact.n)
                    wge(SY, sst, sst.n)
                    sst.inc(SY.dma_start(out=ag3_lam.ap()[b], in_=glam[:]), 16)
                    sact.inc(SC.activation(out=gexp[:], in_=gb_cum[:], func=AF.Exp,
                                           scale=-1.0, bias=LNSC))
                    wge(V, sact, sact.n)
                    wge(V, sst, sst.n)
                    AW = sdve.inc(V.tensor_tensor(out=gabk[:], in0=gq_s[:],
                                                  in1=gexp[:], op=ALU.mult))
                    wge(SY, sdve, AW)
                    wge(SY, sst, sst.n)
                    sst.inc(SY.dma_start(
                        out=ag3_abk.ap()[b, 0].rearrange("c (p g) -> p c g", p=128),
                        in_=gabk[:].rearrange("p (c g) -> p c g", c=GCS)), 16)
                    wge(SC, sst, sst.n)
                    wge(SC, sdve, sdve.n)
                    sact.inc(SC.activation(out=gexp[:], in_=gb_cum[:], func=AF.Exp))
                    wge(V, sact, sact.n)
                    wge(V, sst, sst.n)
                    BW = sdve.inc(V.tensor_tensor(out=gabk[:], in0=gk_s[:],
                                                  in1=gexp[:], op=ALU.mult))
                    wge(SY, sdve, BW)
                    wge(SY, sst, sst.n)
                    sst.inc(SY.dma_start(
                        out=ag3_abk.ap()[b, 1].rearrange("c (p g) -> p c g", p=128),
                        in_=gabk[:].rearrange("p (c g) -> p c g", c=GCS)), 16)
                    wge(SC, sdve, sdve.n)
                    last = None
                    for cc in range(GCS):
                        last = SC.activation(out=gexp[:, cc * GC:(cc + 1) * GC],
                                             in_=gb_cum[:, cc * GC:(cc + 1) * GC],
                                             func=AF.Exp, bias=gnb[:, cc:cc + 1])
                    sact.inc(last)
                    wge(V, sact, sact.n)
                    KH = sdve.inc(V.tensor_tensor(out=gkh[:], in0=gk_s[:],
                                                  in1=gexp[:], op=ALU.mult))
                    wge(TN, sdve, KH)
                    for cc in range(GCS):
                        bank = (psA, psB)[cc % 2]
                        pe_acquire(bank)
                        tr = TN.matmul(bank[0:32, 0:128],
                                       gkh[:, cc * GC:(cc + 1) * GC],
                                       ident_bf[:], start=True, stop=True)
                        pe_done(bank, smm.inc(tr))
                        rd_acquire(SC, bank)
                        ev = SC.copy(kh_tm[:, cc * 128:(cc + 1) * 128],
                                     bank[0:32, 0:128])
                        rd_done(bank, sact, sact.inc(ev))
                    wge(SY, sact, sact.n)
                    wge(SY, sst, sst.n)
                    sst.inc(SY.dma_start(
                        out=ag3_abk.ap()[b, 2].rearrange("c (t v) -> t c v", t=GC),
                        in_=kh_tm[:].rearrange("t (c v) -> t c v", c=GCS)), 16)

                wge(GP, sst, sst.n)
                allgather(ag3_abk.ap(), ag3_abk_o.ap())
                allgather(ag3_lam.ap(), ag3_lam_o.ap())
                AG3_done = sag.n
                c5i.close()

            with ExitStack() as c6:
                E6 = c6.enter_context
                phase_fence()
                wge(GP, sw, sw.n)
                sw.inc(GP.dma_start(out=wog_bf.ap(), in_=wogT), 16)
                wge(GP, sw, sw.n)
                sw.inc(GP.dma_start(out=wig_bf.ap(), in_=wigT), 16)
                ch_ab = [E6(nc.sbuf_tensor(f"ch_ab{i}", [128, 2 * GCS * GC], BF16))
                         for i in range(2)]
                ch_kh = [E6(nc.sbuf_tensor(f"ch_kh{i}", [32, GCS * 128], BF16))
                         for i in range(2)]
                ch_v = [E6(nc.sbuf_tensor(f"ch_v{i}", [32, 4 * VH], BF16))
                        for i in range(2)]
                ch_lam = [E6(nc.sbuf_tensor(f"ch_lam{i}", [128, GCS], F32))
                          for i in range(2)]
                S_f32 = E6(nc.sbuf_tensor("S_f32", [128, VH], F32))
                S_bf = E6(nc.sbuf_tensor("S_bf", [128, VH], BF16))
                ptg_sb = [E6(nc.sbuf_tensor(f"ptg{i}", [32, 32], BF16))
                          for i in range(2)]
                oT_sb = E6(nc.sbuf_tensor("oT_sb", [128, 8 * 128], BF16))
                o2_st = E6(nc.sbuf_tensor("o2_st", [1, 128], F32))
                osq = E6(nc.sbuf_tensor("osq6", [128, 1024], BF16))

                wge(V, sag, AG3_done)
                sdve.inc(V.memset(S_f32[:], 0.0))
                wge(SC, sdve, sdve.n)
                sbf_cnt = sact.inc(SC.copy(S_bf[:], S_f32[:]))
                wge(SY, sag, AG3V_done)
                sld.inc(SY.dma_start(
                    out=v_scr.ap().unsqueeze(0),
                    in_=ag3_v_o.ap().rearrange(
                        "r b t d -> b r t d")[bch_slice, :, :, vh_cols]), 16)
                wge(SY, sld, sld.n)
                wge(SY, sag, AG3_done)
                sld.inc(SY.dma_start(
                    out=abk_scr.ap().unsqueeze(0),
                    in_=ag3_abk_o.ap().rearrange(
                        "r b k c m -> b r k c m")[bch_slice]), 16)
                wge(SY, sld, sld.n)
                sld.inc(SY.dma_start(
                    out=lam_scr.ap().unsqueeze(0),
                    in_=ag3_lam_o.ap().rearrange(
                        "r b p c -> b r p c")[bch_slice]), 16)
                wge(SY, sld, sld.n)

                ub = [(psE, psF), (psC, psD)]
                sec_used = [0, 0]
                v_used = [0, 0]
                ld_sec = [0, 0]
                ld_v = [0, 0]
                u_cnt = {}
                p_cnt = {}
                mk_cnt = {}
                OMM = smm.n
                SUPD = sdve.n

                def emit_loads(g):
                    r, cc = g >> 4, g & 15
                    blkk = g // 4
                    if cc == 0:
                        par = r % 2
                        wge(SY, smm, sec_used[par])
                        wge(SY, sld, sld.n)
                        sld.inc(SY.dma_start(
                            out=ch_ab[par][:, 0:GCS * GC].rearrange(
                                "p (c g2) -> p c g2", c=GCS),
                            in_=abk_scr.ap()[r, 0].rearrange(
                                "c (p g2) -> p c g2", p=128)), 16)
                        sld.inc(SY.dma_start(
                            out=ch_ab[par][:, GCS * GC:2 * GCS * GC].rearrange(
                                "p (c g2) -> p c g2", c=GCS),
                            in_=abk_scr.ap()[r, 1].rearrange(
                                "c (p g2) -> p c g2", p=128)), 16)
                        sld.inc(SY.dma_start(
                            out=ch_kh[par][:].rearrange("t (c v2) -> t c v2", c=GCS),
                            in_=abk_scr.ap()[r, 2].rearrange(
                                "c (t v2) -> t c v2", t=GC)), 16)
                        sld.inc(SY.dma_start(
                            out=ch_lam[par][:],
                            in_=lam_scr.ap()[r]), 16)
                        ld_sec[par] = sld.n
                    if g % 4 == 0:
                        r2, tloc = blkk >> 2, blkk & 3
                        bpar = blkk % 2
                        wge(SY, smm, v_used[bpar])
                        wge(SY, sld, sld.n)
                        sld.inc(SY.dma_start(
                            out=ch_v[bpar][:].rearrange("t (c v2) -> t c v2", c=4),
                            in_=v_scr.ap()[r2].rearrange(
                                "(b2 c t) v2 -> b2 t c v2", c=4, t=GC)[
                                tloc]), 16)
                        ld_v[bpar] = sld.n

                def emit_UP(g):
                    r, cc = g >> 4, g & 15
                    par = r % 2
                    bpar = (g // 4) % 2
                    kh_sl = ch_kh[par][:, cc * 128:(cc + 1) * 128]
                    v_sl = ch_v[bpar][:, (g % 4) * VH:(g % 4 + 1) * VH]
                    a_sl = ch_ab[par][:, cc * GC:(cc + 1) * GC]
                    b_sl = ch_ab[par][:, GCS * GC + cc * GC:
                                      GCS * GC + (cc + 1) * GC]
                    wge(TN, sld, max(ld_sec[par], ld_v[bpar]))
                    ua, ubk = ub[g % 2]
                    pe_acquire(ua, ubk)
                    TN.matmul(ua[:], kh_sl, v_sl[:, 0:512], start=True, stop=True)
                    u2 = TN.matmul(ubk[:], kh_sl, v_sl[:, 512:VH],
                                   start=True, stop=True)
                    u_cnt[g] = smm.inc(u2)
                    pe_done(ua, u_cnt[g])
                    pe_done(ubk, u_cnt[g])
                    pban = (psA, psB)[g % 2]
                    pe_acquire(pban)
                    pt = TN.matmul(pban[0:32, 0:32], b_sl, a_sl,
                                   start=True, stop=True)
                    p_cnt[g] = smm.inc(pt)
                    pe_done(pban, p_cnt[g])
                    rd_acquire(V, pban)
                    wge(V, smm, OMM)
                    mk = V.tensor_tensor(out=ptg_sb[g % 2][:],
                                         in0=pban[0:32, 0:32],
                                         in1=triu_bf[0:32, 0:32], op=ALU.mult)
                    mk_cnt[g] = sdve.inc(mk)
                    rd_done(pban, sdve, mk_cnt[g])

                for g in range(NGC):
                    if g == 0:
                        emit_loads(0)
                        emit_UP(0)
                    if g + 1 < NGC:
                        emit_loads(g + 1)
                        emit_UP(g + 1)
                    r, cc = g >> 4, g & 15
                    blkk = g // 4
                    par = r % 2
                    bpar = blkk % 2
                    a_sl = ch_ab[par][:, cc * GC:(cc + 1) * GC]
                    v_sl = ch_v[bpar][:, (g % 4) * VH:(g % 4 + 1) * VH]
                    wge(TN, sdve, mk_cnt[g])
                    wge(TN, sact, sbf_cnt)
                    pe_acquire(psG)
                    last = None
                    for vt in range(8):
                        TN.matmul(psG[:, vt * 32:(vt + 1) * 32],
                                  v_sl[:, vt * 128:(vt + 1) * 128],
                                  ptg_sb[g % 2][:], start=True, stop=False)
                        last = TN.matmul(psG[:, vt * 32:(vt + 1) * 32],
                                         S_bf[:, vt * 128:(vt + 1) * 128],
                                         a_sl, start=False, stop=True)
                    OMM = smm.inc(last)
                    pe_done(psG, OMM)
                    sec_used[par] = OMM
                    v_used[bpar] = OMM
                    ua, ubk = ub[g % 2]
                    rd_acquire(V, ua, ubk)
                    wge(V, sdve, SUPD)
                    wge(V, sact, sbf_cnt)
                    V.scalar_tensor_tensor(out=S_f32[:, 0:512],
                                           in0=S_f32[:, 0:512],
                                           scalar=ch_lam[par][:, cc:cc + 1],
                                           in1=ua[:],
                                           op0=ALU.mult, op1=ALU.add)
                    s2 = V.scalar_tensor_tensor(out=S_f32[:, 512:VH],
                                                in0=S_f32[:, 512:VH],
                                                scalar=ch_lam[par][:, cc:cc + 1],
                                                in1=ubk[:],
                                                op0=ALU.mult, op1=ALU.add)
                    SUPD = sdve.inc(s2)
                    rd_done(ua, sdve, SUPD)
                    rd_done(ubk, sdve, SUPD)
                    wge(SC, sdve, SUPD)
                    wge(SC, smm, OMM)
                    cpS = SC.copy(S_bf[:], S_f32[:])
                    sbf_cnt = sact.inc(cpS)
                    rd_acquire(SC, psG)
                    wge(SC, sst, sst.n)
                    ev = SC.copy(
                        oT_sb[:].rearrange("p (v2 t) -> p v2 t", v2=8)[
                            :, :, (g % 4) * 32:(g % 4) * 32 + 32],
                        psG[:].rearrange("p (v2 t) -> p v2 t", v2=8))
                    rd_done(psG, sact, sact.inc(ev))
                    if g % 4 == 3:
                        tb = blkk
                        wge(SC, sact, sact.n)
                        sq = SC.activation(out=osq[:], in_=oT_sb[:],
                                           func=AF.Square)
                        sact.inc(sq)
                        wge(TN, sact, sact.n)
                        pe_acquire(ps_row)
                        last = None
                        for vt in range(8):
                            last = TN.matmul(ps_row[0:1, 0:128], ones_col_bf[:],
                                             osq[:, vt * 128:(vt + 1) * 128],
                                             start=(vt == 0), stop=(vt == 7))
                        pe_done(ps_row, smm.inc(last))
                        rd_acquire(V, ps_row)
                        wge(V, sst, sst.n)
                        cpo = V.tensor_copy(out=o2_st[:], in_=ps_row[0:1, 0:128])
                        rd_done(ps_row, sdve, sdve.inc(cpo))
                        wge(SY, sdve, sdve.n)
                        wge(SY, sst, sst.n)
                        sst.inc(SY.dma_start(
                            out=ag4_o2.ap()[:, tb * 128:(tb + 1) * 128],
                            in_=o2_st[:]), 16)
                        wge(SY, sact, sact.n)
                        wge(SY, sst, sst.n)
                        sst.inc(SY.dma_start(
                            out=ag4_ot.ap()[:, :, tb * 128:(tb + 1) * 128]
                            .rearrange("v p t -> p v t"),
                            in_=oT_sb[:].rearrange("p (v2 t) -> p v2 t", v2=8)), 16)
                wge(GP, sst, sst.n)
                allgather(ag4_ot.ap(), ag4_ot_o.ap())
                allgather(ag4_o2.ap(), ag4_o2_o.ap())
                AG4_done = sag.n

            with ExitStack() as c7a:
                E7a = c7a.enter_context
                phase_fence()
                wogs = [E7a(nc.sbuf_tensor(f"wogs{i}", [128, DT * 256], BF16))
                        for i in range(2)]
                wigs = [E7a(nc.sbuf_tensor(f"wigs{i}", [128, DT * 256], BF16))
                        for i in range(2)]
                xlnt7 = E7a(nc.sbuf_tensor("xlnt7", [128, DT * SLAB], BF16))
                otp_all = [E7a(nc.sbuf_tensor(f"otp7_{t}", [128, D], BF16))
                           for t in range(TPB)]
                gw_bc = E7a(nc.sbuf_tensor("gw_bc", [128, D], BF16))
                ep_sil = [E7a(nc.sbuf_tensor(f"epsil{i}", [128, 256], BF16))
                          for i in range(2)]
                ep_t1c = E7a(nc.sbuf_tensor("ept1c", [128, 256], F32))
                rstd_c = E7a(nc.sbuf_tensor("rstdc", [128, 16], F32))
                ep_o2 = E7a(nc.sbuf_tensor("epo2", [128, 2], F32))

                for q4 in range(4):
                    wge(SY, smm, smm.n)
                    wge(SY, sld, sld.n)
                    sld.inc(SY.dma_start(
                        out=brow_buf[:],
                        in_=g_w.unsqueeze(0)[0:1, q4 * 512:(q4 + 1) * 512]), 16)
                    wge(TN, sld, sld.n)
                    bank = (psC, psD, psE, psF)[q4]
                    pe_acquire(bank)
                    gwm = TN.matmul(bank[:], ones_row_f32[:], brow_buf[:],
                                    start=True, stop=True)
                    pe_done(bank, smm.inc(gwm))
                    rd_acquire(SC, bank)
                    cpg = SC.copy(gw_bc[:, q4 * 512:(q4 + 1) * 512], bank[:])
                    rd_done(bank, sact, sact.inc(cpg))

                wge(SY, sag, AG4_done)
                wge(SY, sld, sld.n)
                sld.inc(SY.dma_start(
                    out=ep_ot_scr.ap(),
                    in_=ag4_ot_o.ap()[:, :, :, colslab]), 16)
                wge(SY, sld, sld.n)
                sld.inc(SY.dma_start(
                    out=ep_o2_scr.ap(),
                    in_=ag4_o2_o.ap()[:, :, colslab]), 16)
                wge(SY, sld, sld.n)
                o2_rows = ep_o2_scr.ap().rearrange("r a t -> (r a t)").unsqueeze(1)

                otp_used = smm.n
                xl_used = smm.n
                o2_used = sdve.n
                pan_used = [smm.n, smm.n]
                sil_used = [sdve.n, sdve.n]
                for b in range(B):
                    wge(SY, smm, xl_used)
                    wge(SY, sld, sld.n)
                    sld.inc(SY.dma_start(out=xlnt7[:], in_=xlnt_scr[b]), 16)
                    wge(SY, smm, otp_used)
                    for tl in range(TPB):
                        for vh2 in range(2):
                            rr = 2 * b + vh2
                            sld.inc(SY.dma_start(
                                out=otp_all[tl][:, vh2 * 1024:(vh2 + 1) * 1024]
                                .rearrange("p (v2 t) -> p v2 t", v2=8),
                                in_=ep_ot_scr.ap()[rr, :, :,
                                                   tl * 128:(tl + 1) * 128]
                                .rearrange("v p t -> p v t")), 16)
                    BLD = sld.n
                    for hp in range(8):
                        pbw = wogs[hp % 2]
                        pbi = wigs[hp % 2]
                        psem = (sw, sw2)[hp % 2]
                        wge(GP, smm, pan_used[hp % 2])
                        wge(GP, psem, psem.n)
                        psem.inc(GP.dma_start(
                            out=pbw[:].rearrange("p (i n) -> p i n", i=DT),
                            in_=wog_bf.ap()[hp]), 16)
                        psem.inc(GP.dma_start(
                            out=pbi[:].rearrange("p (i n) -> p i n", i=DT),
                            in_=wig_bf.ap()[hp]), 16)
                        PANLD = psem.n
                        wge(TN, psem, PANLD)
                        wge(TN, sld, BLD)
                        for tl in range(TPB):
                            idx = b * TPB + tl
                            par = (hp * TPB + tl) % 2
                            if hp == 0:
                                wge(SY, sdve, o2_used)
                                wge(SY, sld, sld.n)
                                for vh2 in range(2):
                                    rr = 2 * b + vh2
                                    sld.inc(SY.dma_start(
                                        out=ep_o2[:, vh2:vh2 + 1],
                                        in_=o2_rows[rr * 512 + tl * 128:
                                                    rr * 512 + (tl + 1) * 128,
                                                    :]), 16)
                                O2LD = sld.n
                                wge(V, sld, O2LD)
                                rd = V.tensor_reduce(
                                    out=rstd_c[:, idx:idx + 1], in_=ep_o2[:],
                                    axis=mybir.AxisListType.X, op=ALU.add)
                                o2_used = sdve.inc(rd)
                                wge(V, sdve, sdve.n)
                                sdve.inc(V.tensor_scalar(
                                    out=rstd_c[:, idx:idx + 1],
                                    in0=rstd_c[:, idx:idx + 1],
                                    scalar1=1.0 / D, scalar2=None, op0=ALU.mult))
                                wge(SC, sdve, sdve.n)
                                sact.inc(SC.activation(
                                    out=rstd_c[:, idx:idx + 1],
                                    in_=rstd_c[:, idx:idx + 1],
                                    func=AF.Ln, bias=EPS))
                                wge(SC, sact, sact.n)
                                sact.inc(SC.activation(
                                    out=rstd_c[:, idx:idx + 1],
                                    in_=rstd_c[:, idx:idx + 1],
                                    func=AF.Exp, scale=-0.5))
                            bank = (psC, psD)[tl % 2]
                            pe_acquire(bank)
                            last = None
                            for i in range(DT):
                                last = TN.matmul(
                                    bank[:, 0:256],
                                    xlnt7[:, i * 512 + tl * 128:
                                          i * 512 + (tl + 1) * 128],
                                    pbi[:, i * 256:(i + 1) * 256],
                                    start=(i == 0), stop=False)
                            for i in range(DT):
                                last = TN.matmul(
                                    bank[:, 0:256],
                                    otp_all[tl][:, i * 128:(i + 1) * 128],
                                    pbw[:, i * 256:(i + 1) * 256],
                                    start=False, stop=(i == DT - 1))
                            pe_done(bank, smm.inc(last))
                            rd_acquire(SC, bank)
                            wge(SC, sdve, sil_used[par])
                            sl = SC.activation(out=ep_sil[par][:], in_=bank[:, 0:256],
                                               func=AF.Silu)
                            rd_done(bank, sact, sact.inc(sl))
                            tb_bank = (psA, psB)[par]
                            pe_acquire(tb_bank)
                            tlast = None
                            for u in range(2):
                                i = hp * 2 + u
                                tlast = TN.matmul(
                                    tb_bank[:, u * 128:(u + 1) * 128],
                                    otp_all[tl][:, i * 128:(i + 1) * 128],
                                    ident_bf[:], start=True, stop=True)
                            pe_done(tb_bank, smm.inc(tlast))
                            rd_acquire(V, tb_bank)
                            wge(V, sact, sact.n)
                            wge(V, sdve, sdve.n)
                            t1 = V.scalar_tensor_tensor(
                                out=ep_t1c[:], in0=tb_bank[:, 0:256],
                                scalar=rstd_c[:, idx:idx + 1],
                                in1=gw_bc[:, hp * 256:(hp + 1) * 256],
                                op0=ALU.mult, op1=ALU.mult)
                            rd_done(tb_bank, sdve, sdve.inc(t1))
                            wge(V, sdve, sdve.n)
                            t2 = V.tensor_tensor(out=ep_t1c[:], in0=ep_t1c[:],
                                                 in1=ep_sil[par][:], op=ALU.mult)
                            sdve.inc(t2)
                            wge(V, sdve, sdve.n)
                            ad = V.tensor_tensor(
                                out=xr[b][tl][:, hp * 256:(hp + 1) * 256],
                                in0=ep_t1c[:],
                                in1=xr[b][tl][:, hp * 256:(hp + 1) * 256],
                                op=ALU.add)
                            sil_used[par] = sdve.inc(ad)
                        pan_used[hp % 2] = smm.n
                    otp_used = smm.n
                    xl_used = smm.n

        if n_sl >= 4:
            emit_ffn("b", final=True)
        else:
            wge(SY, sdve, sdve.n)
            wge(SY, sact, sact.n)
            wge(SY, smm, smm.n)
            for b in range(B):
                for t in range(TPB):
                    sout.inc(SY.dma_start(
                        out=y_out[b, t * 128:(t + 1) * 128, :],
                        in_=xr[b][t][:]), 16)

        wge(SY, sout, sout.n)

    return nc


def make_in_maps(inputs):
    x = np.asarray(inputs["L0_attn_a_x"])
    tri = np.triu(np.ones((128, 128), np.float32))
    ident = np.eye(128, dtype=np.float32)
    base = {
        "w_qkv": np.ascontiguousarray(
            np.asarray(inputs["L0_attn_a_W_qkv"]).reshape(
                16, 128, 24, 256).transpose(2, 1, 0, 3)),
        "b_qkv": np.asarray(inputs["L0_attn_a_b_qkv"]),
        "b_qkv_col": np.ascontiguousarray(
            np.asarray(inputs["L0_attn_a_b_qkv"]).reshape(48, 128).T),
        "w_proj": np.ascontiguousarray(
            np.asarray(inputs["L0_attn_a_W_proj"]).reshape(
                16, 128, 8, 256).transpose(2, 1, 0, 3)),
        "b_proj": np.asarray(inputs["L0_attn_a_b_proj"]),
        "w_baseT": np.ascontiguousarray(
            np.asarray(inputs["L0_ffn_W_base"]).T.reshape(
                16, 128, 8, 256).transpose(2, 1, 0, 3)),
        "b_base": np.asarray(inputs["L0_ffn_b_base"]),
        "routerT": np.ascontiguousarray(
            np.asarray(inputs["L0_ffn_router_W"]).T.reshape(
                16, 128, 8).transpose(1, 0, 2)),
        "a_T": np.ascontiguousarray(
            np.asarray(inputs["L0_ffn_A"]).reshape(128, D).T.reshape(
                16, 128, 128).transpose(1, 0, 2)),
        "bw_flat": np.ascontiguousarray(
            np.asarray(inputs["L0_ffn_Bw"]).transpose(0, 2, 1).reshape(128, D)),
        "wqT": np.ascontiguousarray(
            np.asarray(inputs["L0_attn_b_Wq"]).T.reshape(
                16, 128, 128).transpose(1, 0, 2)),
        "wkT": np.ascontiguousarray(
            np.asarray(inputs["L0_attn_b_Wk"]).T.reshape(
                16, 128, 128).transpose(1, 0, 2)),
        "wogT": np.ascontiguousarray(
            np.asarray(inputs["L0_attn_b_Wog"]).T.reshape(
                16, 128, 8, 256).transpose(2, 1, 0, 3)),
        "wigT": np.ascontiguousarray(
            np.asarray(inputs["L0_attn_b_Wig"]).T.reshape(
                16, 128, 8, 256).transpose(2, 1, 0, 3)),
        "g_w": np.asarray(inputs["L0_attn_b_g_weight"]),
        "triu_in": tri,
        "ident_in": ident,
    }
    in_maps = []
    for c in range(NR):
        m = dict(base)
        m["x_my"] = np.ascontiguousarray(x[:, c * SLAB:(c + 1) * SLAB, :])
        in_maps.append(m)
    return in_maps


def kernel(**inputs):
    nc = build(N_SL)
    in_maps = make_in_maps(inputs)
    res = run_bass_kernel_spmd(nc, in_maps, core_ids=list(range(NR)))
    y = np.empty((B, T, D), np.float32)
    for c in range(NR):
        y[:, c * SLAB:(c + 1) * SLAB, :] = np.asarray(res.results[c]["y_my"])
    return y


if __name__ == "__main__":
    nc = build(1)
    print("built OK; instructions:",
          sum(len(b.instructions) for b in nc.m.functions[0].blocks))


# version 19
# speedup vs baseline: 1.0090x; 1.0003x over previous
import math
import numpy as np
import ml_dtypes
import concourse.bass as bass
import concourse.mybir as mybir
from concourse.bass_utils import run_bass_kernel_spmd
from contextlib import ExitStack

F32 = mybir.dt.float32
BF16 = mybir.dt.bfloat16
AF = mybir.ActivationFunctionType
ALU = mybir.AluOpType

B, T, D, H, HD = 4, 4096, 2048, 16, 128
NR = 8
SLAB = T // NR
TPB = SLAB // 128
DT = D // 128
GC = 32
GCS = SLAB // GC
NGC = T // GC
VH = D // 2
SCALE = HD ** -0.5
LNSC = math.log(SCALE)
SHIFT = -12.0
EPS = 1e-5
N_SL = 4


class Ctr:
    def __init__(self, ctx, nc, name):
        self.h = ctx.enter_context(nc.semaphore(name))
        self.n = 0

    def inc(self, instr, k=1):
        instr.then_inc(self.h, k)
        self.n += k
        return self.n


def build(n_sl=N_SL):
    nc = bass.Bass("TRN2", target_bir_lowering=False, debug=False)

    x_in = nc.dram_tensor("x_my", [B, SLAB, D], F32, kind="ExternalInput").ap()
    w_qkv = nc.dram_tensor("w_qkv", [24, 128, DT, 256], F32,
                           kind="ExternalInput").ap()
    b_qkv = nc.dram_tensor("b_qkv", [3 * D], F32, kind="ExternalInput").ap()
    b_qkv_col = nc.dram_tensor("b_qkv_col", [128, 48], F32, kind="ExternalInput").ap()
    w_proj = nc.dram_tensor("w_proj", [8, 128, DT, 256], F32,
                            kind="ExternalInput").ap()
    b_proj = nc.dram_tensor("b_proj", [D], F32, kind="ExternalInput").ap()
    w_baseT = nc.dram_tensor("w_baseT", [8, 128, DT, 256], F32,
                             kind="ExternalInput").ap()
    b_base = nc.dram_tensor("b_base", [D], F32, kind="ExternalInput").ap()
    routerT = nc.dram_tensor("routerT", [128, DT, 8], F32,
                             kind="ExternalInput").ap()
    a_T = nc.dram_tensor("a_T", [128, DT, 128], F32,
                         kind="ExternalInput").ap()
    bw_flat = nc.dram_tensor("bw_flat", [128, D], F32, kind="ExternalInput").ap()
    wqT = nc.dram_tensor("wqT", [128, DT, HD], F32,
                         kind="ExternalInput").ap()
    wkT = nc.dram_tensor("wkT", [128, DT, HD], F32,
                         kind="ExternalInput").ap()
    wogT = nc.dram_tensor("wogT", [8, 128, DT, 256], F32,
                          kind="ExternalInput").ap()
    wigT = nc.dram_tensor("wigT", [8, 128, DT, 256], F32,
                          kind="ExternalInput").ap()
    g_w = nc.dram_tensor("g_w", [D], F32, kind="ExternalInput").ap()
    triu_in = nc.dram_tensor("triu_in", [128, 128], F32, kind="ExternalInput").ap()
    ident_in = nc.dram_tensor("ident_in", [128, 128], F32, kind="ExternalInput").ap()
    y_out = nc.dram_tensor("y_my", [B, SLAB, D], F32, kind="ExternalOutput").ap()

    SEC = 128 * 512
    ag1_in = [nc.dram_tensor(f"ag1_in{b}", [H, 3, SEC], BF16) for b in range(B)]
    ag1_out = [nc.dram_tensor(f"ag1_out{b}", [NR, H, 3, SEC], BF16,
                              addr_space="Shared") for b in range(B)]
    ag2_in = [[nc.dram_tensor(f"ag2_in{b}_{j}", [1, 128, T], BF16)
               for j in range(2)] for b in range(B)]
    ag2_out = [[nc.dram_tensor(f"ag2_out{b}_{j}", [NR, 1, 128, T], BF16,
                               addr_space="Shared") for j in range(2)]
               for b in range(B)]
    ag2_out_unused = [nc.dram_tensor(f"ag2_ou{b}", [NR, 2, 128, T], BF16,
                              addr_space="Shared") for b in range(B)]
    pair_scr = [nc.dram_tensor(f"pair_scr{i}", [NR, 3, SEC], BF16)
                for i in range(2)]
    proj_scr = nc.dram_tensor("proj_scr", [B, NR, 2, 128, 512], BF16)
    ag3_abk = nc.dram_tensor("ag3_abk", [B, 3, GCS, 128 * GC], BF16)
    ag3_abk_o = nc.dram_tensor("ag3_abk_o", [NR, B, 3, GCS, 128 * GC], BF16,
                               addr_space="Shared")
    ag3_lam = nc.dram_tensor("ag3_lam", [B, 128, GCS], F32)
    ag3_lam_o = nc.dram_tensor("ag3_lam_o", [NR, B, 128, GCS], F32,
                               addr_space="Shared")
    ag3_v = nc.dram_tensor("ag3_v", [B, SLAB, D], BF16)
    ag3_v_o = nc.dram_tensor("ag3_v_o", [NR, B, SLAB, D], BF16,
                             addr_space="Shared")
    ag4_ot = nc.dram_tensor("ag4_ot", [8, 128, T], BF16)
    ag4_ot_o = nc.dram_tensor("ag4_ot_o", [NR, 8, 128, T], BF16,
                              addr_space="Shared")
    ag4_o2 = nc.dram_tensor("ag4_o2", [1, T], F32)
    ag4_o2_o = nc.dram_tensor("ag4_o2_o", [NR, 1, T], F32, addr_space="Shared")
    go_scr = nc.dram_tensor("go_scr", [B, SLAB, D], F32).ap()
    wog_bf = nc.dram_tensor("wog_bf", [8, 128, DT, 256], BF16)
    wig_bf = nc.dram_tensor("wig_bf", [8, 128, DT, 256], BF16)
    abk_scr = nc.dram_tensor("abk_scr", [NR, 3, GCS, 128 * GC], BF16)
    lam_scr = nc.dram_tensor("lam_scr", [NR, 128, GCS], F32)
    v_scr = nc.dram_tensor("v_scr", [NR, SLAB, VH], BF16)
    ep_ot_scr = nc.dram_tensor("ep_ot_scr", [NR, 8, 128, 512], BF16)
    ep_o2_scr = nc.dram_tensor("ep_o2_scr", [NR, 1, 512], F32)
    xlnt_scr = nc.dram_tensor("xlnt_scr", [B, 128, DT * SLAB], BF16).ap()

    with ExitStack() as ctx:
        E = ctx.enter_context

        xr = [[E(nc.sbuf_tensor(f"xr{b}_{t}", [128, D], F32)) for t in range(TPB)]
              for b in range(B)]
        triu_bf = E(nc.sbuf_tensor("triu_bf", [128, 128], BF16))
        ident_bf = E(nc.sbuf_tensor("ident_bf", [128, 128], BF16))
        ones_col_bf = E(nc.sbuf_tensor("ones_col_bf", [128, 1], BF16))
        ones_row_f32 = E(nc.sbuf_tensor("ones_row_f32", [1, 128], F32))
        consts = E(nc.sbuf_tensor("consts", [128, 16], F32))
        bias_qkv_col = E(nc.sbuf_tensor("bias_qkv_col", [128, 48], F32))
        brow_buf = E(nc.sbuf_tensor("brow_buf", [1, 512], F32))
        brow_bf = E(nc.sbuf_tensor("brow_bf", [1, 512], BF16))
        ones_row_bf = E(nc.sbuf_tensor("ones_row_bf", [1, 128], BF16))
        ln_sum = E(nc.sbuf_tensor("ln_sum", [128, 4], F32))
        ln_sq = E(nc.sbuf_tensor("ln_sq", [128, 4], F32))
        ln_m = E(nc.sbuf_tensor("ln_m", [128, 4], F32))
        ln_v = E(nc.sbuf_tensor("ln_v", [128, 4], F32))
        LN_BUFS = {}
        stage_bf = [E(nc.sbuf_tensor(f"stgbf{i}", [128, 512], BF16)) for i in range(2)]

        psA = E(nc.psum_tensor("psA", [128, 512], F32))
        psB = E(nc.psum_tensor("psB", [128, 512], F32))
        psC = E(nc.psum_tensor("psC", [128, 512], F32))
        psD = E(nc.psum_tensor("psD", [128, 512], F32))
        psE = E(nc.psum_tensor("psE", [128, 512], F32))
        psF = E(nc.psum_tensor("psF", [128, 512], F32))
        psG = E(nc.psum_tensor("psG", [128, 256], F32))
        ps_row = E(nc.psum_tensor("ps_row", [1, 512], F32))

        sx = Ctr(ctx, nc, "sx")
        sw = Ctr(ctx, nc, "sw")
        sld = Ctr(ctx, nc, "sld")
        sst = Ctr(ctx, nc, "sst")
        sag = Ctr(ctx, nc, "sag")
        smm = Ctr(ctx, nc, "smm")
        sact = Ctr(ctx, nc, "sact")
        sdve = Ctr(ctx, nc, "sdve")
        sout = Ctr(ctx, nc, "sout")
        sw2 = Ctr(ctx, nc, "sw2")
        st1s = [Ctr(ctx, nc, f"st1_{i}") for i in range(4)]
        sldv = Ctr(ctx, nc, "sldv")
        sw3 = Ctr(ctx, nc, "sw3")
        spre = Ctr(ctx, nc, "spre")

        blk = E(nc.Block())

        CONST_VALUES = sorted({SHIFT, SCALE, LNSC, -1.0, -0.5, 2.0, 1.0 / D,
                               EPS, 0.5, 1.0, 1.0 / HD})
        for i, val in enumerate(CONST_VALUES):
            nc.const_aps.register(F32, float(val), consts[:, i:i + 1])

        pid = nc.partition_id()
        V, SC, TN, SY, GP = nc.vector, nc.scalar, nc.tensor, nc.sync, nc.gpsimd
        LAST_WAIT = {}

        def wge(eng, ctr, count):
            key = (id(eng), id(ctr))
            if LAST_WAIT.get(key, -1) >= count:
                return
            eng.wait_ge(ctr.h, count)
            LAST_WAIT[key] = count
        pid_s = nc.snap(pid)
        h_slice = [bass.ts(nc.snap(pid * 2 + j), 1) for j in range(2)]
        bch_slice = bass.ts(nc.snap(pid // 2), 1)
        vh_s = nc.snap(pid % 2)
        vh_cols = bass.ts(vh_s, VH)
        colslab = bass.ts(pid_s, 512)
        slab_slice = bass.ts(pid_s, 1)

        for b in range(B):
            for t in range(TPB):
                sx.inc(SY.dma_start(out=xr[b][t][:],
                                    in_=x_in[b, t * 128:(t + 1) * 128, :]), 16)
        sx.inc(SY.dma_start(out=bias_qkv_col[:], in_=b_qkv_col), 16)
        wge(GP, sw, sw.n)
        sw.inc(GP.dma_start(out=triu_bf[:], in_=triu_in), 16)
        wge(GP, sw, sw.n)
        sw.inc(GP.dma_start(out=ident_bf[:], in_=ident_in), 16)
        for i, val in enumerate(CONST_VALUES):
            V.memset(consts[:, i:i + 1], float(val))
        V.memset(ones_col_bf[:], 1.0)
        V.memset(ones_row_bf[:], 1.0)
        sdve.inc(V.memset(ones_row_f32[:], 1.0))

        for eng in (V, SC, TN, SY, GP):
            wge(eng, sx, sx.n)
            wge(eng, sw, sw.n)
            wge(eng, sdve, sdve.n)

        BANK_WR = {}
        BANK_RD = {}

        def pe_acquire(*banks):
            for bank in banks:
                c = BANK_RD.get(bank.name)
                if c is not None:
                    wge(TN, c[0], c[1])

        def pe_done(bank, cnt):
            BANK_WR[bank.name] = cnt

        def rd_acquire(eng, *banks):
            for bank in banks:
                c = BANK_WR.get(bank.name)
                if c is not None:
                    wge(eng, smm, c)

        def rd_done(bank, ctr, cnt):
            BANK_RD[bank.name] = (ctr, cnt)

        def emit_ln_slab(b, xln_cons=None, sst_cons=None):
            wge(SC, sdve, sdve.n)
            wge(V, sst, sst.n if sst_cons is None else sst_cons)
            wge(V, smm, smm.n if xln_cons is None else xln_cons)
            ln_sqs, xln_buf = LN_BUFS["sqs"], LN_BUFS["xln"]
            for t in range(TPB):
                i1 = V.tensor_scalar(out=xln_buf[t][:], in0=xr[b][t][:], scalar1=0.0,
                                     scalar2=0.0, op0=ALU.add, op1=ALU.add,
                                     accum_out=ln_sum[:, t:t + 1])
                sdve.inc(i1)
                if t < TPB - 1:
                    wge(V, sdve, sdve.n)
            for t in range(TPB):
                i2 = SC.activation(out=ln_sqs[:], in_=xr[b][t][:], func=AF.Square,
                                   accum_out=ln_sq[:, t:t + 1])
                sact.inc(i2)
                if t < TPB - 1:
                    wge(SC, sact, sact.n)
            wge(V, sdve, sdve.n)
            wge(V, sact, sact.n)
            sdve.inc(V.tensor_scalar(out=ln_m[:], in0=ln_sum[:], scalar1=1.0 / D,
                                     scalar2=None, op0=ALU.mult))
            wge(V, sdve, sdve.n)
            sdve.inc(V.scalar_tensor_tensor(out=ln_v[:], in0=ln_m[:], scalar=-1.0,
                                            in1=ln_m[:], op0=ALU.mult,
                                            op1=ALU.mult))
            wge(V, sdve, sdve.n)
            sdve.inc(V.scalar_tensor_tensor(out=ln_v[:], in0=ln_sq[:],
                                            scalar=1.0 / D, in1=ln_v[:],
                                            op0=ALU.mult, op1=ALU.add))
            wge(SC, sdve, sdve.n)
            sact.inc(SC.activation(out=ln_v[:], in_=ln_v[:], func=AF.Ln, bias=EPS))
            wge(SC, sact, sact.n)
            sact.inc(SC.activation(out=ln_v[:], in_=ln_v[:], func=AF.Exp,
                                   scale=-0.5))
            wge(V, sact, sact.n)
            outs = []
            for t in range(TPB):
                i7 = V.tensor_scalar(out=xln_buf[t][:], in0=xr[b][t][:],
                                     scalar1=ln_m[:, t:t + 1],
                                     scalar2=ln_v[:, t:t + 1],
                                     op0=ALU.subtract, op1=ALU.mult)
                outs.append(sdve.inc(i7))
            return outs

        def transpose_tile_into(xlnt_sb, t, xi, xln_ready, xlnt_guard=None,
                                ready_ctr=None, sst_guard=None):
            xln_buf = LN_BUFS["xln"]
            wge(TN, ready_ctr if ready_ctr is not None else sdve, xln_ready)
            for grp in range(4):
                bank = (psA, psB)[grp % 2]
                pe_acquire(bank)
                last = None
                for u in range(4):
                    i = grp * 4 + u
                    last = TN.matmul(bank[:, u * 128:(u + 1) * 128],
                                     xln_buf[xi][:, i * 128:(i + 1) * 128],
                                     ident_bf[:], start=True, stop=True)
                pe_done(bank, smm.inc(last))
                rd_acquire(SC, bank)
                if xlnt_guard is not None:
                    wge(SC, smm, xlnt_guard)
                if sst_guard is not None:
                    wge(SC, sst, sst_guard)
                dst = xlnt_sb[:].rearrange("p (i n) -> p i n", i=DT)[
                    :, grp * 4:(grp + 1) * 4, t * 128:(t + 1) * 128]
                ev = SC.copy(dst, bank[:].rearrange("p (u n) -> p u n", u=4))
                rd_done(bank, sact, sact.inc(ev))

        def ln_alloc(EE, tag):

            LN_BUFS["sqs"] = EE(nc.sbuf_tensor(f"lnsqs{tag}", [128, D], BF16))
            LN_BUFS["xln"] = [EE(nc.sbuf_tensor(f"xln{tag}{i}", [128, D], BF16))
                              for i in range(4)]

        def emit_ln_transpose_slab(b, xlnt_sb, store_scr=False, v_dst=None):
            rs = emit_ln_slab(b)
            for t in range(TPB):
                transpose_tile_into(xlnt_sb, t, t, rs[t])
                if v_dst is not None:
                    wge(SY, sdve, rs[t])
                    wge(SY, sst, sst.n)
                    sst.inc(SY.dma_start(
                        out=v_dst[b, t * 128:(t + 1) * 128, :],
                        in_=LN_BUFS["xln"][t][:]), 16)
            if store_scr:
                wge(SY, sact, sact.n)
                wge(SY, sst, sst.n)
                sst.inc(SY.dma_start(out=xlnt_scr[b], in_=xlnt_sb[:]), 16)

        def load_panel(buf_sb, src_cols_ap, ncols):
            nt = DT
            return GP.dma_start(
                out=buf_sb[:].rearrange("p (i n) -> p i n", i=nt),
                in_=src_cols_ap.rearrange("(i p) n -> p i n", p=128))

        def allgather(in_ap, out_ap):
            return sag.inc(GP.collective_compute(
                "AllGather", ALU.bypass, replica_groups=[list(range(NR))],
                ins=[in_ap.opt()], outs=[out_ap.opt()]))

        def phase_fence():
            for eng in (V, SC, TN, SY, GP):
                for c in (sld, sst, smm, sact, sdve, sw, sw2, sw3, spre,
                          sldv, st1s[0], st1s[1], st1s[2], st1s[3]):
                    wge(eng, c, c.n)

        if n_sl >= 1:
            with ExitStack() as c1:
                E1 = c1.enter_context
                xlnt2 = [E1(nc.sbuf_tensor(f"xlnt1_{i}", [128, DT * SLAB], BF16))
                         for i in range(2)]
                ln_alloc(E1, "p1")
                wpan = [E1(nc.sbuf_tensor(f"wpan1_{i}", [128, DT * 256], BF16))
                        for i in range(2)]
                stg1 = [E1(nc.sbuf_tensor(f"stg1_{i}", [128, 512], BF16))
                        for i in range(4)]
                AG1_done = []
                panel_used = [0, 0]
                stage_st = [0, 0, 0, 0]
                stg_i = [0]
                brow_rd = 0
                QEND = 0
                p1_sst0 = sst.n
                prev_trans = 0
                for pr in range(2):
                    for bb in range(2):
                        b = 2 * pr + bb
                        rs = emit_ln_slab(b, xln_cons=prev_trans,
                                          sst_cons=p1_sst0)
                        for t in range(TPB):
                            transpose_tile_into(xlnt2[bb], t, t, rs[t],
                                                xlnt_guard=QEND if pr else None)
                        prev_trans = smm.n
                    trans_done = sact.n
                    for pan in range(24):
                        pbuf = wpan[pan % 2]
                        psem = (sw, sw2)[pan % 2]
                        GP.wait_ge(smm.h, panel_used[pan % 2])
                        wge(GP, psem, psem.n)
                        psem.inc(GP.dma_start(
                            out=pbuf[:].rearrange("p (i n) -> p i n", i=DT),
                            in_=w_qkv[pan]), 16)
                        PANLD = psem.n
                        if pan >= 16:
                            pv = pan - 16
                            wge(GP, sw3, sw3.n)
                            wge(GP, smm, brow_rd)
                            sw3.inc(GP.dma_start(
                                out=brow_bf[0:1, 0:256],
                                in_=b_qkv.unsqueeze(0)[
                                    0:1, 2 * D + pv * 256:
                                    2 * D + (pv + 1) * 256]), 16)
                            BROWLD = sw3.n
                        wge(TN, psem, PANLD)
                        if pan == 0:
                            wge(TN, sact, trans_done)
                        if pan < 16:
                            for su in range(2):
                                ot = pan * 2 + su
                                for bb in range(2):
                                    b = 2 * pr + bb
                                    bank = (psC, psD, psE, psF)[
                                        (su % 2) * 2 + bb]
                                    pe_acquire(bank)
                                    last = None
                                    for i in range(DT):
                                        last = TN.matmul(
                                            bank[:],
                                            pbuf[:, i * 256 + su * 128:
                                                 i * 256 + (su + 1) * 128],
                                            xlnt2[bb][:, i * 512:(i + 1) * 512],
                                            start=(i == 0), stop=(i == DT - 1))
                                    pe_done(bank, smm.inc(last))
                                    rd_acquire(SC, bank)
                                    sl = stg_i[0] % 4
                                    stg_i[0] += 1
                                    wge(SC, st1s[sl], stage_st[sl])
                                    st = stg1[sl]
                                    ev = SC.activation(
                                        out=st[:], in_=bank[:],
                                        func=AF.Identity,
                                        bias=bias_qkv_col[:, ot:ot + 1])
                                    rd_done(bank, sact, sact.inc(ev))
                                    wge(SY, sact, sact.n)
                                    hh, kind = (ot, 0) if ot < 16 else (ot - 16, 1)
                                    dst = ag1_in[b].ap()[hh, kind].rearrange(
                                        "(p n) -> p n", n=512)
                                    stage_st[sl] = st1s[sl].inc(
                                        SY.dma_start(out=dst, in_=st[:]), 16)
                        else:
                            pv = pan - 16
                            wge(TN, sw3, BROWLD)
                            for t in range(TPB):
                                for bb in range(2):
                                    b = 2 * pr + bb
                                    bank = (psC, psD, psE, psF)[
                                        (t % 2) * 2 + bb]
                                    pe_acquire(bank)
                                    last = None
                                    for i in range(DT):
                                        last = TN.matmul(
                                            bank[:, 0:256],
                                            xlnt2[bb][:, i * 512 + t * 128:
                                                      i * 512 + (t + 1) * 128],
                                            pbuf[:, i * 256:(i + 1) * 256],
                                            start=(i == 0), stop=False)
                                    last = TN.matmul(bank[:, 0:256],
                                                     ones_row_bf[:],
                                                     brow_bf[0:1, 0:256],
                                                     start=False, stop=True)
                                    pe_done(bank, smm.inc(last))
                                    brow_rd = smm.n
                                    rd_acquire(SC, bank)
                                    sl = stg_i[0] % 4
                                    stg_i[0] += 1
                                    wge(SC, st1s[sl], stage_st[sl])
                                    st = stg1[sl]
                                    ev = SC.copy(st[:, 0:256], bank[:, 0:256])
                                    rd_done(bank, sact, sact.inc(ev))
                                    wge(SY, sact, sact.n)
                                    dst = ag1_in[b].ap()[
                                        2 * pv:2 * pv + 2, 2].rearrange(
                                        "h (t2 v) -> t2 h v", v=128)[
                                        t * 128:(t + 1) * 128, :, :]
                                    srcv = st[:, 0:256].rearrange(
                                        "p (h v) -> p h v", h=2)
                                    stage_st[sl] = st1s[sl].inc(
                                        SY.dma_start(out=dst, in_=srcv), 16)
                        panel_used[pan % 2] = smm.n
                    QEND = smm.n
                    if pr == 1:
                        for j in range(2):
                            wge(GP, sag, AG1_done[0])
                            if j:
                                wge(GP, spre, 16)
                            gsrc0 = ag1_out[0].ap().rearrange(
                                "r h k m -> h r k m")[h_slice[j]]
                            spre.inc(GP.dma_start(
                                out=pair_scr[j].ap().unsqueeze(0),
                                in_=gsrc0), 16)
                    for bb in range(2):
                        b = 2 * pr + bb
                        wge(GP, sst, sst.n)
                        for k in range(4):
                            wge(GP, st1s[k], st1s[k].n)
                        allgather(ag1_in[b].ap(), ag1_out[b].ap())
                        AG1_done.append(sag.n)


            with ExitStack() as c2:
                E2 = c2.enter_context
                phase_fence()
                kt_sb2 = [E2(nc.sbuf_tensor(f"kt_sb{i}", [128, T], BF16))
                          for i in range(2)]
                v_sb2 = [E2(nc.sbuf_tensor(f"v_sb{i}", [128, T], BF16))
                         for i in range(2)]
                qt_sb2 = [E2(nc.sbuf_tensor(f"qt_sb{i}", [128, T], BF16))
                          for i in range(2)]
                pt_sb = [E2(nc.sbuf_tensor(f"pt{i}", [128, 512], BF16))
                         for i in range(4)]
                attn_d = E2(nc.sbuf_tensor("attn_d", [1, 512], F32))
                attn_db = E2(nc.sbuf_tensor("attn_db", [128, 512], F32))
                wpan3 = [E2(nc.sbuf_tensor(f"wpan3_{i}", [128, DT * 256], BF16))
                         for i in range(2)]

                sbanks = [psA, psB, psE, psF]
                obank = [psC, psD]
                kv_used = [smm.n, smm.n]
                kv_ld = [0, 0]
                kv_ld_qk = [0, 0]
                pt_pv = [0, 0, 0, 0]
                stage_o = [sst.n, sst.n]
                pend_bc = [None]
                pend_ml = [0]

                def emit_pair_loads(p):
                    b, j = p >> 1, p & 1
                    wge(SY, smm, kv_used[p % 2])
                    if p >= 2:
                        wge(SY, sag, AG1_done[b])
                        wge(SY, sld, sld.n)
                        gsrc = ag1_out[b].ap().rearrange(
                            "r h k m -> h r k m")[h_slice[j]]
                        sld.inc(SY.dma_start(
                            out=pair_scr[p % 2].ap().unsqueeze(0),
                            in_=gsrc), 16)
                        wge(SY, sld, sld.n)
                    else:
                        wge(SY, sld, sld.n)
                        wge(SY, spre, 16 * (p + 1))
                    rows_pn = pair_scr[p % 2].ap().rearrange(
                        "r k (p n) -> (r k p) n", n=512)
                    rows_v = pair_scr[p % 2].ap().rearrange(
                        "r k (j2 p v) -> (r k j2 p) v", j2=4, p=128)
                    for r in range(NR):
                        base = r * 3
                        sld.inc(SY.dma_start(
                            out=qt_sb2[p % 2][:, r * 512:(r + 1) * 512],
                            in_=rows_pn[(base + 0) * 128:(base + 1) * 128, :]), 16)
                        sld.inc(SY.dma_start(
                            out=kt_sb2[p % 2][:, r * 512:(r + 1) * 512],
                            in_=rows_pn[(base + 1) * 128:(base + 2) * 128, :]), 16)
                    kv_ld_qk[p % 2] = sld.n
                    wge(SY, sldv, sldv.n)
                    for r in range(NR):
                        base = r * 3
                        for j2l in range(4):
                            rr = (base + 2) * 4 + j2l
                            sldv.inc(SY.dma_start(
                                out=v_sb2[p % 2][:, (4 * r + j2l) * 128:
                                                 (4 * r + j2l + 1) * 128],
                                in_=rows_v[rr * 128:(rr + 1) * 128, :]), 16)
                    kv_ld[p % 2] = sldv.n

                AG2_done = {}
                po_used = [smm.n]
                p3_pan_used = [smm.n, smm.n]
                p3_brow_rd = [smm.n]

                def emit_proj(b, par):
                    wge(SY, sag, AG2_done[b])
                    wge(SY, sld, sld.n)
                    for jj2 in range(2):
                        sld.inc(SY.dma_start(
                            out=proj_scr.ap()[b, :, jj2],
                            in_=ag2_out[b][jj2].ap()[:, 0, :, colslab]), 16)
                    wge(SY, sld, sld.n)
                    wge(SY, smm, kv_used[par])
                    wge(SY, smm, po_used[0])
                    for h in range(H):
                        r, jj = h >> 1, h & 1
                        buf = kt_sb2[par] if h < 8 else v_sb2[par]
                        hh = h % 8
                        sld.inc(SY.dma_start(
                            out=buf[:, hh * 512:(hh + 1) * 512],
                            in_=proj_scr.ap()[b, r, jj]), 16)
                    po_ld = sld.n
                    for hp in range(8):
                        pbuf = wpan3[hp % 2]
                        psem = (sw, sw2)[hp % 2]
                        wge(GP, smm, p3_pan_used[hp % 2])
                        wge(GP, psem, psem.n)
                        psem.inc(GP.dma_start(
                            out=pbuf[:].rearrange("p (i n) -> p i n", i=DT),
                            in_=w_proj[hp]), 16)
                        PANLD = psem.n
                        wge(GP, sw3, sw3.n)
                        wge(GP, smm, p3_brow_rd[0])
                        sw3.inc(GP.dma_start(
                            out=brow_bf[0:1, 0:256],
                            in_=b_proj.unsqueeze(0)[
                                0:1, hp * 256:(hp + 1) * 256]), 16)
                        BROWLD = sw3.n
                        wge(TN, psem, PANLD)
                        wge(TN, sw3, BROWLD)
                        wge(TN, sld, po_ld)
                        for tl in range(TPB):
                            pe_acquire(psG)
                            for h in range(H):
                                buf = kt_sb2[par] if h < 8 else v_sb2[par]
                                hh = h % 8
                                TN.matmul(
                                    psG[:, 0:256],
                                    buf[:, hh * 512 + tl * 128:
                                        hh * 512 + (tl + 1) * 128],
                                    pbuf[:, h * 256:(h + 1) * 256],
                                    start=(h == 0), stop=False)
                            last = TN.matmul(psG[:, 0:256], ones_row_bf[:],
                                             brow_bf[0:1, 0:256],
                                             start=False, stop=True)
                            pe_done(psG, smm.inc(last))
                            rd_acquire(V, psG)
                            ad = V.tensor_tensor(
                                out=xr[b][tl][:, hp * 256:(hp + 1) * 256],
                                in0=psG[:, 0:256],
                                in1=xr[b][tl][:, hp * 256:(hp + 1) * 256],
                                op=ALU.add)
                            rd_done(psG, sdve, sdve.inc(ad))
                        p3_pan_used[hp % 2] = smm.n
                        p3_brow_rd[0] = smm.n
                    po_used[0] = smm.n
                    kv_used[par] = smm.n

                emit_pair_loads(0)
                for p in range(8):
                    b, j = p >> 1, p & 1
                    if p + 1 < 8:
                        emit_pair_loads(p + 1)
                    kt_sb, v_sb, qt_sb = kt_sb2[p % 2], v_sb2[p % 2], qt_sb2[p % 2]
                    wge(TN, sld, kv_ld_qk[p % 2])
                    pend = None
                    for qc in range(8):
                        nk = 4 * qc + 4
                        ob = obank[qc % 2]
                        exp_cnt = {}
                        msk_cnt = {}
                        s_cnt = {}

                        def emit_S(j2):
                            m = j2 - 4 * qc
                            coloff = 0 if m < 0 else m * 128
                            sb = sbanks[j2 % 4]
                            pe_acquire(sb)
                            inst = TN.matmul(
                                sb[:, coloff:512],
                                kt_sb[:, j2 * 128:(j2 + 1) * 128],
                                qt_sb[:, qc * 512 + coloff:(qc + 1) * 512],
                                start=True, stop=True)
                            s_cnt[j2] = smm.inc(inst)
                            pe_done(sb, s_cnt[j2])

                        def emit_exp(j2):
                            m = j2 - 4 * qc
                            coloff = 0 if m < 0 else m * 128
                            sb = sbanks[j2 % 4]
                            wge(SC, smm, s_cnt[j2])
                            wge(SC, smm, pt_pv[j2 % 4])
                            ex = SC.activation(out=pt_sb[j2 % 4][:, coloff:512],
                                               in_=sb[:, coloff:512],
                                               func=AF.Exp, scale=SCALE, bias=SHIFT)
                            cnt = sact.inc(ex)
                            rd_done(sb, sact, cnt)
                            exp_cnt[j2] = cnt
                            if m >= 0:
                                wge(V, sact, cnt)
                                mk = V.tensor_tensor(
                                    out=pt_sb[j2 % 4][:, coloff:coloff + 128],
                                    in0=pt_sb[j2 % 4][:, coloff:coloff + 128],
                                    in1=triu_bf[:], op=ALU.mult)
                                msk_cnt[j2] = sdve.inc(mk)

                        def emit_PV(j2):
                            m = j2 - 4 * qc
                            coloff = 0 if m < 0 else m * 128
                            if j2 == 0:
                                wge(TN, sldv, kv_ld[p % 2])
                            wge(TN, sact, exp_cnt[j2])
                            if m >= 0:
                                wge(TN, sdve, msk_cnt[j2])
                            if j2 == 0:
                                pe_acquire(ob)
                                pe_acquire(ps_row)
                            TN.matmul(ob[:, coloff:512],
                                      v_sb[:, j2 * 128:(j2 + 1) * 128],
                                      pt_sb[j2 % 4][:, coloff:512],
                                      start=(j2 == 0), stop=(j2 == nk - 1))
                            dnm = TN.matmul(ps_row[0:1, coloff:512],
                                            ones_col_bf[:],
                                            pt_sb[j2 % 4][:, coloff:512],
                                            start=(j2 == 0), stop=(j2 == nk - 1))
                            pt_pv[j2 % 4] = smm.inc(dnm)

                        def emit_epilogue_front(st8):
                            qc0, ob0 = st8
                            rd_acquire(V, ps_row)
                            if pend_bc[0] is not None:
                                wge(V, smm, pend_bc[0])
                            rc = V.reciprocal(out=attn_d[:], in_=ps_row[0:1, :])
                            rd_done(ps_row, sdve, sdve.inc(rc))
                            return sdve.n

                        def emit_epilogue_tail(st8, rc_cnt):
                            qc0, ob0 = st8
                            bb = sbanks[(4 * qc0 + 4 + 2) % 4]
                            pe_acquire(bb)
                            wge(TN, sdve, rc_cnt)
                            bc = TN.matmul(bb[:], ones_row_f32[:], attn_d[:],
                                           start=True, stop=True)
                            pend_bc[0] = smm.inc(bc)
                            pe_done(bb, pend_bc[0])
                            rd_acquire(SC, bb)
                            wge(SC, sdve, pend_ml[0])
                            cp = SC.copy(attn_db[:], bb[:])
                            rd_done(bb, sact, sact.inc(cp))
                            wge(V, sact, sact.n)
                            wge(V, sst, stage_o[qc0 % 2])
                            rd_acquire(V, ob0)
                            ml = V.tensor_tensor(out=stage_bf[qc0 % 2][:],
                                                 in0=ob0[:], in1=attn_db[:],
                                                 op=ALU.mult)
                            OEP = sdve.inc(ml)
                            pend_ml[0] = OEP
                            rd_done(ob0, sdve, OEP)
                            wge(SY, sdve, OEP)
                            wge(SY, sst, sst.n)
                            stage_o[qc0 % 2] = sst.inc(SY.dma_start(
                                out=ag2_in[b][j].ap()[0, :,
                                                      qc0 * 512:(qc0 + 1) * 512],
                                in_=stage_bf[qc0 % 2][:]), 16)

                        emit_S(0)
                        emit_exp(0)
                        if nk > 1:
                            emit_S(1)
                            emit_exp(1)
                        for j2 in range(nk):
                            if j2 + 2 < nk:
                                emit_S(j2 + 2)
                                emit_exp(j2 + 2)
                            emit_PV(j2)
                        if pend is not None:
                            emit_epilogue_tail(*pend)
                            pend = None
                        pe_done(ob, smm.n)
                        pe_done(ps_row, smm.n)
                        rcc = emit_epilogue_front((qc, ob))
                        pend = ((qc, ob), rcc)
                    emit_epilogue_tail(*pend)
                    pend = None
                    kv_used[p % 2] = smm.n
                    if j == 0 and b >= 1:
                        emit_proj(b - 1, 0)
                    wge(GP, sst, sst.n)
                    allgather(ag2_in[b][j].ap(), ag2_out[b][j].ap())
                    if j == 1:
                        AG2_done[b] = sag.n
                emit_proj(3, 1)

        def emit_ffn(tag, final=False):
            with ExitStack() as cf:
                EF = cf.enter_context
                phase_fence()
                xlnt = EF(nc.sbuf_tensor(f"xlntF{tag}", [128, DT * SLAB], BF16))
                at_sb = EF(nc.sbuf_tensor(f"at{tag}", [128, DT * 128], BF16))
                rt_sb = EF(nc.sbuf_tensor(f"rt{tag}", [128, DT * 8], BF16))
                hgt_all = EF(nc.sbuf_tensor(f"hgt{tag}", [128, 16 * 128], BF16))
                r8 = EF(nc.sbuf_tensor(f"r8{tag}", [128, 8], F32))
                rmax = EF(nc.sbuf_tensor(f"rmax{tag}", [128, 1], F32))
                roh = EF(nc.sbuf_tensor(f"roh{tag}", [128, 8], F32))
                hg_sb = EF(nc.sbuf_tensor(f"hg{tag}", [128, 128], BF16))
                bwf_pre = EF(nc.sbuf_tensor(f"bwfp{tag}", [128, D], BF16))
                cfl = EF(ExitStack())
                ln_alloc(cfl.enter_context, "f" + tag)

                wge(GP, smm, smm.n)
                wge(GP, sw, sw.n)
                sw.inc(GP.dma_start(
                    out=at_sb[:].rearrange("p (i n) -> p i n", i=DT),
                    in_=a_T), 16)
                wge(GP, sw, sw.n)
                sw.inc(GP.dma_start(
                    out=rt_sb[:].rearrange("p (i n) -> p i n", i=DT),
                    in_=routerT), 16)
                wge(GP, sw, sw.n)
                sw.inc(GP.dma_start(out=bwf_pre[:], in_=bw_flat), 16)
                for b in range(B):
                    emit_ln_transpose_slab(b, xlnt, store_scr=True)
                    wge(TN, sw, sw.n)
                    wge(TN, sact, sact.n)
                    for tl in range(TPB):
                        pe_acquire(psG)
                        last = None
                        for i in range(DT):
                            last = TN.matmul(
                                psG[:, 0:8],
                                xlnt[:, i * 512 + tl * 128:i * 512 + (tl + 1) * 128],
                                rt_sb[:, i * 8:(i + 1) * 8],
                                start=(i == 0), stop=(i == DT - 1))
                        pe_done(psG, smm.inc(last))
                        rd_acquire(V, psG)
                        cpr = V.tensor_copy(out=r8[:], in_=psG[:, 0:8])
                        rd_done(psG, sdve, sdve.inc(cpr))
                        wge(V, sdve, sdve.n)
                        sdve.inc(V.tensor_reduce(out=rmax[:], in_=r8[:],
                                                 axis=mybir.AxisListType.X,
                                                 op=ALU.max))
                        wge(V, sdve, sdve.n)
                        sdve.inc(V.tensor_scalar(out=roh[:], in0=r8[:],
                                                 scalar1=rmax[:], scalar2=2.0,
                                                 op0=ALU.is_equal, op1=ALU.mult))
                        pe_acquire(psA)
                        last = None
                        for i in range(DT):
                            last = TN.matmul(
                                psA[:, 0:128],
                                xlnt[:, i * 512 + tl * 128:i * 512 + (tl + 1) * 128],
                                at_sb[:, i * 128:(i + 1) * 128],
                                start=(i == 0), stop=(i == DT - 1))
                        pe_done(psA, smm.inc(last))
                        rd_acquire(V, psA)
                        wge(V, sdve, sdve.n)
                        gt = V.scalar_tensor_tensor(
                            out=hg_sb[:].rearrange("p (a c2) -> p a c2", a=8),
                            in0=psA[:, 0:128].rearrange("p (a c2) -> p a c2", a=8),
                            scalar=1.0,
                            in1=roh[:].unsqueeze(2).broadcast_to([128, 8, 16]),
                            op0=ALU.mult, op1=ALU.mult)
                        gcnt = sdve.inc(gt)
                        rd_done(psA, sdve, gcnt)
                        wge(TN, sdve, gcnt)
                        pe_acquire(psB)
                        tr = TN.matmul(psB[:, 0:128], hg_sb[:], ident_bf[:],
                                       start=True, stop=True)
                        pe_done(psB, smm.inc(tr))
                        rd_acquire(SC, psB)
                        ev = SC.copy(hgt_all[:, (b * TPB + tl) * 128:
                                             (b * TPB + tl + 1) * 128],
                                     psB[:, 0:128])
                        rd_done(psB, sact, sact.inc(ev))
                cfl.close()
                with ExitStack() as cf2:
                    EF2 = cf2.enter_context
                    phase_fence()
                    whs = [EF2(nc.sbuf_tensor(f"wh{tag}{i}", [128, DT * 256],
                                              BF16)) for i in range(2)]
                    xl2 = [xlnt,
                           EF2(nc.sbuf_tensor(f"xlnt2{tag}",
                                              [128, DT * SLAB], BF16))]
                    bwf_sb = bwf_pre
                    pan_used = [smm.n, smm.n]
                    xl_used = [smm.n, smm.n]
                    xl_ld = [0, 0]
                    brow_rd = smm.n
                    for k in range(2):
                        wge(SY, sld, sld.n)
                        wge(SY, sst, sst.n)
                        sld.inc(SY.dma_start(out=xl2[k][:], in_=xlnt_scr[k]), 16)
                        xl_ld[k] = sld.n
                    seq = [(hp2, b2) for hp2 in range(8) for b2 in range(B)]
                    pbuf = None
                    for si, (hp, b) in enumerate(seq):
                        k = b % 2
                        if b == 0:
                            pbuf = whs[hp % 2]
                            psem = (sw, sw2)[hp % 2]
                            wge(GP, smm, pan_used[hp % 2])
                            wge(GP, psem, psem.n)
                            psem.inc(GP.dma_start(
                                out=pbuf[:].rearrange("p (i n) -> p i n", i=DT),
                                in_=w_baseT[hp]), 16)
                            PANLD = psem.n
                            wge(GP, sw3, sw3.n)
                            wge(GP, smm, brow_rd)
                            sw3.inc(GP.dma_start(
                                out=brow_bf[0:1, 0:256],
                                in_=b_base.unsqueeze(0)[
                                    0:1, hp * 256:(hp + 1) * 256]), 16)
                            BROWLD = sw3.n
                            wge(TN, psem, PANLD)
                            wge(TN, sw3, BROWLD)
                        wge(TN, sld, xl_ld[k])
                        for tl in range(TPB):
                            bank = (psC, psD, psE, psF)[tl % 4]
                            pe_acquire(bank)
                            for i in range(DT):
                                TN.matmul(
                                    bank[:, 0:256],
                                    xl2[k][:, i * 512 + tl * 128:
                                           i * 512 + (tl + 1) * 128],
                                    pbuf[:, i * 256:(i + 1) * 256],
                                    start=(i == 0), stop=False)
                            TN.matmul(bank[:, 0:256],
                                      hgt_all[:, (b * TPB + tl) * 128:
                                              (b * TPB + tl + 1) * 128],
                                      bwf_sb[:, hp * 256:(hp + 1) * 256],
                                      start=False, stop=False)
                            last = TN.matmul(bank[:, 0:256], ones_row_bf[:],
                                             brow_bf[0:1, 0:256],
                                             start=False, stop=True)
                            pe_done(bank, smm.inc(last))
                            rd_acquire(V, bank)
                            ad = V.tensor_tensor(
                                out=xr[b][tl][:, hp * 256:(hp + 1) * 256],
                                in0=bank[:, 0:256],
                                in1=xr[b][tl][:, hp * 256:(hp + 1) * 256],
                                op=ALU.add)
                            rd_done(bank, sdve, sdve.inc(ad))
                            if final and hp == 7:
                                wge(SY, sdve, sdve.n)
                                sout.inc(SY.dma_start(
                                    out=y_out[b, tl * 128:(tl + 1) * 128, :],
                                    in_=xr[b][tl][:]), 16)
                        xl_used[k] = smm.n
                        if b == B - 1:
                            pan_used[hp % 2] = smm.n
                            brow_rd = smm.n
                        if si + 2 < len(seq):
                            nb = seq[si + 2][1]
                            wge(SY, smm, xl_used[k])
                            wge(SY, sld, sld.n)
                            sld.inc(SY.dma_start(out=xl2[nb % 2][:],
                                                 in_=xlnt_scr[nb]), 16)
                            xl_ld[nb % 2] = sld.n

        if n_sl >= 2:
            emit_ffn("a")

        if n_sl >= 3:
            vh_expr = pid % 2
            bch_expr = pid // 2
            with ExitStack() as c5:
                E5 = c5.enter_context
                phase_fence()
                xlnt = E5(nc.sbuf_tensor("xlnt5", [128, DT * SLAB], BF16))
                ln16_s = E5(nc.sbuf_tensor("ln16s", [128, 16], F32))
                ln16_q = E5(nc.sbuf_tensor("ln16q", [128, 16], F32))
                ln16_m = E5(nc.sbuf_tensor("ln16m", [128, 16], F32))
                ln16_v = E5(nc.sbuf_tensor("ln16v", [128, 16], F32))
                c5i = E5(ExitStack())
                E5i = c5i.enter_context
                ln_alloc(E5i, "p5")
                wq_sb = E5i(nc.sbuf_tensor("wq_sb", [128, DT * 128], BF16))
                wk_sb = E5i(nc.sbuf_tensor("wk_sb", [128, DT * 128], BF16))
                gq_s = E5i(nc.sbuf_tensor("gq_s", [128, 512], F32))
                gk_s = E5i(nc.sbuf_tensor("gk_s", [128, 512], F32))
                gsp = E5i(nc.sbuf_tensor("gsp", [128, 512], F32))
                gb_cum = E5i(nc.sbuf_tensor("gb_cum", [128, 512], F32))
                gexp = E5i(nc.sbuf_tensor("gexp", [128, 512], F32))
                gnb = E5i(nc.sbuf_tensor("gnb", [128, GCS], F32))
                glam = E5i(nc.sbuf_tensor("glam", [128, GCS], F32))
                gabk = E5i(nc.sbuf_tensor("gabk", [128, 512], BF16))
                gkh = E5i(nc.sbuf_tensor("gkh", [128, 512], BF16))
                kh_tm = E5i(nc.sbuf_tensor("kh_tm", [32, GCS * 128], BF16))
                zeros32 = E5i(nc.sbuf_tensor("zeros32", [128, 32], F32))
                sdve.inc(V.memset(zeros32[:], 0.0))

                wge(GP, smm, smm.n)
                wge(GP, sw, sw.n)
                sw.inc(GP.dma_start(
                    out=wq_sb[:].rearrange("p (i n) -> p i n", i=DT),
                    in_=wqT), 16)
                wge(GP, sw, sw.n)
                sw.inc(GP.dma_start(
                    out=wk_sb[:].rearrange("p (i n) -> p i n", i=DT),
                    in_=wkT), 16)

                xln_buf5 = LN_BUFS["xln"]
                wge(SC, sdve, sdve.n)
                wge(V, sst, sst.n)
                wge(V, smm, smm.n)
                for b in range(B):
                    for t in range(TPB):
                        cix = b * TPB + t
                        sdve.inc(V.tensor_scalar(
                            out=xln_buf5[t][:], in0=xr[b][t][:],
                            scalar1=0.0, scalar2=0.0,
                            op0=ALU.add, op1=ALU.add,
                            accum_out=ln16_s[:, cix:cix + 1]))
                        if cix < 15:
                            wge(V, sdve, sdve.n)
                for b in range(B):
                    for t in range(TPB):
                        cix = b * TPB + t
                        sact.inc(SC.activation(
                            out=LN_BUFS["sqs"][:], in_=xr[b][t][:],
                            func=AF.Square,
                            accum_out=ln16_q[:, cix:cix + 1]))
                        if cix < 15:
                            wge(SC, sact, sact.n)
                wge(V, sdve, sdve.n)
                wge(V, sact, sact.n)
                sdve.inc(V.tensor_scalar(out=ln16_m[:], in0=ln16_s[:],
                                         scalar1=1.0 / D, scalar2=None,
                                         op0=ALU.mult))
                wge(V, sdve, sdve.n)
                sdve.inc(V.scalar_tensor_tensor(out=ln16_v[:], in0=ln16_m[:],
                                                scalar=-1.0, in1=ln16_m[:],
                                                op0=ALU.mult, op1=ALU.mult))
                wge(V, sdve, sdve.n)
                sdve.inc(V.scalar_tensor_tensor(out=ln16_v[:], in0=ln16_q[:],
                                                scalar=1.0 / D, in1=ln16_v[:],
                                                op0=ALU.mult, op1=ALU.add))
                wge(SC, sdve, sdve.n)
                sact.inc(SC.activation(out=ln16_v[:], in_=ln16_v[:],
                                       func=AF.Ln, bias=EPS))
                wge(SC, sact, sact.n)
                sact.inc(SC.activation(out=ln16_v[:], in_=ln16_v[:],
                                       func=AF.Exp, scale=-0.5))
                wge(V, sact, sact.n)
                vst5 = [0, 0, 0, 0]
                for b in range(B):
                    for t in range(TPB):
                        cix = b * TPB + t
                        wge(V, sst, vst5[t])
                        AP7 = sdve.inc(V.tensor_scalar(
                            out=xln_buf5[t][:], in0=xr[b][t][:],
                            scalar1=ln16_m[:, cix:cix + 1],
                            scalar2=ln16_v[:, cix:cix + 1],
                            op0=ALU.subtract, op1=ALU.mult))
                        wge(SY, sdve, AP7)
                        wge(SY, sst, sst.n)
                        vst5[t] = sst.inc(SY.dma_start(
                            out=ag3_v.ap()[b, t * 128:(t + 1) * 128, :],
                            in_=xln_buf5[t][:]), 16)
                wge(GP, sst, sst.n)
                allgather(ag3_v.ap(), ag3_v_o.ap())
                AG3V_done = sag.n
                TRANS_done = 0
                for b in range(B):
                    wge(GP, smm, TRANS_done)
                    wge(GP, sw, sw.n)
                    for t in range(TPB):
                        sw.inc(GP.dma_start(
                            out=xln_buf5[t][:],
                            in_=ag3_v.ap()[b, t * 128:(t + 1) * 128, :]), 16)
                    XLN_LD = sw.n
                    ST_prev = sst.n
                    for t in range(TPB):
                        transpose_tile_into(xlnt, t, t, XLN_LD,
                                            ready_ctr=sw, sst_guard=ST_prev)
                    TRANS_done = smm.n
                    wge(SY, sact, sact.n)
                    wge(SY, sst, sst.n)
                    sst.inc(SY.dma_start(out=xlnt_scr[b], in_=xlnt[:]), 16)
                    wge(TN, sw, sw.n)
                    wge(TN, sact, sact.n)
                    last = None
                    for i in range(DT):
                        last = TN.matmul(psC[:],
                                         wq_sb[:, i * 128:(i + 1) * 128],
                                         xlnt[:, i * 512:(i + 1) * 512],
                                         start=(i == 0), stop=(i == DT - 1))
                    smm.inc(last)
                    last = None
                    for i in range(DT):
                        last = TN.matmul(psD[:],
                                         wk_sb[:, i * 128:(i + 1) * 128],
                                         xlnt[:, i * 512:(i + 1) * 512],
                                         start=(i == 0), stop=(i == DT - 1))
                    smm.inc(last)
                    wge(SC, smm, smm.n)
                    sact.inc(SC.activation(out=gq_s[:], in_=psC[:], func=AF.Silu))
                    sact.inc(SC.activation(out=gk_s[:], in_=psD[:], func=AF.Sigmoid))
                    wge(SC, sact, sact.n)
                    sact.inc(SC.activation(out=gsp[:], in_=gk_s[:], func=AF.Exp))
                    wge(SC, sact, sact.n)
                    sact.inc(SC.activation(out=gsp[:], in_=gsp[:], func=AF.Ln,
                                           bias=1.0))
                    wge(V, sact, sact.n)
                    last = None
                    for cc in range(GCS):
                        last = V.tensor_tensor_scan(
                            out=gb_cum[:, cc * GC:(cc + 1) * GC],
                            data0=gsp[:, cc * GC:(cc + 1) * GC], data1=zeros32[:],
                            initial=0.0, op0=ALU.add, op1=ALU.add)
                    CUM = sdve.inc(last)
                    wge(SC, sdve, CUM)
                    sact.inc(SC.activation(out=glam[:], in_=gb_cum[:, GC - 1::GC],
                                           func=AF.Exp, scale=-1.0))
                    wge(V, sdve, CUM)
                    sdve.inc(V.tensor_scalar(out=gnb[:], in0=gb_cum[:, GC - 1::GC],
                                             scalar1=-1.0, scalar2=None,
                                             op0=ALU.mult))
                    wge(SY, sact, sact.n)
                    wge(SY, sst, sst.n)
                    sst.inc(SY.dma_start(out=ag3_lam.ap()[b], in_=glam[:]), 16)
                    sact.inc(SC.activation(out=gexp[:], in_=gb_cum[:], func=AF.Exp,
                                           scale=-1.0, bias=LNSC))
                    wge(V, sact, sact.n)
                    wge(V, sst, sst.n)
                    AW = sdve.inc(V.tensor_tensor(out=gabk[:], in0=gq_s[:],
                                                  in1=gexp[:], op=ALU.mult))
                    wge(SY, sdve, AW)
                    wge(SY, sst, sst.n)
                    sst.inc(SY.dma_start(
                        out=ag3_abk.ap()[b, 0].rearrange("c (p g) -> p c g", p=128),
                        in_=gabk[:].rearrange("p (c g) -> p c g", c=GCS)), 16)
                    wge(SC, sst, sst.n)
                    wge(SC, sdve, sdve.n)
                    sact.inc(SC.activation(out=gexp[:], in_=gb_cum[:], func=AF.Exp))
                    wge(V, sact, sact.n)
                    wge(V, sst, sst.n)
                    BW = sdve.inc(V.tensor_tensor(out=gabk[:], in0=gk_s[:],
                                                  in1=gexp[:], op=ALU.mult))
                    wge(SY, sdve, BW)
                    wge(SY, sst, sst.n)
                    sst.inc(SY.dma_start(
                        out=ag3_abk.ap()[b, 1].rearrange("c (p g) -> p c g", p=128),
                        in_=gabk[:].rearrange("p (c g) -> p c g", c=GCS)), 16)
                    wge(SC, sdve, sdve.n)
                    last = None
                    for cc in range(GCS):
                        last = SC.activation(out=gexp[:, cc * GC:(cc + 1) * GC],
                                             in_=gb_cum[:, cc * GC:(cc + 1) * GC],
                                             func=AF.Exp, bias=gnb[:, cc:cc + 1])
                    sact.inc(last)
                    wge(V, sact, sact.n)
                    KH = sdve.inc(V.tensor_tensor(out=gkh[:], in0=gk_s[:],
                                                  in1=gexp[:], op=ALU.mult))
                    wge(TN, sdve, KH)
                    for cc in range(GCS):
                        bank = (psA, psB)[cc % 2]
                        pe_acquire(bank)
                        tr = TN.matmul(bank[0:32, 0:128],
                                       gkh[:, cc * GC:(cc + 1) * GC],
                                       ident_bf[:], start=True, stop=True)
                        pe_done(bank, smm.inc(tr))
                        rd_acquire(SC, bank)
                        ev = SC.copy(kh_tm[:, cc * 128:(cc + 1) * 128],
                                     bank[0:32, 0:128])
                        rd_done(bank, sact, sact.inc(ev))
                    wge(SY, sact, sact.n)
                    wge(SY, sst, sst.n)
                    sst.inc(SY.dma_start(
                        out=ag3_abk.ap()[b, 2].rearrange("c (t v) -> t c v", t=GC),
                        in_=kh_tm[:].rearrange("t (c v) -> t c v", c=GCS)), 16)

                wge(GP, sst, sst.n)
                allgather(ag3_abk.ap(), ag3_abk_o.ap())
                allgather(ag3_lam.ap(), ag3_lam_o.ap())
                AG3_done = sag.n
                c5i.close()

            with ExitStack() as c6:
                E6 = c6.enter_context
                phase_fence()
                wge(GP, sw, sw.n)
                sw.inc(GP.dma_start(out=wog_bf.ap(), in_=wogT), 16)
                wge(GP, sw, sw.n)
                sw.inc(GP.dma_start(out=wig_bf.ap(), in_=wigT), 16)
                ch_ab = [E6(nc.sbuf_tensor(f"ch_ab{i}", [128, 2 * GCS * GC], BF16))
                         for i in range(2)]
                ch_kh = [E6(nc.sbuf_tensor(f"ch_kh{i}", [32, GCS * 128], BF16))
                         for i in range(2)]
                ch_v = [E6(nc.sbuf_tensor(f"ch_v{i}", [32, 4 * VH], BF16))
                        for i in range(2)]
                ch_lam = [E6(nc.sbuf_tensor(f"ch_lam{i}", [128, GCS], F32))
                          for i in range(2)]
                S_f32 = E6(nc.sbuf_tensor("S_f32", [128, VH], F32))
                S_bf = E6(nc.sbuf_tensor("S_bf", [128, VH], BF16))
                ptg_sb = [E6(nc.sbuf_tensor(f"ptg{i}", [32, 32], BF16))
                          for i in range(2)]
                oT_sb = E6(nc.sbuf_tensor("oT_sb", [128, 8 * 128], BF16))
                o2_st = E6(nc.sbuf_tensor("o2_st", [1, 128], F32))
                osq = E6(nc.sbuf_tensor("osq6", [128, 1024], BF16))

                wge(V, sag, AG3_done)
                sdve.inc(V.memset(S_f32[:], 0.0))
                wge(SC, sdve, sdve.n)
                sbf_cnt = sact.inc(SC.copy(S_bf[:], S_f32[:]))
                wge(SY, sag, AG3V_done)
                sld.inc(SY.dma_start(
                    out=v_scr.ap().unsqueeze(0),
                    in_=ag3_v_o.ap().rearrange(
                        "r b t d -> b r t d")[bch_slice, :, :, vh_cols]), 16)
                wge(SY, sld, sld.n)
                wge(SY, sag, AG3_done)
                sld.inc(SY.dma_start(
                    out=abk_scr.ap().unsqueeze(0),
                    in_=ag3_abk_o.ap().rearrange(
                        "r b k c m -> b r k c m")[bch_slice]), 16)
                wge(SY, sld, sld.n)
                sld.inc(SY.dma_start(
                    out=lam_scr.ap().unsqueeze(0),
                    in_=ag3_lam_o.ap().rearrange(
                        "r b p c -> b r p c")[bch_slice]), 16)
                wge(SY, sld, sld.n)

                ub = [(psE, psF), (psC, psD)]
                sec_used = [0, 0]
                v_used = [0, 0]
                ld_sec = [0, 0]
                ld_v = [0, 0]
                u_cnt = {}
                p_cnt = {}
                mk_cnt = {}
                OMM = smm.n
                SUPD = sdve.n

                def emit_loads(g):
                    r, cc = g >> 4, g & 15
                    blkk = g // 4
                    if cc == 0:
                        par = r % 2
                        wge(SY, smm, sec_used[par])
                        wge(SY, sld, sld.n)
                        sld.inc(SY.dma_start(
                            out=ch_ab[par][:, 0:GCS * GC].rearrange(
                                "p (c g2) -> p c g2", c=GCS),
                            in_=abk_scr.ap()[r, 0].rearrange(
                                "c (p g2) -> p c g2", p=128)), 16)
                        sld.inc(SY.dma_start(
                            out=ch_ab[par][:, GCS * GC:2 * GCS * GC].rearrange(
                                "p (c g2) -> p c g2", c=GCS),
                            in_=abk_scr.ap()[r, 1].rearrange(
                                "c (p g2) -> p c g2", p=128)), 16)
                        sld.inc(SY.dma_start(
                            out=ch_kh[par][:].rearrange("t (c v2) -> t c v2", c=GCS),
                            in_=abk_scr.ap()[r, 2].rearrange(
                                "c (t v2) -> t c v2", t=GC)), 16)
                        sld.inc(SY.dma_start(
                            out=ch_lam[par][:],
                            in_=lam_scr.ap()[r]), 16)
                        ld_sec[par] = sld.n
                    if g % 4 == 0:
                        r2, tloc = blkk >> 2, blkk & 3
                        bpar = blkk % 2
                        wge(SY, smm, v_used[bpar])
                        wge(SY, sld, sld.n)
                        sld.inc(SY.dma_start(
                            out=ch_v[bpar][:].rearrange("t (c v2) -> t c v2", c=4),
                            in_=v_scr.ap()[r2].rearrange(
                                "(b2 c t) v2 -> b2 t c v2", c=4, t=GC)[
                                tloc]), 16)
                        ld_v[bpar] = sld.n

                def emit_UP(g):
                    r, cc = g >> 4, g & 15
                    par = r % 2
                    bpar = (g // 4) % 2
                    kh_sl = ch_kh[par][:, cc * 128:(cc + 1) * 128]
                    v_sl = ch_v[bpar][:, (g % 4) * VH:(g % 4 + 1) * VH]
                    a_sl = ch_ab[par][:, cc * GC:(cc + 1) * GC]
                    b_sl = ch_ab[par][:, GCS * GC + cc * GC:
                                      GCS * GC + (cc + 1) * GC]
                    wge(TN, sld, max(ld_sec[par], ld_v[bpar]))
                    ua, ubk = ub[g % 2]
                    pe_acquire(ua, ubk)
                    TN.matmul(ua[:], kh_sl, v_sl[:, 0:512], start=True, stop=True)
                    u2 = TN.matmul(ubk[:], kh_sl, v_sl[:, 512:VH],
                                   start=True, stop=True)
                    u_cnt[g] = smm.inc(u2)
                    pe_done(ua, u_cnt[g])
                    pe_done(ubk, u_cnt[g])
                    pban = (psA, psB)[g % 2]
                    pe_acquire(pban)
                    pt = TN.matmul(pban[0:32, 0:32], b_sl, a_sl,
                                   start=True, stop=True)
                    p_cnt[g] = smm.inc(pt)
                    pe_done(pban, p_cnt[g])
                    rd_acquire(V, pban)
                    wge(V, smm, OMM)
                    mk = V.tensor_tensor(out=ptg_sb[g % 2][:],
                                         in0=pban[0:32, 0:32],
                                         in1=triu_bf[0:32, 0:32], op=ALU.mult)
                    mk_cnt[g] = sdve.inc(mk)
                    rd_done(pban, sdve, mk_cnt[g])

                for g in range(NGC):
                    if g == 0:
                        emit_loads(0)
                        emit_UP(0)
                    if g + 1 < NGC:
                        emit_loads(g + 1)
                        emit_UP(g + 1)
                    r, cc = g >> 4, g & 15
                    blkk = g // 4
                    par = r % 2
                    bpar = blkk % 2
                    a_sl = ch_ab[par][:, cc * GC:(cc + 1) * GC]
                    v_sl = ch_v[bpar][:, (g % 4) * VH:(g % 4 + 1) * VH]
                    wge(TN, sdve, mk_cnt[g])
                    wge(TN, sact, sbf_cnt)
                    pe_acquire(psG)
                    last = None
                    for vt in range(8):
                        TN.matmul(psG[:, vt * 32:(vt + 1) * 32],
                                  v_sl[:, vt * 128:(vt + 1) * 128],
                                  ptg_sb[g % 2][:], start=True, stop=False)
                        last = TN.matmul(psG[:, vt * 32:(vt + 1) * 32],
                                         S_bf[:, vt * 128:(vt + 1) * 128],
                                         a_sl, start=False, stop=True)
                    OMM = smm.inc(last)
                    pe_done(psG, OMM)
                    sec_used[par] = OMM
                    v_used[bpar] = OMM
                    ua, ubk = ub[g % 2]
                    rd_acquire(V, ua, ubk)
                    wge(V, sdve, SUPD)
                    wge(V, sact, sbf_cnt)
                    V.scalar_tensor_tensor(out=S_f32[:, 0:512],
                                           in0=S_f32[:, 0:512],
                                           scalar=ch_lam[par][:, cc:cc + 1],
                                           in1=ua[:],
                                           op0=ALU.mult, op1=ALU.add)
                    s2 = V.scalar_tensor_tensor(out=S_f32[:, 512:VH],
                                                in0=S_f32[:, 512:VH],
                                                scalar=ch_lam[par][:, cc:cc + 1],
                                                in1=ubk[:],
                                                op0=ALU.mult, op1=ALU.add)
                    SUPD = sdve.inc(s2)
                    rd_done(ua, sdve, SUPD)
                    rd_done(ubk, sdve, SUPD)
                    wge(SC, sdve, SUPD)
                    wge(SC, smm, OMM)
                    cpS = SC.copy(S_bf[:], S_f32[:])
                    sbf_cnt = sact.inc(cpS)
                    rd_acquire(SC, psG)
                    wge(SC, sst, sst.n)
                    ev = SC.copy(
                        oT_sb[:].rearrange("p (v2 t) -> p v2 t", v2=8)[
                            :, :, (g % 4) * 32:(g % 4) * 32 + 32],
                        psG[:].rearrange("p (v2 t) -> p v2 t", v2=8))
                    rd_done(psG, sact, sact.inc(ev))
                    if g % 4 == 3:
                        tb = blkk
                        wge(SC, sact, sact.n)
                        sq = SC.activation(out=osq[:], in_=oT_sb[:],
                                           func=AF.Square)
                        sact.inc(sq)
                        wge(TN, sact, sact.n)
                        pe_acquire(ps_row)
                        last = None
                        for vt in range(8):
                            last = TN.matmul(ps_row[0:1, 0:128], ones_col_bf[:],
                                             osq[:, vt * 128:(vt + 1) * 128],
                                             start=(vt == 0), stop=(vt == 7))
                        pe_done(ps_row, smm.inc(last))
                        rd_acquire(V, ps_row)
                        wge(V, sst, sst.n)
                        cpo = V.tensor_copy(out=o2_st[:], in_=ps_row[0:1, 0:128])
                        rd_done(ps_row, sdve, sdve.inc(cpo))
                        wge(SY, sdve, sdve.n)
                        wge(SY, sst, sst.n)
                        sst.inc(SY.dma_start(
                            out=ag4_o2.ap()[:, tb * 128:(tb + 1) * 128],
                            in_=o2_st[:]), 16)
                        wge(SY, sact, sact.n)
                        wge(SY, sst, sst.n)
                        sst.inc(SY.dma_start(
                            out=ag4_ot.ap()[:, :, tb * 128:(tb + 1) * 128]
                            .rearrange("v p t -> p v t"),
                            in_=oT_sb[:].rearrange("p (v2 t) -> p v2 t", v2=8)), 16)
                wge(GP, sst, sst.n)
                allgather(ag4_ot.ap(), ag4_ot_o.ap())
                allgather(ag4_o2.ap(), ag4_o2_o.ap())
                AG4_done = sag.n

            with ExitStack() as c7a:
                E7a = c7a.enter_context
                phase_fence()
                wogs = [E7a(nc.sbuf_tensor(f"wogs{i}", [128, DT * 256], BF16))
                        for i in range(2)]
                wigs = [E7a(nc.sbuf_tensor(f"wigs{i}", [128, DT * 256], BF16))
                        for i in range(2)]
                xlnt7 = E7a(nc.sbuf_tensor("xlnt7", [128, DT * SLAB], BF16))
                otp_all = [E7a(nc.sbuf_tensor(f"otp7_{t}", [128, D], BF16))
                           for t in range(TPB)]
                gw_bc = E7a(nc.sbuf_tensor("gw_bc", [128, D], BF16))
                ep_sil = [E7a(nc.sbuf_tensor(f"epsil{i}", [128, 256], BF16))
                          for i in range(2)]
                ep_t1c = E7a(nc.sbuf_tensor("ept1c", [128, 256], F32))
                rstd_c = E7a(nc.sbuf_tensor("rstdc", [128, 16], F32))
                ep_o2 = E7a(nc.sbuf_tensor("epo2", [128, 2], F32))

                for q4 in range(4):
                    wge(SY, smm, smm.n)
                    wge(SY, sld, sld.n)
                    sld.inc(SY.dma_start(
                        out=brow_buf[:],
                        in_=g_w.unsqueeze(0)[0:1, q4 * 512:(q4 + 1) * 512]), 16)
                    wge(TN, sld, sld.n)
                    bank = (psC, psD, psE, psF)[q4]
                    pe_acquire(bank)
                    gwm = TN.matmul(bank[:], ones_row_f32[:], brow_buf[:],
                                    start=True, stop=True)
                    pe_done(bank, smm.inc(gwm))
                    rd_acquire(SC, bank)
                    cpg = SC.copy(gw_bc[:, q4 * 512:(q4 + 1) * 512], bank[:])
                    rd_done(bank, sact, sact.inc(cpg))

                wge(SY, sag, AG4_done)
                wge(SY, sld, sld.n)
                sld.inc(SY.dma_start(
                    out=ep_ot_scr.ap(),
                    in_=ag4_ot_o.ap()[:, :, :, colslab]), 16)
                wge(SY, sld, sld.n)
                sld.inc(SY.dma_start(
                    out=ep_o2_scr.ap(),
                    in_=ag4_o2_o.ap()[:, :, colslab]), 16)
                wge(SY, sld, sld.n)
                o2_rows = ep_o2_scr.ap().rearrange("r a t -> (r a t)").unsqueeze(1)

                otp_used = smm.n
                xl_used = smm.n
                o2_used = sdve.n
                pan_used = [smm.n, smm.n]
                sil_used = [sdve.n, sdve.n]
                for b in range(B):
                    wge(SY, smm, xl_used)
                    wge(SY, sld, sld.n)
                    sld.inc(SY.dma_start(out=xlnt7[:], in_=xlnt_scr[b]), 16)
                    wge(SY, smm, otp_used)
                    for tl in range(TPB):
                        for vh2 in range(2):
                            rr = 2 * b + vh2
                            sld.inc(SY.dma_start(
                                out=otp_all[tl][:, vh2 * 1024:(vh2 + 1) * 1024]
                                .rearrange("p (v2 t) -> p v2 t", v2=8),
                                in_=ep_ot_scr.ap()[rr, :, :,
                                                   tl * 128:(tl + 1) * 128]
                                .rearrange("v p t -> p v t")), 16)
                    BLD = sld.n
                    for hp in range(8):
                        pbw = wogs[hp % 2]
                        pbi = wigs[hp % 2]
                        psem = (sw, sw2)[hp % 2]
                        wge(GP, smm, pan_used[hp % 2])
                        wge(GP, psem, psem.n)
                        psem.inc(GP.dma_start(
                            out=pbw[:].rearrange("p (i n) -> p i n", i=DT),
                            in_=wog_bf.ap()[hp]), 16)
                        psem.inc(GP.dma_start(
                            out=pbi[:].rearrange("p (i n) -> p i n", i=DT),
                            in_=wig_bf.ap()[hp]), 16)
                        PANLD = psem.n
                        wge(TN, psem, PANLD)
                        wge(TN, sld, BLD)
                        for tl in range(TPB):
                            idx = b * TPB + tl
                            par = (hp * TPB + tl) % 2
                            if hp == 0:
                                wge(SY, sdve, o2_used)
                                wge(SY, sld, sld.n)
                                for vh2 in range(2):
                                    rr = 2 * b + vh2
                                    sld.inc(SY.dma_start(
                                        out=ep_o2[:, vh2:vh2 + 1],
                                        in_=o2_rows[rr * 512 + tl * 128:
                                                    rr * 512 + (tl + 1) * 128,
                                                    :]), 16)
                                O2LD = sld.n
                                wge(V, sld, O2LD)
                                rd = V.tensor_reduce(
                                    out=rstd_c[:, idx:idx + 1], in_=ep_o2[:],
                                    axis=mybir.AxisListType.X, op=ALU.add)
                                o2_used = sdve.inc(rd)
                                wge(V, sdve, sdve.n)
                                sdve.inc(V.tensor_scalar(
                                    out=rstd_c[:, idx:idx + 1],
                                    in0=rstd_c[:, idx:idx + 1],
                                    scalar1=1.0 / D, scalar2=None, op0=ALU.mult))
                                wge(SC, sdve, sdve.n)
                                sact.inc(SC.activation(
                                    out=rstd_c[:, idx:idx + 1],
                                    in_=rstd_c[:, idx:idx + 1],
                                    func=AF.Ln, bias=EPS))
                                wge(SC, sact, sact.n)
                                sact.inc(SC.activation(
                                    out=rstd_c[:, idx:idx + 1],
                                    in_=rstd_c[:, idx:idx + 1],
                                    func=AF.Exp, scale=-0.5))
                            bank = (psC, psD)[tl % 2]
                            pe_acquire(bank)
                            last = None
                            for i in range(DT):
                                last = TN.matmul(
                                    bank[:, 0:256],
                                    xlnt7[:, i * 512 + tl * 128:
                                          i * 512 + (tl + 1) * 128],
                                    pbi[:, i * 256:(i + 1) * 256],
                                    start=(i == 0), stop=False)
                            for i in range(DT):
                                last = TN.matmul(
                                    bank[:, 0:256],
                                    otp_all[tl][:, i * 128:(i + 1) * 128],
                                    pbw[:, i * 256:(i + 1) * 256],
                                    start=False, stop=(i == DT - 1))
                            pe_done(bank, smm.inc(last))
                            rd_acquire(SC, bank)
                            wge(SC, sdve, sil_used[par])
                            sl = SC.activation(out=ep_sil[par][:], in_=bank[:, 0:256],
                                               func=AF.Silu)
                            rd_done(bank, sact, sact.inc(sl))
                            tb_bank = (psA, psB)[par]
                            pe_acquire(tb_bank)
                            tlast = None
                            for u in range(2):
                                i = hp * 2 + u
                                tlast = TN.matmul(
                                    tb_bank[:, u * 128:(u + 1) * 128],
                                    otp_all[tl][:, i * 128:(i + 1) * 128],
                                    ident_bf[:], start=True, stop=True)
                            pe_done(tb_bank, smm.inc(tlast))
                            rd_acquire(V, tb_bank)
                            wge(V, sact, sact.n)
                            wge(V, sdve, sdve.n)
                            t1 = V.scalar_tensor_tensor(
                                out=ep_t1c[:], in0=tb_bank[:, 0:256],
                                scalar=rstd_c[:, idx:idx + 1],
                                in1=gw_bc[:, hp * 256:(hp + 1) * 256],
                                op0=ALU.mult, op1=ALU.mult)
                            rd_done(tb_bank, sdve, sdve.inc(t1))
                            wge(V, sdve, sdve.n)
                            t2 = V.tensor_tensor(out=ep_t1c[:], in0=ep_t1c[:],
                                                 in1=ep_sil[par][:], op=ALU.mult)
                            sdve.inc(t2)
                            wge(V, sdve, sdve.n)
                            ad = V.tensor_tensor(
                                out=xr[b][tl][:, hp * 256:(hp + 1) * 256],
                                in0=ep_t1c[:],
                                in1=xr[b][tl][:, hp * 256:(hp + 1) * 256],
                                op=ALU.add)
                            sil_used[par] = sdve.inc(ad)
                        pan_used[hp % 2] = smm.n
                    otp_used = smm.n
                    xl_used = smm.n

        if n_sl >= 4:
            emit_ffn("b", final=True)
        else:
            wge(SY, sdve, sdve.n)
            wge(SY, sact, sact.n)
            wge(SY, smm, smm.n)
            for b in range(B):
                for t in range(TPB):
                    sout.inc(SY.dma_start(
                        out=y_out[b, t * 128:(t + 1) * 128, :],
                        in_=xr[b][t][:]), 16)

        wge(SY, sout, sout.n)

    return nc


def make_in_maps(inputs):
    x = np.asarray(inputs["L0_attn_a_x"])
    tri = np.triu(np.ones((128, 128), np.float32))
    ident = np.eye(128, dtype=np.float32)
    base = {
        "w_qkv": np.ascontiguousarray(
            np.asarray(inputs["L0_attn_a_W_qkv"]).reshape(
                16, 128, 24, 256).transpose(2, 1, 0, 3)),
        "b_qkv": np.asarray(inputs["L0_attn_a_b_qkv"]),
        "b_qkv_col": np.ascontiguousarray(
            np.asarray(inputs["L0_attn_a_b_qkv"]).reshape(48, 128).T),
        "w_proj": np.ascontiguousarray(
            np.asarray(inputs["L0_attn_a_W_proj"]).reshape(
                16, 128, 8, 256).transpose(2, 1, 0, 3)),
        "b_proj": np.asarray(inputs["L0_attn_a_b_proj"]),
        "w_baseT": np.ascontiguousarray(
            np.asarray(inputs["L0_ffn_W_base"]).T.reshape(
                16, 128, 8, 256).transpose(2, 1, 0, 3)),
        "b_base": np.asarray(inputs["L0_ffn_b_base"]),
        "routerT": np.ascontiguousarray(
            np.asarray(inputs["L0_ffn_router_W"]).T.reshape(
                16, 128, 8).transpose(1, 0, 2)),
        "a_T": np.ascontiguousarray(
            np.asarray(inputs["L0_ffn_A"]).reshape(128, D).T.reshape(
                16, 128, 128).transpose(1, 0, 2)),
        "bw_flat": np.ascontiguousarray(
            np.asarray(inputs["L0_ffn_Bw"]).transpose(0, 2, 1).reshape(128, D)),
        "wqT": np.ascontiguousarray(
            np.asarray(inputs["L0_attn_b_Wq"]).T.reshape(
                16, 128, 128).transpose(1, 0, 2)),
        "wkT": np.ascontiguousarray(
            np.asarray(inputs["L0_attn_b_Wk"]).T.reshape(
                16, 128, 128).transpose(1, 0, 2)),
        "wogT": np.ascontiguousarray(
            np.asarray(inputs["L0_attn_b_Wog"]).T.reshape(
                16, 128, 8, 256).transpose(2, 1, 0, 3)),
        "wigT": np.ascontiguousarray(
            np.asarray(inputs["L0_attn_b_Wig"]).T.reshape(
                16, 128, 8, 256).transpose(2, 1, 0, 3)),
        "g_w": np.asarray(inputs["L0_attn_b_g_weight"]),
        "triu_in": tri,
        "ident_in": ident,
    }
    in_maps = []
    for c in range(NR):
        m = dict(base)
        m["x_my"] = np.ascontiguousarray(x[:, c * SLAB:(c + 1) * SLAB, :])
        in_maps.append(m)
    return in_maps


def kernel(**inputs):
    nc = build(N_SL)
    in_maps = make_in_maps(inputs)
    res = run_bass_kernel_spmd(nc, in_maps, core_ids=list(range(NR)))
    y = np.empty((B, T, D), np.float32)
    for c in range(NR):
        y[:, c * SLAB:(c + 1) * SLAB, :] = np.asarray(res.results[c]["y_my"])
    return y


if __name__ == "__main__":
    nc = build(1)
    print("built OK; instructions:",
          sum(len(b.instructions) for b in nc.m.functions[0].blocks))
